# version 03
# baseline (speedup 1.0000x reference)
import numpy as np
import contextlib
import concourse.bass as bass
import concourse.mybir as mybir
from concourse.bass_utils import run_bass_kernel_spmd

F32 = mybir.dt.float32
BF16 = mybir.dt.bfloat16
AF = mybir.ActivationFunctionType
ALU = mybir.AluOpType
AX = mybir.AxisListType


class Sched:
    ENGINES = ("sync", "gpsimd", "tensor", "vector", "scalar")

    def __init__(self, nc):
        self.nc = nc
        self.q = {e: [] for e in self.ENGINES}
        self.tot = {}
        self._waited = {e: {} for e in self.ENGINES}

    def new_sem(self, stack, name):
        h = stack.enter_context(self.nc.semaphore(name))
        self.tot[name] = 0
        return h

    def emit(self, eng, fn, waits=(), inc=None):
        waits = [w for w in waits if w is not None]
        wd = self._waited[eng]
        norm = []
        for w in waits:
            s, v = w
            if v > 0 and wd.get(id(s), -1) < v:
                wd[id(s)] = v
                norm.append((s, v))
        after = None
        if inc is not None:
            s, name, n = inc
            self.tot[name] += n
            after = self.tot[name]

        def thunk(e, _fn=fn, _waits=tuple(norm), _inc=inc):
            for (s, v) in _waits:
                e.wait_ge(s, v)
            instr = _fn(e)
            if _inc is not None:
                s, name, n = _inc
                assert instr is not None, "inc requires an instruction"
                instr.then_inc(s, n)
        self.q[eng].append(thunk)
        return after

    def total(self, name):
        return self.tot[name]

    def run(self, block):
        sched = self

        @block.sync
        def _(e):
            for t in sched.q["sync"]:
                t(e)

        @block.gpsimd
        def _(e):
            for t in sched.q["gpsimd"]:
                t(e)

        @block.tensor
        def _(e):
            for t in sched.q["tensor"]:
                t(e)

        @block.vector
        def _(e):
            for t in sched.q["vector"]:
                t(e)

        @block.scalar
        def _(e):
            for t in sched.q["scalar"]:
                t(e)


D_MODEL = 2048
T_SEQ = 4096
NKT = 16
DSTATE = 16


def host_prep_mamba(inputs, core):
    x = np.asarray(inputs["L0_attn_x"][core])
    xT = np.ascontiguousarray(x.T)
    xpw = np.ascontiguousarray(
        np.asarray(inputs["L0_attn_x_proj_weight"]).reshape(NKT, 128, 96)
        .transpose(1, 0, 2).reshape(128, NKT * 96))
    dtw = np.ascontiguousarray(inputs["L0_attn_dt_proj_weight"])
    dtb = np.ascontiguousarray(
        np.asarray(inputs["L0_attn_dt_proj_bias"]).reshape(NKT, 128).T)
    alog = np.ascontiguousarray(
        np.asarray(inputs["L0_attn_A_log"]).reshape(NKT, 128, DSTATE)
        .transpose(1, 0, 2).reshape(128, NKT * DSTATE))
    dcol = np.ascontiguousarray(
        np.asarray(inputs["L0_attn_D"]).reshape(NKT, 128).T)
    sel16 = np.zeros((16, 16 * 128), np.float32)
    for n in range(16):
        sel16[n, n * 128:(n + 1) * 128] = 1.0
    ones_col = np.ones((128, 1), np.float32)
    ones_row = np.ones((1, 128), np.float32)
    fconst = np.tile(np.array([[1.0, 1e-5, -0.5, 0.0]], np.float32), (128, 1))
    return {
        "xT": xT, "xpw": xpw, "dtw": dtw, "dtb": dtb, "alog": alog,
        "dcol": dcol, "sel16": sel16, "ones_col": ones_col, "ones_row": ones_row,
        "fconst": fconst,
    }



def build_mamba(Tc=256, T=T_SEQ):
    nc = bass.Bass("TRN2", target_bir_lowering=False, debug=False)
    NCH = T // Tc
    W = NKT * Tc

    xT_d = nc.dram_tensor("xT", [D_MODEL, T], F32, kind="ExternalInput").ap()
    xpw_d = nc.dram_tensor("xpw", [128, NKT * 96], F32, kind="ExternalInput").ap()
    dtw_d = nc.dram_tensor("dtw", [64, D_MODEL], F32, kind="ExternalInput").ap()
    dtb_d = nc.dram_tensor("dtb", [128, NKT], F32, kind="ExternalInput").ap()
    alog_d = nc.dram_tensor("alog", [128, NKT * DSTATE], F32, kind="ExternalInput").ap()
    dcol_d = nc.dram_tensor("dcol", [128, NKT], F32, kind="ExternalInput").ap()
    sel16_d = nc.dram_tensor("sel16", [16, 2048], F32, kind="ExternalInput").ap()
    onec_d = nc.dram_tensor("ones_col", [128, 1], F32, kind="ExternalInput").ap()
    oner_d = nc.dram_tensor("ones_row", [1, 128], F32, kind="ExternalInput").ap()
    fconst_d = nc.dram_tensor("fconst", [128, 4], F32, kind="ExternalInput").ap()
    x1_d = nc.dram_tensor("x1", [D_MODEL, T], F32, kind="ExternalOutput").ap()

    xT_r = xT_d.rearrange("(k p) t -> p k t", p=128)
    x1_r = x1_d.rearrange("(k p) t -> p k t", p=128)

    st = contextlib.ExitStack()
    S = lambda name, shape, dt: st.enter_context(nc.sbuf_tensor(name, shape, dt))
    Ps = lambda name, shape: st.enter_context(nc.psum_tensor(name, shape, F32))
    SEM = lambda name: st.enter_context(nc.semaphore(name))

    with st:
        xpw_bf = S("xpw_bf", [128, NKT * 96], BF16)
        dtw_bf = S("dtw_bf", [64, D_MODEL], BF16)
        dtb_s = S("dtb_s", [128, NKT], F32)
        alog_s = S("alog_s", [128, NKT * DSTATE], F32)
        acoef = S("acoef", [128, NKT * DSTATE], F32)
        dcol_s = S("dcol_s", [128, NKT], F32)
        d1col = S("d1col", [128, NKT], F32)
        sel_bf = S("sel_bf", [16, 2048], BF16)
        onec_bf = S("onec_bf", [128, 1], BF16)
        oner_bf = S("oner_bf", [1, 128], BF16)
        fc_s = S("fc_s", [128, 4], F32)

        x_s = S("x_s", [128, W], F32)
        x_bf = S("x_bf", [128, W], BF16)
        delta_all = S("delta_all", [128, W], F32)
        u_s = S("u_s", [128, W], F32)
        y_all = S("y_all", [128, W], F32)
        Bbc = S("Bbc", [128, DSTATE * Tc], BF16)
        Cbc = S("Cbc", [128, DSTATE * Tc], BF16)
        deltain_bf = S("deltain_bf", [64, Tc], BF16)
        Bsb = S("Bsb", [16, Tc], BF16)
        Csb = S("Csb", [16, Tc], BF16)
        ez = S("ez", [128, max(Tc, NKT * DSTATE)], F32)
        dA = [S(f"dA{i}", [128, DSTATE * Tc], F32) for i in range(2)]
        dBu = S("dBu", [128, DSTATE * Tc], F32)
        states = S("states", [128, DSTATE * Tc], F32)
        carry = S("carry", [128, NKT * DSTATE], F32)
        resid_bf = S("resid_bf", [128, W], BF16)
        ln_tmp = S("ln_tmp", [1, 4 * Tc], F32)
        mr_bf = S("mr_bf", [1, 2 * Tc], BF16)

        P0 = Ps("P0", [64, 512])
        P1 = Ps("P1", [16, 512])
        P2 = Ps("P2", [128, 512])
        P3 = Ps("P3", [128, 512])
        P4 = Ps("P4", [128, 512])
        P5 = Ps("P5", [128, 512])
        P6 = Ps("P6", [1, 512])
        P7 = Ps("P7", [128, 512])

        s_ld = SEM("s_ld"); s_lg = SEM("s_lg"); s_mm = SEM("s_mm")
        s_sc = SEM("s_sc"); s_vec = SEM("s_vec"); s_st = SEM("s_st")

        MM = 44
        SC = 45
        VC = 21
        INIT_LD = 64
        INIT_LG = 80
        INIT_SC = 1
        INIT_VC = 1

        def mmb(c): return c * MM
        def scb(c): return INIT_SC + c * SC
        def vcb(c): return INIT_VC + c * VC

        nTc = DSTATE * Tc

        with nc.Block() as block:

            @block.sync
            def _(sync):
                sync.dma_start(out=dtb_s[:], in_=dtb_d).then_inc(s_ld, 16)
                sync.dma_start(out=alog_s[:], in_=alog_d).then_inc(s_ld, 16)
                sync.dma_start(out=dcol_s[:], in_=dcol_d).then_inc(s_ld, 16)
                sync.dma_start(out=fc_s[:], in_=fconst_d).then_inc(s_ld, 16)
                for c in range(NCH):
                    if c > 0:
                        sync.wait_ge(s_vec, vcb(c - 1) + VC)
                        sync.dma_start(
                            out=x1_r[:, :, (c - 1) * Tc:c * Tc],
                            in_=u_s.ap().rearrange("p (k t) -> p k t", k=NKT),
                        ).then_inc(s_st, 16)
                        sync.wait_ge(s_sc, scb(c - 1) + 43)
                    else:
                        sync.wait_ge(s_ld, 64)
                    sync.dma_start(
                        out=x_s.ap().rearrange("p (k t) -> p k t", k=NKT),
                        in_=xT_r[:, :, c * Tc:(c + 1) * Tc]).then_inc(s_ld, 16)
                c = NCH
                sync.wait_ge(s_vec, vcb(c - 1) + VC)
                sync.dma_start(
                    out=x1_r[:, :, (c - 1) * Tc:c * Tc],
                    in_=u_s.ap().rearrange("p (k t) -> p k t", k=NKT),
                ).then_inc(s_st, 16)
                sync.wait_ge(s_st, NCH * 16)

            @block.gpsimd
            def _(g):
                g.dma_start(out=xpw_bf[:], in_=xpw_d).then_inc(s_lg, 16)
                g.dma_start(out=dtw_bf[:], in_=dtw_d).then_inc(s_lg, 16)
                g.dma_start(out=sel_bf[:], in_=sel16_d).then_inc(s_lg, 16)
                g.dma_start(out=onec_bf[:], in_=onec_d).then_inc(s_lg, 16)
                g.dma_start(out=oner_bf[:], in_=oner_d).then_inc(s_lg, 16)
                for c in range(NCH):
                    if c > 0:
                        g.wait_ge(s_sc, scb(c - 1) + 44)
                        g.wait_ge(s_mm, mmb(c - 1) + 43)
                    g.dma_start(
                        out=x_bf.ap().rearrange("p (k t) -> p k t", k=NKT),
                        in_=xT_r[:, :, c * Tc:(c + 1) * Tc]).then_inc(s_lg, 16)

            @block.tensor
            def _(te):
                for c in range(NCH):
                    te.wait_ge(s_ld, INIT_LD + 16 * (c + 1))
                    te.wait_ge(s_lg, INIT_LG + 16 * (c + 1))
                    if c > 0:
                        te.wait_ge(s_sc, scb(c))
                        te.wait_ge(s_vec, vcb(c))
                    else:
                        te.wait_ge(s_vec, INIT_VC)
                    for k in range(NKT):
                        i = nc.tensor.matmul(
                            P0[0:64, 0:Tc],
                            xpw_bf[:, k * 96:k * 96 + 64],
                            x_bf[:, k * Tc:(k + 1) * Tc],
                            start=(k == 0), stop=(k == NKT - 1))
                    i.then_inc(s_mm, 1)
                    for k in range(NKT):
                        nc.tensor.matmul(
                            P1[0:16, 0:Tc],
                            xpw_bf[:, k * 96 + 64:k * 96 + 80],
                            x_bf[:, k * Tc:(k + 1) * Tc],
                            start=(k == 0), stop=(k == NKT - 1))
                    for k in range(NKT):
                        i = nc.tensor.matmul(
                            P1[0:16, Tc:2 * Tc],
                            xpw_bf[:, k * 96 + 80:k * 96 + 96],
                            x_bf[:, k * Tc:(k + 1) * Tc],
                            start=(k == 0), stop=(k == NKT - 1))
                    i.then_inc(s_mm, 1)
                    for kk in range(8):
                        te.wait_ge(s_sc, scb(c) + 1)
                        if kk >= 2:
                            te.wait_ge(s_sc, scb(c) + 2 + (kk - 1))
                        Pb = P2 if kk % 2 == 0 else P3
                        nc.tensor.matmul(
                            Pb[:, 0:Tc], dtw_bf[:, (2 * kk) * 128:(2 * kk + 1) * 128],
                            deltain_bf[:], start=True, stop=True)
                        nc.tensor.matmul(
                            Pb[:, Tc:2 * Tc], dtw_bf[:, (2 * kk + 1) * 128:(2 * kk + 2) * 128],
                            deltain_bf[:], start=True, stop=True).then_inc(s_mm, 1)
                    for n in range(DSTATE):
                        te.wait_ge(s_sc, scb(c) + 2)
                        if n >= 2:
                            te.wait_ge(s_sc, scb(c) + 10 + (n - 1))
                        Pb = P4 if n % 2 == 0 else P5
                        nc.tensor.matmul(Pb[:, 0:Tc], sel_bf[:, n * 128:(n + 1) * 128],
                                         Bsb[:], start=True, stop=True).then_inc(s_mm, 1)
                        nc.tensor.matmul(Pb[:, Tc:2 * Tc], sel_bf[:, n * 128:(n + 1) * 128],
                                         Csb[:], start=True, stop=True).then_inc(s_mm, 1)
                    te.wait_ge(s_sc, scb(c) + 44)
                    te.wait_ge(s_vec, vcb(c) + 19)
                    for k in range(NKT):
                        nc.tensor.matmul(P6[0:1, 0:Tc], onec_bf[:],
                                         resid_bf[:, k * Tc:(k + 1) * Tc],
                                         start=(k == 0), stop=(k == NKT - 1))
                    for k in range(NKT):
                        i = nc.tensor.matmul(P6[0:1, Tc:2 * Tc], onec_bf[:],
                                             x_bf[:, k * Tc:(k + 1) * Tc],
                                             start=(k == 0), stop=(k == NKT - 1))
                    i.then_inc(s_mm, 1)
                    te.wait_ge(s_sc, scb(c) + 45)
                    nc.tensor.matmul(P7[:, 0:Tc], oner_bf[:], mr_bf[0:1, 0:Tc],
                                     start=True, stop=True)
                    nc.tensor.matmul(P7[:, Tc:2 * Tc], oner_bf[:], mr_bf[0:1, Tc:2 * Tc],
                                     start=True, stop=True).then_inc(s_mm, 1)

            @block.scalar
            def _(sc):
                sc.wait_ge(s_ld, 64)
                sc.activation(out=ez[:, 0:NKT * DSTATE], in_=alog_s[:],
                              func=AF.Exp).then_inc(s_sc, 1)
                for c in range(NCH):
                    sc.wait_ge(s_mm, mmb(c) + 1)
                    sc.copy(deltain_bf[:], P0[0:64, 0:Tc]).then_inc(s_sc, 1)
                    sc.wait_ge(s_mm, mmb(c) + 2)
                    sc.copy(Bsb[:], P1[0:16, 0:Tc])
                    sc.copy(Csb[:], P1[0:16, Tc:2 * Tc]).then_inc(s_sc, 1)
                    if c > 0:
                        sc.wait_ge(s_st, 16 * c)
                    for kk in range(8):
                        sc.wait_ge(s_mm, mmb(c) + 2 + kk + 1)
                        Pb = P2 if kk % 2 == 0 else P3
                        for j in range(2):
                            k = 2 * kk + j
                            sc.activation(out=ez[:, 0:Tc], in_=Pb[:, j * Tc:(j + 1) * Tc],
                                          func=AF.Exp, bias=dtb_s[:, k:k + 1])
                            i = sc.activation(out=delta_all[:, k * Tc:(k + 1) * Tc],
                                              in_=ez[:, 0:Tc], func=AF.Ln, bias=fc_s[:, 0:1])
                        i.then_inc(s_sc, 1)
                    for n in range(DSTATE):
                        sc.wait_ge(s_mm, mmb(c) + 10 + 2 * n + 2)
                        Pb = P4 if n % 2 == 0 else P5
                        sc.copy(Bbc[:, n * Tc:(n + 1) * Tc], Pb[:, 0:Tc])
                        sc.copy(Cbc[:, n * Tc:(n + 1) * Tc], Pb[:, Tc:2 * Tc]).then_inc(s_sc, 1)
                    for k in range(NKT):
                        if k >= 2:
                            sc.wait_ge(s_vec, vcb(c) + 1 + (k - 1))
                        for n in range(DSTATE):
                            i = sc.activation(
                                out=dA[k % 2][:, n * Tc:(n + 1) * Tc],
                                in_=delta_all[:, k * Tc:(k + 1) * Tc],
                                func=AF.Exp, scale=acoef[:, k * DSTATE + n:k * DSTATE + n + 1])
                        i.then_inc(s_sc, 1)
                    sc.wait_ge(s_vec, vcb(c) + 17)
                    if c > 0:
                        sc.wait_ge(s_st, 16 * c)
                    for k in range(NKT):
                        i = sc.activation(out=u_s[:, k * Tc:(k + 1) * Tc],
                                          in_=x_s[:, k * Tc:(k + 1) * Tc],
                                          func=AF.Copy, scale=d1col[:, k:k + 1])
                    i.then_inc(s_sc, 1)
                    sc.wait_ge(s_vec, vcb(c) + 18)
                    sc.activation(out=x_bf[:], in_=delta_all[:],
                                  func=AF.Square).then_inc(s_sc, 1)
                    sc.wait_ge(s_vec, vcb(c) + 20)
                    sc.activation(out=ln_tmp[0:1, Tc:2 * Tc], in_=ln_tmp[0:1, 3 * Tc:4 * Tc],
                                  func=AF.Ln, bias=fc_s[0:1, 1:2])
                    sc.activation(out=ln_tmp[0:1, 2 * Tc:3 * Tc], in_=ln_tmp[0:1, Tc:2 * Tc],
                                  func=AF.Exp, scale=fc_s[0:1, 2:3])
                    sc.copy(mr_bf[0:1, 0:Tc], ln_tmp[0:1, 0:Tc])
                    sc.copy(mr_bf[0:1, Tc:2 * Tc], ln_tmp[0:1, 2 * Tc:3 * Tc]).then_inc(s_sc, 1)

            @block.vector
            def _(v):
                v.wait_ge(s_sc, 1)
                v.wait_ge(s_ld, 64)
                v.tensor_scalar_mul(acoef[:], ez[:, 0:NKT * DSTATE], -1.0)
                v.tensor_scalar_add(d1col[:], dcol_s[:], 1.0)
                v.memset(carry[:], 0.0).then_inc(s_vec, 1)
                for c in range(NCH):
                    v.wait_ge(s_sc, scb(c) + 10)
                    v.tensor_tensor(out=u_s[:], in0=delta_all[:], in1=x_s[:],
                                    op=ALU.mult).then_inc(s_vec, 1)
                    for k in range(NKT):
                        v.wait_ge(s_sc, scb(c) + 27 + k)
                        u_b = u_s.ap()[:, k * Tc:(k + 1) * Tc][:, None, :].broadcast_to(
                            (128, DSTATE, Tc))
                        v.tensor_tensor(
                            out=dBu.ap().rearrange("p (n t) -> p n t", n=DSTATE),
                            in0=u_b,
                            in1=Bbc.ap().rearrange("p (n t) -> p n t", n=DSTATE),
                            op=ALU.mult)
                        for n in range(DSTATE):
                            v.tensor_tensor_scan(
                                out=states[:, n * Tc:(n + 1) * Tc],
                                data0=dA[k % 2][:, n * Tc:(n + 1) * Tc],
                                data1=dBu[:, n * Tc:(n + 1) * Tc],
                                initial=carry[:, k * DSTATE + n:k * DSTATE + n + 1],
                                op0=ALU.mult, op1=ALU.add)
                        v.tensor_copy(
                            carry[:, k * DSTATE:(k + 1) * DSTATE],
                            states.ap().rearrange("p (n t) -> p n t", n=DSTATE)[:, :, Tc - 1])
                        v.tensor_tensor(out=dA[k % 2][:], in0=states[:], in1=Cbc[:],
                                        op=ALU.mult)
                        ym = dA[k % 2]
                        v.tensor_tensor(out=dBu[:, 0:8 * Tc], in0=ym[:, 0:8 * Tc],
                                        in1=ym[:, 8 * Tc:16 * Tc], op=ALU.add)
                        v.tensor_tensor(out=ym[:, 0:4 * Tc], in0=dBu[:, 0:4 * Tc],
                                        in1=dBu[:, 4 * Tc:8 * Tc], op=ALU.add)
                        v.tensor_tensor(out=dBu[:, 0:2 * Tc], in0=ym[:, 0:2 * Tc],
                                        in1=ym[:, 2 * Tc:4 * Tc], op=ALU.add)
                        v.tensor_tensor(out=y_all[:, k * Tc:(k + 1) * Tc],
                                        in0=dBu[:, 0:Tc], in1=dBu[:, Tc:2 * Tc],
                                        op=ALU.add).then_inc(s_vec, 1)
                    v.wait_ge(s_sc, scb(c) + 43)
                    v.tensor_tensor(out=delta_all[:], in0=u_s[:], in1=y_all[:],
                                    op=ALU.add).then_inc(s_vec, 1)
                    v.tensor_copy(resid_bf[:], delta_all[:]).then_inc(s_vec, 1)
                    v.wait_ge(s_mm, mmb(c) + 43)
                    v.tensor_scalar(out=ln_tmp[0:1, 0:Tc], in0=P6[0:1, 0:Tc],
                                    scalar1=1.0 / D_MODEL, scalar2=None, op0=ALU.mult)
                    v.tensor_scalar(out=ln_tmp[0:1, Tc:2 * Tc], in0=P6[0:1, Tc:2 * Tc],
                                    scalar1=1.0 / D_MODEL, scalar2=None, op0=ALU.mult)
                    v.tensor_tensor(out=ln_tmp[0:1, 2 * Tc:3 * Tc], in0=ln_tmp[0:1, 0:Tc],
                                    in1=ln_tmp[0:1, 0:Tc], op=ALU.mult)
                    v.tensor_tensor(out=ln_tmp[0:1, 3 * Tc:4 * Tc], in0=ln_tmp[0:1, Tc:2 * Tc],
                                    in1=ln_tmp[0:1, 2 * Tc:3 * Tc],
                                    op=ALU.subtract).then_inc(s_vec, 1)
                    v.wait_ge(s_mm, mmb(c) + 44)
                    mean_b = P7.ap()[:, 0:Tc][:, None, :].broadcast_to((128, NKT, Tc))
                    rstd_b = P7.ap()[:, Tc:2 * Tc][:, None, :].broadcast_to((128, NKT, Tc))
                    v.tensor_tensor(out=y_all.ap().rearrange("p (k t) -> p k t", k=NKT),
                                    in0=delta_all.ap().rearrange("p (k t) -> p k t", k=NKT),
                                    in1=mean_b, op=ALU.subtract)
                    v.tensor_tensor(out=u_s.ap().rearrange("p (k t) -> p k t", k=NKT),
                                    in0=y_all.ap().rearrange("p (k t) -> p k t", k=NKT),
                                    in1=rstd_b, op=ALU.mult).then_inc(s_vec, 1)

    return nc


D_MODEL = 2048
NKT = 16
SH_INTER = 5632
MOE_INTER = 1408
NEXP = 8


def host_prep_moe(inputs, x1):
    gup = np.asarray(inputs["L0_ffn_shared_gate_up_w"])
    gup_sl = np.ascontiguousarray(
        gup.reshape(88, 128, 16, 128).transpose(0, 3, 2, 1)
        .reshape(88, 128, 2048))
    sdw = np.asarray(inputs["L0_ffn_shared_down_w"])
    sdw_sl = np.ascontiguousarray(
        sdw.reshape(16, 128, 2, 22, 128).transpose(2, 0, 4, 3, 1)
        .reshape(2, 16, 128, 22 * 128))
    w13 = np.asarray(inputs["L0_ffn_w13"])
    w13_sl = np.ascontiguousarray(
        w13.reshape(8, 22, 128, 16, 128).transpose(0, 1, 4, 3, 2)
        .reshape(8, 22, 128, 2048))
    w2 = np.asarray(inputs["L0_ffn_w2"])
    w2_sl = np.ascontiguousarray(
        w2.reshape(8, 16, 128, 11, 128).transpose(0, 1, 4, 3, 2)
        .reshape(8, 16, 128, 11 * 128))
    gwT = np.ascontiguousarray(np.asarray(inputs["L0_ffn_gate_w"]).T)
    sgw_col = np.ascontiguousarray(
        np.asarray(inputs["L0_ffn_shared_gate_w"])[0].reshape(16, 128).T)
    sel8 = np.zeros((8, 8 * 128), np.float32)
    for n in range(8):
        sel8[n, n * 128:(n + 1) * 128] = 1.0
    return {
        "x1": np.ascontiguousarray(x1), "gup_sl": gup_sl, "sdw_sl": sdw_sl,
        "w13_sl": w13_sl, "w2_sl": w2_sl, "gwT": gwT, "sgw_col": sgw_col, "sel8": sel8,
        "ident": np.eye(128, dtype=np.float32),
        "ones_col": np.ones((128, 1), np.float32),
        "ones_row": np.ones((1, 128), np.float32),
        "fconst": np.tile(np.array([[1.0, 1e-5, -0.5, 0.0]], np.float32), (128, 1)),
    }


def build_moe(Tc=512, T=4096, debug=False):
    nc = bass.Bass("TRN2", target_bir_lowering=False, debug=False)
    NCH = T // Tc
    W = NKT * Tc
    NB = Tc // 128

    x1_d = nc.dram_tensor("x1", [D_MODEL, T], F32, kind="ExternalInput").ap()
    gup_d = nc.dram_tensor("gup_sl", [88, 128, 2048], F32, kind="ExternalInput").ap()
    sdw_d = nc.dram_tensor("sdw_sl", [2, 16, 128, 22 * 128], F32, kind="ExternalInput").ap()
    w13_d = nc.dram_tensor("w13_sl", [NEXP, 22, 128, 2048], F32, kind="ExternalInput").ap()
    w2_d = nc.dram_tensor("w2_sl", [NEXP, 16, 128, 11 * 128], F32, kind="ExternalInput").ap()
    gwT_d = nc.dram_tensor("gwT", [D_MODEL, 8], F32, kind="ExternalInput").ap()
    sgw_d = nc.dram_tensor("sgw_col", [128, NKT], F32, kind="ExternalInput").ap()
    sel8_d = nc.dram_tensor("sel8", [8, 1024], F32, kind="ExternalInput").ap()
    ident_d = nc.dram_tensor("ident", [128, 128], F32, kind="ExternalInput").ap()
    onec_d = nc.dram_tensor("ones_col", [128, 1], F32, kind="ExternalInput").ap()
    oner_d = nc.dram_tensor("ones_row", [1, 128], F32, kind="ExternalInput").ap()
    fconst_d = nc.dram_tensor("fconst", [128, 4], F32, kind="ExternalInput").ap()
    x2_d = nc.dram_tensor("x2", [D_MODEL, T], F32, kind="ExternalOutput").ap()
    if debug:
        dbg_w_d = nc.dram_tensor("dbg_w", [128, 32], F32, kind="ExternalOutput").ap()
        dbg_sh_d = nc.dram_tensor("dbg_sh", [128, NKT * Tc], F32, kind="ExternalOutput").ap()
        dbg_hid_d = nc.dram_tensor("dbg_hid", [128, 11 * Tc], F32, kind="ExternalOutput").ap()

    x1_r = x1_d.rearrange("(k p) t -> p k t", p=128)
    x2_r = x2_d.rearrange("(k p) t -> p k t", p=128)
    gwT_r = gwT_d.rearrange("(k p) e -> p k e", p=128)

    st = contextlib.ExitStack()
    S = lambda name, shape, dt: st.enter_context(nc.sbuf_tensor(name, shape, dt))
    Ps = lambda name, shape: st.enter_context(nc.psum_tensor(name, shape, F32))

    with st:
        sch = Sched(nc)
        x_bf = S("x_bf", [128, W], BF16)
        x_f = S("x_f", [128, W], F32)
        slabG = [S(f"slabG{i}", [128, NKT * 128], BF16) for i in range(2)]
        slabU = [S(f"slabU{i}", [128, NKT * 128], BF16) for i in range(2)]
        slabD = [S(f"slabD{i}", [128, 22 * 128], BF16) for i in range(2)]
        hid = S("hid", [128, 22 * Tc], BF16)
        out_acc = S("out_acc", [128, W], F32)
        tmp_silu = [S(f"tmps{i}", [128, Tc], BF16) for i in range(2)]
        gw_bf = S("gw_bf", [128, NKT * 8], BF16)
        sgw_bf = S("sgw_bf", [128, NKT * 1], BF16)
        sel8_bf = S("sel8_bf", [8, 1024], BF16)
        ident_f = S("ident_f", [128, 128], F32)
        onec_bf = S("onec_bf", [128, 1], BF16)
        oner_bf = S("oner_bf", [1, 128], BF16)
        fc_s = S("fc_s", [128, 4], F32)
        nmax = S("nmax", [128, NB], F32)
        ssum = S("ssum", [128, NB], F32)
        sinv = S("sinv", [128, NB], F32)
        pexp = S("pexp", [128, NB * 8], F32)
        pfull = S("pfull", [128, NB * 8], F32)
        m1 = S("m1", [128, NB], F32)
        m2 = S("m2", [128, NB], F32)
        ge = S("ge", [128, NB * 8], F32)
        pmask = S("pmask", [128, NB * 8], F32)
        wfull = S("wfull", [128, NB * 8], F32)
        wrow = S("wrow", [8, Tc], BF16)
        sig_bf = S("sig_bf", [1, Tc], BF16)
        resid_bf = S("resid_bf", [128, W], BF16)
        ln_tmp = S("ln_tmp", [1, 4 * Tc], F32)
        mr_bf = S("mr_bf", [1, 2 * Tc], BF16)
        spc = S("spc", [128, 1024], F32)

        PB = [Ps(f"PB{i}", [128, 512]) for i in range(8)]

        s_xl = sch.new_sem(st, "s_xl")
        s_xb = sch.new_sem(st, "s_xb")
        s_wd = sch.new_sem(st, "s_wd")
        s_mm = sch.new_sem(st, "s_mm")
        s_sc = sch.new_sem(st, "s_sc")
        s_vec = sch.new_sem(st, "s_vec")
        s_st = sch.new_sem(st, "s_st")
        E = sch.emit

        E("gpsimd", lambda e: e.dma_start(out=sel8_bf[:], in_=sel8_d), inc=(s_xb, "s_xb", 16))
        E("gpsimd", lambda e: e.dma_start(out=onec_bf[:], in_=onec_d), inc=(s_xb, "s_xb", 16))
        E("gpsimd", lambda e: e.dma_start(out=oner_bf[:], in_=oner_d), inc=(s_xb, "s_xb", 16))
        E("gpsimd", lambda e: e.dma_start(
            out=gw_bf.ap().rearrange("p (k e) -> p k e", k=NKT), in_=gwT_r),
            inc=(s_xb, "s_xb", 16))
        E("gpsimd", lambda e: e.dma_start(out=sgw_bf[:], in_=sgw_d),
            inc=(s_xb, "s_xb", 16))
        E("sync", lambda e: e.dma_start(out=fc_s[:], in_=fconst_d), inc=(s_xl, "s_xl", 16))
        E("sync", lambda e: e.dma_start(out=ident_f[:], in_=ident_d), inc=(s_xl, "s_xl", 16))
        E("vector", lambda e: e.memset(spc[:], 0.0), inc=(s_vec, "s_vec", 1))
        INIT_XB = sch.total("s_xb")
        INIT_XL = sch.total("s_xl")

        bank_free = {i: None for i in range(8)}
        tmp_free = {0: None, 1: None}
        slab_free = {("G", 0): None, ("G", 1): None, ("U", 0): None, ("U", 1): None,
                     ("D", 0): None, ("D", 1): None}
        prev_apply = None
        prev_store_tot = 0
        prev_lastup = None
        prev_residadd = None
        state = {"last_xbf_mm": None}

        def up_phase(jn, gsrc, usrc, hid_slot):
            par = jn % 2
            gs = E("gpsimd", lambda e, _g=gsrc, _p=par: e.dma_start(
                out=slabG[_p][:], in_=_g),
                waits=[slab_free[("G", par)]], inc=(s_wd, "s_wd", 16))
            us = E("gpsimd", lambda e, _u=usrc, _p=par: e.dma_start(
                out=slabU[_p][:], in_=_u),
                waits=[slab_free[("U", par)]], inc=(s_wd, "s_wd", 16))
            bg = 0 if par == 0 else 2
            bu = 1 if par == 0 else 3

            def pg_mms(e, _p=par, _b=bg):
                i = None
                for k in range(NKT):
                    i = nc.tensor.matmul(PB[_b][:, 0:Tc],
                                         slabG[_p][:, k * 128:(k + 1) * 128],
                                         x_bf[:, k * Tc:(k + 1) * Tc],
                                         start=(k == 0), stop=(k == NKT - 1))
                return i
            pg = E("tensor", pg_mms, waits=[(s_wd, sch.total("s_wd")), bank_free[bg]],
                   inc=(s_mm, "s_mm", 1))
            slab_free[("G", par)] = (s_mm, pg)

            def pu_mms(e, _p=par, _b=bu):
                i = None
                for k in range(NKT):
                    i = nc.tensor.matmul(PB[_b][:, 0:Tc],
                                         slabU[_p][:, k * 128:(k + 1) * 128],
                                         x_bf[:, k * Tc:(k + 1) * Tc],
                                         start=(k == 0), stop=(k == NKT - 1))
                return i
            pu = E("tensor", pu_mms, waits=[(s_wd, sch.total("s_wd")), bank_free[bu]],
                   inc=(s_mm, "s_mm", 1))
            slab_free[("U", par)] = (s_mm, pu)
            state["last_xbf_mm"] = pu
            sl = E("scalar", lambda e, _p=par, _b=bg: e.activation(
                out=tmp_silu[_p][:], in_=PB[_b][:, 0:Tc], func=AF.Silu),
                waits=[(s_mm, pg), tmp_free[par]], inc=(s_sc, "s_sc", 1))
            bank_free[bg] = (s_sc, sl)
            hv = E("vector", lambda e, _p=par, _b=bu, _h=hid_slot: e.tensor_tensor(
                out=hid[:, _h * Tc:(_h + 1) * Tc], in0=tmp_silu[_p][:],
                in1=PB[_b][:, 0:Tc], op=ALU.mult),
                waits=[(s_mm, pu), (s_sc, sl)], inc=(s_vec, "s_vec", 1))
            bank_free[bu] = (s_vec, hv)
            tmp_free[par] = (s_vec, hv)
            return hv

        def down_phase(dtn, nslab, dsrc, evac_mode, hid_ready):
            par = dtn % 2
            ds = E("gpsimd", lambda e, _d=dsrc, _p=par, _n=nslab: e.dma_start(
                out=slabD[_p][:, 0:_n * 128], in_=_d),
                waits=[slab_free[("D", par)]], inc=(s_wd, "s_wd", 16))
            bb = 4 if par == 0 else 5

            def pd_mms(e, _p=par, _b=bb, _n=nslab):
                i = None
                for j in range(_n):
                    i = nc.tensor.matmul(PB[_b][:, 0:Tc],
                                         slabD[_p][:, j * 128:(j + 1) * 128],
                                         hid[:, j * Tc:(j + 1) * Tc],
                                         start=(j == 0), stop=(j == _n - 1))
                return i
            pd = E("tensor", pd_mms,
                   waits=[(s_wd, sch.total("s_wd")), bank_free[bb], (s_vec, hid_ready)],
                   inc=(s_mm, "s_mm", 1))
            slab_free[("D", par)] = (s_mm, pd)
            if evac_mode == "copy":
                ev = E("vector", lambda e, _b=bb, _dt=dtn: e.tensor_copy(
                    out_acc[:, _dt * Tc:(_dt + 1) * Tc], PB[_b][:, 0:Tc]),
                    waits=[(s_mm, pd)], inc=(s_vec, "s_vec", 1))
            else:
                ev = E("vector", lambda e, _b=bb, _dt=dtn: e.tensor_tensor(
                    out=out_acc[:, _dt * Tc:(_dt + 1) * Tc],
                    in0=out_acc[:, _dt * Tc:(_dt + 1) * Tc],
                    in1=PB[_b][:, 0:Tc], op=ALU.add),
                    waits=[(s_mm, pd)], inc=(s_vec, "s_vec", 1))
            bank_free[bb] = (s_vec, ev)
            return ev

        for c in range(NCH):
            t0 = c * Tc

            xf_waits = [prev_residadd and (s_vec, prev_residadd)]
            if prev_store_tot:
                xf_waits.append((s_st, prev_store_tot))
            xl_tot = E("sync", lambda e, t0=t0: e.dma_start(
                out=x_f.ap().rearrange("p (k t) -> p k t", k=NKT),
                in_=x1_r[:, :, t0:t0 + Tc]), waits=xf_waits, inc=(s_xl, "s_xl", 16))
            xb_waits = [prev_lastup and (s_mm, prev_lastup)]
            xb_tot = E("gpsimd", lambda e, t0=t0: e.dma_start(
                out=x_bf.ap().rearrange("p (k t) -> p k t", k=NKT),
                in_=x1_r[:, :, t0:t0 + Tc]), waits=xb_waits, inc=(s_xb, "s_xb", 16))

            def router_mms(e):
                i = None
                for b in range(NB):
                    for k in range(NKT):
                        i = nc.tensor.matmul(
                            PB[6][:, b * 8:(b + 1) * 8],
                            x_bf[:, k * Tc + b * 128:k * Tc + (b + 1) * 128],
                            gw_bf[:, k * 8:(k + 1) * 8],
                            start=(k == 0), stop=(k == NKT - 1))
                return i
            rt = E("tensor", router_mms,
                   waits=[(s_xb, xb_tot), bank_free[6],
                          prev_apply and (s_vec, prev_apply)],
                   inc=(s_mm, "s_mm", 1))

            def rt_max(e):
                i = None
                for b in range(NB):
                    i = e.reduce_max(nmax[:, b:b + 1], PB[6][:, b * 8:(b + 1) * 8],
                                     axis=AX.X, negate=True)
                return i
            rtm = E("vector", rt_max, waits=[(s_mm, rt)], inc=(s_vec, "s_vec", 1))

            def rt_exp(e):
                i = None
                for b in range(NB):
                    i = e.activation(out=pexp[:, b * 8:(b + 1) * 8],
                                     in_=PB[6][:, b * 8:(b + 1) * 8], func=AF.Exp,
                                     bias=nmax[:, b:b + 1],
                                     accum_out=ssum[:, b:b + 1])
                return i
            rte = E("scalar", rt_exp, waits=[(s_vec, rtm)], inc=(s_sc, "s_sc", 1))
            bank_free[6] = (s_sc, rte)

            def rt_recip(e):
                return e.reciprocal(sinv[:], ssum[:])
            rtr = E("vector", rt_recip, waits=[(s_sc, rte)], inc=(s_vec, "s_vec", 1))

            def rt_m1(e):
                i = None
                for b in range(NB):
                    i = e.reduce_max(m1[:, b:b + 1], pexp[:, b * 8:(b + 1) * 8],
                                     axis=AX.X, negate=True)
                return i
            rtm1 = E("vector", rt_m1, waits=[(s_sc, rte)], inc=(s_vec, "s_vec", 1))

            def rt_s1(e):
                i = None
                for b in range(NB):
                    i = e.activation(out=ge[:, b * 8:(b + 1) * 8],
                                     in_=pexp[:, b * 8:(b + 1) * 8], func=AF.Sign,
                                     bias=m1[:, b:b + 1])
                return i
            rs1 = E("scalar", rt_s1, waits=[(s_vec, rtm1)], inc=(s_sc, "s_sc", 1))

            def spacer(e):
                e.tensor_copy(spc[:, 512:1024], spc[:, 0:512])
            def rt_pm(e):
                e.tensor_scalar_add(pmask[:], ge[:], 1.0)
                spacer(e)
                e.tensor_scalar_mul(pmask[:], pmask[:], -1e30)
                spacer(e)
                e.tensor_tensor(out=pmask[:], in0=pmask[:], in1=pexp[:], op=ALU.add)
                spacer(e)
                i = None
                for b in range(NB):
                    i = e.reduce_max(m2[:, b:b + 1], pmask[:, b * 8:(b + 1) * 8],
                                     axis=AX.X, negate=True)
                return i
            rpm = E("vector", rt_pm, waits=[(s_sc, rs1)], inc=(s_vec, "s_vec", 1))

            def rt_s2(e):
                i = None
                for b in range(NB):
                    i = e.activation(out=ge[:, b * 8:(b + 1) * 8],
                                     in_=pexp[:, b * 8:(b + 1) * 8], func=AF.Sign,
                                     bias=m2[:, b:b + 1])
                return i
            rs2 = E("scalar", rt_s2, waits=[(s_vec, rpm)], inc=(s_sc, "s_sc", 1))

            def rt_w(e):
                e.tensor_scalar_add(pmask[:], ge[:], 1.0)
                spacer(e)
                e.tensor_scalar_min(pmask[:], pmask[:], 1.0)
                spacer(e)
                e.tensor_tensor(out=wfull[:], in0=pexp[:], in1=pmask[:], op=ALU.mult)
                spacer(e)
                i = None
                for b in range(NB):
                    i = e.tensor_tensor(out=wfull[:, b * 8:(b + 1) * 8],
                                        in0=wfull[:, b * 8:(b + 1) * 8],
                                        in1=sinv.ap()[:, b:b + 1].broadcast_to((128, 8)),
                                        op=ALU.mult)
                return i
            rtw = E("vector", rt_w, waits=[(s_sc, rs2)], inc=(s_vec, "s_vec", 1))
            rtt = rtw

            def w_tp(e):
                i = None
                for b in range(NB):
                    i = nc.tensor.transpose(PB[5][0:8, b * 128:(b + 1) * 128],
                                            wfull[:, b * 8:(b + 1) * 8], ident_f[:])
                return i
            wtp = E("tensor", w_tp, waits=[(s_vec, rtt), bank_free[5], (s_xl, INIT_XL)],
                    inc=(s_mm, "s_mm", 1))
            wr = E("scalar", lambda e: e.copy(wrow[:], PB[5][0:8, 0:Tc]),
                   waits=[(s_mm, wtp)], inc=(s_sc, "s_sc", 1))
            bank_free[5] = (s_sc, wr)

            def sg_mms(e):
                i = None
                for k in range(NKT):
                    i = nc.tensor.matmul(PB[7][0:1, 0:Tc],
                                         sgw_bf[:, k:k + 1],
                                         x_bf[:, k * Tc:(k + 1) * Tc],
                                         start=(k == 0), stop=(k == NKT - 1))
                return i
            sg = E("tensor", sg_mms, waits=[(s_xb, xb_tot), bank_free[7]],
                   inc=(s_mm, "s_mm", 1))
            sgs = E("scalar", lambda e: e.activation(out=sig_bf[:], in_=PB[7][0:1, 0:Tc],
                                                     func=AF.Sigmoid),
                    waits=[(s_mm, sg)], inc=(s_sc, "s_sc", 1))
            bank_free[7] = (s_sc, sgs)
            sigbc = E("tensor", lambda e: nc.tensor.matmul(
                PB[6][:, 0:Tc], oner_bf[:], sig_bf[:], start=True, stop=True),
                waits=[(s_sc, sgs), bank_free[6]], inc=(s_mm, "s_mm", 1))

            for H in range(2):
                hid_ready = None
                for j in range(22):
                    jj = H * 22 + j
                    gsrc = gup_d[jj]
                    usrc = gup_d[44 + jj]
                    hid_ready = up_phase(j, gsrc, usrc, j)
                for dt in range(NKT):
                    dsrc = sdw_d[H, dt]
                    down_phase(dt, 22, dsrc, "copy" if H == 0 else "add", hid_ready)

            def gate_tt(e):
                sb = PB[6].ap()[:, 0:Tc][:, None, :].broadcast_to((128, NKT, Tc))
                return e.tensor_tensor(out=out_acc.ap().rearrange("p (k t) -> p k t", k=NKT),
                                       in0=out_acc.ap().rearrange("p (k t) -> p k t", k=NKT),
                                       in1=sb, op=ALU.mult)
            gt = E("vector", gate_tt, waits=[(s_mm, sigbc)], inc=(s_vec, "s_vec", 1))
            bank_free[6] = (s_vec, gt)
            if debug and c == 0:
                E("sync", lambda e: e.dma_start(out=dbg_w_d, in_=wfull[:]),
                  waits=[(s_vec, gt)], inc=(s_st, "s_st", 16))
                dbg1 = E("sync", lambda e: e.dma_start(out=dbg_sh_d, in_=out_acc[:]),
                  waits=[(s_vec, gt)], inc=(s_st, "s_st", 16))
                prev_store_tot = dbg1

            for ex in range(NEXP):
                pw = E("tensor", lambda e, _e=ex: nc.tensor.matmul(
                    PB[6][:, 0:Tc], sel8_bf[:, _e * 128:(_e + 1) * 128], wrow[:],
                    start=True, stop=True),
                    waits=[bank_free[6], (s_sc, wr)], inc=(s_mm, "s_mm", 1))
                hid_ready = None
                for j in range(11):
                    gsrc = w13_d[ex, j]
                    usrc = w13_d[ex, 11 + j]
                    hid_ready = up_phase(j, gsrc, usrc, j)

                def scale_tt(e):
                    wb = PB[6].ap()[:, 0:Tc][:, None, :].broadcast_to((128, 11, Tc))
                    return e.tensor_tensor(
                        out=hid.ap()[:, 0:11 * Tc].rearrange("p (j t) -> p j t", j=11),
                        in0=hid.ap()[:, 0:11 * Tc].rearrange("p (j t) -> p j t", j=11),
                        in1=wb, op=ALU.mult)
                sct = E("vector", scale_tt, waits=[(s_mm, pw), (s_vec, hid_ready)],
                        inc=(s_vec, "s_vec", 1))
                bank_free[6] = (s_vec, sct)
                if debug and c == 0 and ex == 0:
                    dbg2 = E("gpsimd", lambda e: e.dma_start(out=dbg_hid_d, in_=hid[:, 0:11*Tc]),
                      waits=[(s_vec, sct)], inc=(s_st, "s_st", 16))
                    down_gate = dbg2
                for dt in range(NKT):
                    dsrc = w2_d[ex, dt]
                    down_phase(dt, 11, dsrc, "add", sct)

            ra = E("vector", lambda e: e.tensor_tensor(out=x_f[:], in0=x_f[:],
                                                       in1=out_acc[:], op=ALU.add),
                   waits=[(s_xl, xl_tot)], inc=(s_vec, "s_vec", 1))
            prev_residadd = ra

            sq = E("scalar", lambda e: e.activation(
                out=hid[:, 0:W], in_=x_f[:], func=AF.Square),
                waits=[(s_vec, ra)], inc=(s_sc, "s_sc", 1))
            rb = E("vector", lambda e: e.tensor_copy(resid_bf[:], x_f[:]),
                   inc=(s_vec, "s_vec", 1))

            def ln_sums(e):
                for k in range(NKT):
                    nc.tensor.matmul(PB[7][0:1, 0:Tc], onec_bf[:],
                                     resid_bf[:, k * Tc:(k + 1) * Tc],
                                     start=(k == 0), stop=(k == NKT - 1))
                i = None
                for k in range(NKT):
                    i = nc.tensor.matmul(PB[5][0:1, 0:Tc], onec_bf[:],
                                         hid[:, k * Tc:(k + 1) * Tc],
                                         start=(k == 0), stop=(k == NKT - 1))
                return i
            lns = E("tensor", ln_sums,
                    waits=[(s_vec, rb), (s_sc, sq), bank_free[7], bank_free[5]],
                    inc=(s_mm, "s_mm", 1))

            def ln_stats(e):
                e.tensor_scalar(out=ln_tmp[0:1, 0:Tc], in0=PB[7][0:1, 0:Tc],
                                scalar1=1.0 / D_MODEL, scalar2=None, op0=ALU.mult)
                e.tensor_scalar(out=ln_tmp[0:1, Tc:2 * Tc], in0=PB[5][0:1, 0:Tc],
                                scalar1=1.0 / D_MODEL, scalar2=None, op0=ALU.mult)
                e.tensor_tensor(out=ln_tmp[0:1, 2 * Tc:3 * Tc], in0=ln_tmp[0:1, 0:Tc],
                                in1=ln_tmp[0:1, 0:Tc], op=ALU.mult)
                return e.tensor_tensor(out=ln_tmp[0:1, 3 * Tc:4 * Tc],
                                       in0=ln_tmp[0:1, Tc:2 * Tc],
                                       in1=ln_tmp[0:1, 2 * Tc:3 * Tc], op=ALU.subtract)
            lst = E("vector", ln_stats, waits=[(s_mm, lns)], inc=(s_vec, "s_vec", 1))

            def ln_scalar(e):
                e.activation(out=ln_tmp[0:1, Tc:2 * Tc], in_=ln_tmp[0:1, 3 * Tc:4 * Tc],
                             func=AF.Ln, bias=fc_s[0:1, 1:2])
                e.activation(out=ln_tmp[0:1, 2 * Tc:3 * Tc], in_=ln_tmp[0:1, Tc:2 * Tc],
                             func=AF.Exp, scale=fc_s[0:1, 2:3])
                e.copy(mr_bf[0:1, 0:Tc], ln_tmp[0:1, 0:Tc])
                return e.copy(mr_bf[0:1, Tc:2 * Tc], ln_tmp[0:1, 2 * Tc:3 * Tc])
            lsc = E("scalar", ln_scalar, waits=[(s_vec, lst), (s_xl, INIT_XL)],
                    inc=(s_sc, "s_sc", 1))
            bank_free[7] = (s_vec, lst)
            bank_free[5] = (s_vec, lst)

            def ln_bcast(e):
                nc.tensor.matmul(PB[6][:, 0:Tc], oner_bf[:], mr_bf[0:1, 0:Tc],
                                 start=True, stop=True)
                return nc.tensor.matmul(PB[7][:, 0:Tc], oner_bf[:], mr_bf[0:1, Tc:2 * Tc],
                                        start=True, stop=True)
            lbc = E("tensor", ln_bcast, waits=[(s_sc, lsc), bank_free[6], bank_free[7]],
                    inc=(s_mm, "s_mm", 1))

            def ln_apply(e):
                mb = PB[6].ap()[:, 0:Tc][:, None, :].broadcast_to((128, NKT, Tc))
                rb_ = PB[7].ap()[:, 0:Tc][:, None, :].broadcast_to((128, NKT, Tc))
                e.tensor_tensor(out=out_acc.ap().rearrange("p (k t) -> p k t", k=NKT),
                                in0=x_f.ap().rearrange("p (k t) -> p k t", k=NKT),
                                in1=mb, op=ALU.subtract)
                return e.tensor_tensor(out=x_f.ap().rearrange("p (k t) -> p k t", k=NKT),
                                       in0=out_acc.ap().rearrange("p (k t) -> p k t", k=NKT),
                                       in1=rb_, op=ALU.mult)
            app = E("vector", ln_apply, waits=[(s_mm, lbc)], inc=(s_vec, "s_vec", 1))
            prev_apply = app
            bank_free[6] = (s_vec, app)
            bank_free[7] = (s_vec, app)

            prev_store_tot = E("sync", lambda e, t0=t0: e.dma_start(
                out=x2_r[:, :, t0:t0 + Tc],
                in_=x_f.ap().rearrange("p (k t) -> p k t", k=NKT)),
                waits=[(s_vec, app)], inc=(s_st, "s_st", 16))
            prev_lastup = state["last_xbf_mm"]

        E("sync", lambda e: None, waits=[(s_st, prev_store_tot)])

        with nc.Block() as block:
            sch.run(block)

    return nc


D_MODEL = 2048
NKT = 16
NQH = 16
NKH = 8
HD = 128
WINDOW = 1024
ATTN_SCALE = 0.08838834764831845
NEG = -1.0e9


def host_prep_attn(inputs, x2, T=4096):
    qk = np.asarray(inputs["L1_attn_q_kernel"])
    qk_sl = np.ascontiguousarray(
        qk.reshape(16, 16, 128, 128).transpose(0, 2, 1, 3).reshape(16, 128, 2048))
    kvk = np.asarray(inputs["L1_attn_kv_kernel"])
    kvk_sl = np.ascontiguousarray(
        kvk.reshape(2, 8, 16, 128, 128).transpose(0, 1, 3, 2, 4).reshape(2, 8, 128, 2048))
    ok = np.asarray(inputs["L1_attn_out_kernel"])
    ok_sl = np.ascontiguousarray(
        ok.reshape(16, 128, 16, 128).transpose(2, 1, 0, 3).reshape(16, 128, 2048))
    qns = np.asarray(inputs["L1_attn_q_norm_scale"]).reshape(128, 1)
    kns = np.asarray(inputs["L1_attn_k_norm_scale"]).reshape(128, 1)
    ts_ = 10000.0 ** (np.arange(0, HD, 2, dtype=np.float64) / HD)
    sinus = np.arange(T, dtype=np.float64)[None, :] / ts_[:, None]
    cosT = np.tile(np.cos(sinus).astype(np.float32), (2, 1))
    sinT = np.tile(np.sin(sinus).astype(np.float32), (2, 1))
    ii = np.arange(128)[:, None]
    jj = np.arange(128)[None, :]
    maskA = np.where(ii < jj, 0.0, NEG).astype(np.float32)
    maskC = np.where(ii >= jj, 0.0, NEG).astype(np.float32)
    return {
        "x2": np.ascontiguousarray(x2),
        "qk_sl": qk_sl, "kvk_sl": kvk_sl, "ok_sl": ok_sl,
        "qns": np.ascontiguousarray(qns), "kns": np.ascontiguousarray(kns),
        "cosT": cosT, "sinT": sinT, "maskA": maskA, "maskC": maskC,
        "ident": np.eye(128, dtype=np.float32),
        "ones_col": np.ones((128, 1), np.float32),
        "ones_row": np.ones((1, 128), np.float32),
        "fconst": np.tile(np.array([[1.0, 1e-5, -0.5, 0.0, 1e-6]], np.float32), (128, 1)),
    }


def build_attn(Tc=512, T=4096, debug=False):
    nc = bass.Bass("TRN2", target_bir_lowering=False, debug=False)
    NCH = T // Tc
    W = NKT * Tc
    NB = Tc // 128

    x2_d = nc.dram_tensor("x2", [D_MODEL, T], F32, kind="ExternalInput").ap()
    qk_d = nc.dram_tensor("qk_sl", [NQH, 128, D_MODEL], F32, kind="ExternalInput").ap()
    kvk_d = nc.dram_tensor("kvk_sl", [2, NKH, 128, D_MODEL], F32, kind="ExternalInput").ap()
    ok_d = nc.dram_tensor("ok_sl", [NKT, 128, D_MODEL], F32, kind="ExternalInput").ap()
    qns_d = nc.dram_tensor("qns", [128, 1], F32, kind="ExternalInput").ap()
    kns_d = nc.dram_tensor("kns", [128, 1], F32, kind="ExternalInput").ap()
    cos_d = nc.dram_tensor("cosT", [128, T], F32, kind="ExternalInput").ap()
    sin_d = nc.dram_tensor("sinT", [128, T], F32, kind="ExternalInput").ap()
    maskA_d = nc.dram_tensor("maskA", [128, 128], F32, kind="ExternalInput").ap()
    maskC_d = nc.dram_tensor("maskC", [128, 128], F32, kind="ExternalInput").ap()
    ident_d = nc.dram_tensor("ident", [128, 128], F32, kind="ExternalInput").ap()
    onec_d = nc.dram_tensor("ones_col", [128, 1], F32, kind="ExternalInput").ap()
    oner_d = nc.dram_tensor("ones_row", [1, 128], F32, kind="ExternalInput").ap()
    fconst_d = nc.dram_tensor("fconst", [128, 5], F32, kind="ExternalInput").ap()
    x3_d = nc.dram_tensor("x3", [D_MODEL, T], F32, kind="ExternalOutput").ap()
    if debug:
        dbg_kc_d = nc.dram_tensor("dbg_kc", [128, 3 * NKH * Tc], F32, kind="ExternalOutput").ap()
        dbg_vc_d = nc.dram_tensor("dbg_vc", [128, 3 * NKH * Tc], F32, kind="ExternalOutput").ap()
        dbg_q_d = nc.dram_tensor("dbg_q", [128, Tc], F32, kind="ExternalOutput").ap()
        dbg_p_d = nc.dram_tensor("dbg_p", [128, 9 * 128], F32, kind="ExternalOutput").ap()
        dbg_enc_d = nc.dram_tensor("dbg_enc", [128, NQH * Tc], F32, kind="ExternalOutput").ap()
        dbg_s_d = nc.dram_tensor("dbg_s", [128, 1152], F32, kind="ExternalOutput").ap()

    x2_r = x2_d.rearrange("(k p) t -> p k t", p=128)
    x3_r = x3_d.rearrange("(k p) t -> p k t", p=128)

    st = contextlib.ExitStack()
    S = lambda name, shape, dt: st.enter_context(nc.sbuf_tensor(name, shape, dt))
    PsT = lambda name, shape, dt: st.enter_context(nc.psum_tensor(name, shape, dt))

    with st:
        sch = Sched(nc)
        x_bf = S("x_bf", [128, W], BF16)
        x_f = S("x_f", [128, W], F32)
        wslab = [S(f"wslab{i}", [128, NKT * 128], BF16) for i in range(2)]
        kcache = S("kcache", [128, 3 * NKH * Tc], BF16)
        vcache = S("vcache", [128, 3 * NKH * Tc], BF16)
        enc_all = S("enc_all", [128, NQH * Tc], BF16)
        pf = S("pf", [128, Tc], F32)
        pn = S("pn", [128, Tc], F32)
        p2 = S("p2", [128, Tc], BF16)
        vbf = S("vbf", [128, Tc], BF16)
        qrope = S("qrope", [128, Tc], BF16)
        P_sb = S("P_sb", [128, 9 * 128], BF16)
        PT_sb = S("PT_sb", [128, 9 * 128], BF16)
        cos_s = S("cos_s", [128, Tc], F32)
        sin_s = S("sin_s", [128, Tc], F32)
        rtmp = S("rtmp", [64, 2 * Tc], F32)
        qns_s = S("qns_s", [128, 1], F32)
        kns_s = S("kns_s", [128, 1], F32)
        ident_bf = S("ident_bf", [128, 128], BF16)
        maskA_bf = S("maskA_bf", [128, 128], BF16)
        maskC_bf = S("maskC_bf", [128, 128], BF16)
        onec_bf = S("onec_bf", [128, 1], BF16)
        oner_bf = S("oner_bf", [1, 128], BF16)
        fc_s = S("fc_s", [128, 5], F32)
        rr = S("rr", [1, Tc], F32)
        rr_bf = S("rr_bf", [1, Tc], BF16)
        nmx = S("nmx", [128, 1], F32)
        ssum_a = S("ssum_a", [128, 1], F32)
        rs = S("rs", [128, 1], F32)
        spc = S("spc", [128, 1024], F32)
        dbg_s_sb = S("dbg_s_sb", [128, 1152], F32)
        resid_bf = S("resid_bf", [128, W], BF16)
        ln_tmp = S("ln_tmp", [1, 6 * Tc], F32)
        mr_bf = S("mr_bf", [1, 2 * Tc], BF16)

        PS = PsT("PS", [128, 1152], F32)
        PS2 = PsT("PS2", [128, 1152], F32)
        PE_ = PsT("PE_", [128, 512], F32)
        PT = PsT("PT", [128, 1024], BF16)

        s_xl = sch.new_sem(st, "s_xl")
        s_xb = sch.new_sem(st, "s_xb")
        s_wd = sch.new_sem(st, "s_wd")
        s_mm = sch.new_sem(st, "s_mm")
        s_sc = sch.new_sem(st, "s_sc")
        s_vec = sch.new_sem(st, "s_vec")
        s_st = sch.new_sem(st, "s_st")
        s_dbg = sch.new_sem(st, "s_dbg")
        E = sch.emit

        for ap_, dst in [(qns_d, qns_s), (kns_d, kns_s), (fconst_d, fc_s)]:
            E("sync", lambda e, _a=ap_, _d=dst: e.dma_start(out=_d[:], in_=_a),
              inc=(s_xl, "s_xl", 16))
        for ap_, dst in [(ident_d, ident_bf), (maskA_d, maskA_bf), (maskC_d, maskC_bf),
                         (onec_d, onec_bf), (oner_d, oner_bf)]:
            E("gpsimd", lambda e, _a=ap_, _d=dst: e.dma_start(out=_d[:], in_=_a),
              inc=(s_xb, "s_xb", 16))
        INIT_XL = sch.total("s_xl")
        qsc = E("vector", lambda e: e.tensor_scalar_mul(qns_s[:], qns_s[:], ATTN_SCALE),
                waits=[(s_xl, INIT_XL)], inc=(s_vec, "s_vec", 1))

        bank = {"PS": None, "PS2": None, "PE": None, "PR": None, "PT": None}
        pf_free = [None]
        p2_free = [None]
        slab_free = {0: None, 1: None}
        prev_store = 0
        prev_apply = None
        state = {"last_xbf_mm": None, "last_kc_read": None, "last_vc_read": None,
                 "kc_written": {}, "vc_written": {}}

        def spacer(e):
            e.tensor_copy(spc[:, 512:1024], spc[:, 0:512])

        for c in range(NCH):
            t0 = c * Tc
            slot = c % 3

            xf_w = [(s_st, prev_store)] if prev_store else []
            xl_tot = E("sync", lambda e, t0=t0: e.dma_start(
                out=x_f.ap().rearrange("p (k t) -> p k t", k=NKT),
                in_=x2_r[:, :, t0:t0 + Tc]), waits=xf_w + ([(s_vec, prev_apply)] if prev_apply else []),
                inc=(s_xl, "s_xl", 16))
            xb_w = [(s_mm, state["last_xbf_mm"])] if state["last_xbf_mm"] else []
            xb_tot = E("gpsimd", lambda e, t0=t0: e.dma_start(
                out=x_bf.ap().rearrange("p (k t) -> p k t", k=NKT),
                in_=x2_r[:, :, t0:t0 + Tc]), waits=xb_w, inc=(s_xb, "s_xb", 16))
            cs_tot = E("sync", lambda e, t0=t0: e.dma_start(out=cos_s[:], in_=cos_d[:, t0:t0 + Tc]),
                       waits=[(s_vec, prev_apply)] if prev_apply else [],
                       inc=(s_xl, "s_xl", 16))
            cs_tot = E("sync", lambda e, t0=t0: e.dma_start(out=sin_s[:], in_=sin_d[:, t0:t0 + Tc]),
                       inc=(s_xl, "s_xl", 16))

            def load_slab(par, src):
                h = E("gpsimd", lambda e, _s=src, _p=par: e.dma_start(
                    out=wslab[_p][:], in_=_s),
                    waits=[slab_free[par]], inc=(s_wd, "s_wd", 16))
                return h

            def rms_rope(scale_col, dst, dst_off, hsrc_evac, rope_extra):
                sq = E("scalar", lambda e: e.activation(out=p2[:], in_=pf[:], func=AF.Square),
                       waits=[p2_free[0]] if p2_free[0] else [], inc=(s_sc, "s_sc", 1))
                def ssq_mm(e):
                    return nc.tensor.matmul(PE_[0:1, 0:Tc], onec_bf[:], p2[:],
                                            start=True, stop=True)
                sq_mm = E("tensor", ssq_mm, waits=[(s_sc, sq)],
                          inc=(s_mm, "s_mm", 1))
                p2_free[0] = (s_mm, sq_mm)
                def rr_v(e):
                    return e.tensor_scalar(out=rr[:], in0=PE_[0:1, 0:Tc],
                                           scalar1=1.0 / HD, scalar2=None, op0=ALU.mult)
                rrv = E("vector", rr_v, waits=[(s_mm, sq_mm)], inc=(s_vec, "s_vec", 1))
                def rr_s(e):
                    e.activation(out=rr[:], in_=rr[:], func=AF.Ln, bias=fc_s[0:1, 4:5])
                    e.activation(out=rr[:], in_=rr[:], func=AF.Exp, scale=fc_s[0:1, 2:3])
                    return e.copy(rr_bf[:], rr[:])
                rrs = E("scalar", rr_s, waits=[(s_vec, rrv)], inc=(s_sc, "s_sc", 1))
                def rb_mm(e):
                    return nc.tensor.matmul(PE_[:, 0:Tc], oner_bf[:], rr_bf[:],
                                            start=True, stop=True)
                rbm = E("tensor", rb_mm, waits=[(s_sc, rrs)], inc=(s_mm, "s_mm", 1))
                def norm_v(e):
                    return e.scalar_tensor_tensor(out=pn[:], in0=pf[:], scalar=scale_col[:, 0:1],
                                                  in1=PE_[:, 0:Tc], op0=ALU.mult, op1=ALU.mult)
                nrm = E("vector", norm_v, waits=[(s_mm, rbm)], inc=(s_vec, "s_vec", 1))
                bank["PE"] = (s_vec, nrm)
                pf_free[0] = (s_vec, nrm)
                def rope_v(e):
                    e.tensor_tensor(out=rtmp[:, 0:Tc], in0=pn[0:64, :], in1=cos_s[0:64, :], op=ALU.mult)
                    e.tensor_tensor(out=rtmp[:, Tc:2 * Tc], in0=pn[64:128, :], in1=sin_s[64:128, :], op=ALU.mult)
                    e.tensor_tensor(out=dst[0:64, dst_off:dst_off + Tc],
                                    in0=rtmp[:, 0:Tc], in1=rtmp[:, Tc:2 * Tc], op=ALU.subtract)
                    e.tensor_tensor(out=rtmp[:, 0:Tc], in0=pn[64:128, :], in1=cos_s[64:128, :], op=ALU.mult)
                    e.tensor_tensor(out=rtmp[:, Tc:2 * Tc], in0=pn[0:64, :], in1=sin_s[0:64, :], op=ALU.mult)
                    return e.tensor_tensor(out=dst[64:128, dst_off:dst_off + Tc],
                                           in0=rtmp[:, 0:Tc], in1=rtmp[:, Tc:2 * Tc], op=ALU.add)
                rp = E("vector", rope_v,
                       waits=[(s_xl, cs_tot)] + [w for w in rope_extra if w],
                       inc=(s_vec, "s_vec", 1))
                return rp

            for kh in range(NKH):
                par = kh % 2
                ks = load_slab(par, kvk_d[0, kh])
                def kp_mms(e, _p=par):
                    i = None
                    for k in range(NKT):
                        i = nc.tensor.matmul(PE_[:, 0:Tc],
                                             wslab[_p][:, k * 128:(k + 1) * 128],
                                             x_bf[:, k * Tc:(k + 1) * Tc],
                                             start=(k == 0), stop=(k == NKT - 1))
                    return i
                kp = E("tensor", kp_mms,
                       waits=[(s_wd, sch.total("s_wd")), bank["PE"], (s_xb, xb_tot)],
                       inc=(s_mm, "s_mm", 1))
                slab_free[par] = (s_mm, kp)
                state["last_xbf_mm"] = kp
                kev = E("scalar", lambda e: e.copy(pf[:], PE_[:, 0:Tc]),
                        waits=[(s_mm, kp)] + ([pf_free[0]] if pf_free[0] else []),
                        inc=(s_sc, "s_sc", 1))
                bank["PE"] = (s_sc, kev)
                vs = load_slab(1 - par, kvk_d[1, kh])
                def vp_mms(e, _p=1 - par):
                    i = None
                    for k in range(NKT):
                        i = nc.tensor.matmul(PS[:, 0:Tc],
                                             wslab[_p][:, k * 128:(k + 1) * 128],
                                             x_bf[:, k * Tc:(k + 1) * Tc],
                                             start=(k == 0), stop=(k == NKT - 1))
                    return i
                vp = E("tensor", vp_mms,
                       waits=[(s_wd, sch.total("s_wd")), bank["PS"], (s_xb, xb_tot)],
                       inc=(s_mm, "s_mm", 1))
                slab_free[1 - par] = (s_mm, vp)
                state["last_xbf_mm"] = vp
                vev = E("scalar", lambda e: e.copy(vbf[:], PS[:, 0:Tc]),
                        waits=[(s_mm, vp)], inc=(s_sc, "s_sc", 1))
                bank["PS"] = (s_sc, vev)
                koff = (slot * NKH + kh) * Tc
                rp = rms_rope(kns_s, kcache, koff, kev, [state["last_kc_read"]])
                state["kc_written"][(slot, kh)] = (s_vec, rp)
                def v_tp(e):
                    i = None
                    for tb in range(NB):
                        i = nc.tensor.transpose(PT[:, tb * 128:(tb + 1) * 128],
                                                vbf[:, tb * 128:(tb + 1) * 128], ident_bf[:])
                    return i
                vtp = E("tensor", v_tp, waits=[(s_sc, vev), bank["PT"], (s_xb, sch.total("s_xb"))],
                        inc=(s_mm, "s_mm", 1))
                voff = (slot * NKH + kh) * Tc
                vc_war = state["last_vc_read"]
                vce = E("scalar", lambda e, _vo=voff: e.copy(
                    vcache[:, _vo:_vo + Tc], PT[:, 0:NB * 128]),
                    waits=[(s_mm, vtp)] + ([vc_war] if vc_war else []),
                    inc=(s_sc, "s_sc", 1))
                bank["PT"] = (s_sc, vce)
                state["vc_written"][(slot, kh)] = (s_sc, vce)


            dbg_gate = None
            if debug and c == 0:
                kc_all = [state["kc_written"][(0, kh)] for kh in range(NKH)]
                vc_all = [state["vc_written"][(0, kh)] for kh in range(NKH)]
                E("gpsimd", lambda e: e.dma_start(out=dbg_kc_d[:, 0:NKH * Tc],
                                                   in_=kcache[:, 0:NKH * Tc]),
                  waits=kc_all, inc=(s_dbg, "s_dbg", 16))
                dbg_gate = E("gpsimd", lambda e: e.dma_start(out=dbg_vc_d[:, 0:NKH * Tc],
                                                             in_=vcache[:, 0:NKH * Tc]),
                  waits=vc_all, inc=(s_dbg, "s_dbg", 16))

            for n in range(NQH):
                par = n % 2
                qs = load_slab(par, qk_d[n])
                def qp_mms(e, _p=par):
                    i = None
                    for k in range(NKT):
                        i = nc.tensor.matmul(PE_[:, 0:Tc],
                                             wslab[_p][:, k * 128:(k + 1) * 128],
                                             x_bf[:, k * Tc:(k + 1) * Tc],
                                             start=(k == 0), stop=(k == NKT - 1))
                    return i
                qp = E("tensor", qp_mms,
                       waits=[(s_wd, sch.total("s_wd")), bank["PE"], (s_xb, xb_tot)],
                       inc=(s_mm, "s_mm", 1))
                slab_free[par] = (s_mm, qp)
                state["last_xbf_mm"] = qp
                qev = E("scalar", lambda e: e.copy(pf[:], PE_[:, 0:Tc]),
                        waits=[(s_mm, qp)] + ([pf_free[0]] if pf_free[0] else []),
                        inc=(s_sc, "s_sc", 1))
                bank["PE"] = (s_sc, qev)
                rp = rms_rope(qns_s, qrope, 0, qev, [state.get("qrope_free")])
                if debug and c == 0 and n == 0:
                    dbg_gate = E("gpsimd", lambda e: e.dma_start(out=dbg_q_d, in_=qrope[:]),
                      waits=[(s_vec, rp)], inc=(s_dbg, "s_dbg", 16))
                kh = n // 2

                for b in range(NB):
                    qb = (t0 + b * 128) // 128
                    nsb = min(qb, 8) + 1
                    j0 = 8 - (nsb - 1)
                    PSb = PS if b % 2 == 0 else PS2
                    bkey = "PS" if b % 2 == 0 else "PS2"
                    def s_mms(e, _b=b, _qb=qb, _nsb=nsb, _kh=kh, _PSb=PSb):
                        i = None
                        for jj_ in range(_nsb):
                            sb = _qb - (_nsb - 1) + jj_
                            sc_ = (sb // NB) % 3
                            soff = (sc_ * NKH + _kh) * Tc + (sb % NB) * 128
                            msk = None
                            if jj_ == _nsb - 1:
                                msk = maskC_bf
                            elif jj_ == 0 and _qb >= 8:
                                msk = maskA_bf
                            i = nc.tensor.matmul(
                                _PSb[:, jj_ * 128:(jj_ + 1) * 128],
                                qrope[:, _b * 128:(_b + 1) * 128],
                                kcache[:, soff:soff + 128],
                                start=True, stop=(msk is None))
                            if msk is not None:
                                i = nc.tensor.matmul(
                                    _PSb[:, jj_ * 128:(jj_ + 1) * 128],
                                    ident_bf[:], msk[:], start=False, stop=True)
                        return i
                    kc_waits = [state["kc_written"].get((((qb - (nsb - 1) + jj_) // NB) % 3, kh))
                                for jj_ in range(nsb)]
                    dw = [(s_dbg, sch.total("s_dbg"))] if (debug and c == 0) else []
                    smm = E("tensor", s_mms,
                            waits=[(s_vec, rp), bank[bkey]] + kc_waits + dw,
                            inc=(s_mm, "s_mm", 1))
                    if debug and c == 0 and n == 0 and b == 0:
                        wlen = nsb * 128
                        dcp = E("scalar", lambda e, _w=wlen, _P=PSb: e.copy(
                            dbg_s_sb[:, 0:_w], _P[:, 0:_w]),
                            waits=[(s_mm, smm)], inc=(s_sc, "s_sc", 1))
                        dbg_gate = E("gpsimd", lambda e, _w=wlen: e.dma_start(
                            out=dbg_s_d[:, 0:_w], in_=dbg_s_sb[:, 0:_w]),
                            waits=[(s_sc, dcp)], inc=(s_dbg, "s_dbg", 16))
                    state["last_s_mm"] = smm
                    wlen = nsb * 128
                    rmx = E("vector", lambda e, _w=wlen, _P=PSb: e.reduce_max(
                        nmx[:], _P[:, 0:_w], axis=AX.X, negate=True),
                        waits=[(s_mm, smm)] + ([state.get("exp_free")] if state.get("exp_free") else []),
                        inc=(s_vec, "s_vec", 1))
                    exp_ = E("scalar", lambda e, _w=wlen, _P=PSb: e.activation(
                        out=P_sb[:, 0:_w], in_=_P[:, 0:_w], func=AF.Exp,
                        bias=nmx[:, 0:1], accum_out=ssum_a[:, 0:1]),
                        waits=[(s_vec, rmx)], inc=(s_sc, "s_sc", 1))
                    bank[bkey] = (s_sc, exp_)
                    state["exp_free"] = (s_sc, exp_)
                    def nrm_p(e, _w=wlen):
                        e.reciprocal(rs[:], ssum_a[:])
                        spacer(e)
                        return e.tensor_scalar(out=P_sb[:, 0:_w], in0=P_sb[:, 0:_w],
                                               scalar1=rs[:, 0:1], scalar2=None, op0=ALU.mult)
                    nrm = E("vector", nrm_p, waits=[(s_sc, exp_)], inc=(s_vec, "s_vec", 1))
                    if debug and c == 0 and n == 0 and b == 0:
                        dbg_gate = E("gpsimd", lambda e, _w=wlen: e.dma_start(
                            out=dbg_p_d[:, 0:_w], in_=P_sb[:, 0:_w]),
                          waits=[(s_vec, nrm)], inc=(s_dbg, "s_dbg", 16))
                    if debug and c == 0 and n == 0 and b == 0:
                        pass
                    n1 = min(nsb, 8)
                    def p_tp(e, _n1=n1):
                        i = None
                        for jj_ in range(_n1):
                            i = nc.tensor.transpose(PT[:, jj_ * 128:(jj_ + 1) * 128],
                                                    P_sb[:, jj_ * 128:(jj_ + 1) * 128],
                                                    ident_bf[:])
                        return i
                    ptp = E("tensor", p_tp, waits=[(s_vec, nrm), bank["PT"]],
                            inc=(s_mm, "s_mm", 1))
                    pte = E("scalar", lambda e, _n1=n1: e.copy(
                        PT_sb[:, 0:_n1 * 128], PT[:, 0:_n1 * 128]),
                            waits=[(s_mm, ptp)], inc=(s_sc, "s_sc", 1))
                    bank["PT"] = (s_sc, pte)
                    if nsb > 8:
                        ptp2 = E("tensor", lambda e: nc.tensor.transpose(
                            PT[:, 0:128], P_sb[:, 8 * 128:9 * 128], ident_bf[:]),
                            waits=[bank["PT"]], inc=(s_mm, "s_mm", 1))
                        pte = E("scalar", lambda e: e.copy(
                            PT_sb[:, 8 * 128:9 * 128], PT[:, 0:128]),
                            waits=[(s_mm, ptp2)], inc=(s_sc, "s_sc", 1))
                        bank["PT"] = (s_sc, pte)
                    def pv_mms(e, _b=b, _qb=qb, _nsb=nsb, _kh=kh):
                        i = None
                        for jj_ in range(_nsb):
                            sb = _qb - (_nsb - 1) + jj_
                            sc_ = (sb // NB) % 3
                            voff = (sc_ * NKH + _kh) * Tc + (sb % NB) * 128
                            i = nc.tensor.matmul(
                                PE_[:, 0:128],
                                vcache[:, voff:voff + 128],
                                PT_sb[:, jj_ * 128:(jj_ + 1) * 128],
                                start=(jj_ == 0), stop=(jj_ == _nsb - 1))
                        return i
                    vc_waits = [state["vc_written"].get((((qb - (nsb - 1) + jj_) // NB) % 3, kh))
                                for jj_ in range(nsb)]
                    pv = E("tensor", pv_mms, waits=[(s_sc, pte), bank["PE"]] + vc_waits,
                           inc=(s_mm, "s_mm", 1))
                    state["last_pv"] = pv
                    eev = E("scalar", lambda e, _n=n, _b=b: e.copy(
                        enc_all[:, _n * Tc + _b * 128:_n * Tc + (_b + 1) * 128],
                        PE_[:, 0:128]),
                        waits=[(s_mm, pv)], inc=(s_sc, "s_sc", 1))
                    bank["PE"] = (s_sc, eev)

                state["qrope_free"] = (s_mm, state["last_s_mm"])

            if debug and c == 0:
                E("gpsimd", lambda e: e.dma_start(out=dbg_enc_d, in_=enc_all[:]),
                  waits=[(s_sc, sch.total("s_sc"))], inc=(s_dbg, "s_dbg", 16))

            state["last_kc_read"] = (s_mm, state["last_s_mm"])
            state["last_vc_read"] = (s_mm, state["last_pv"])

            last_enc = sch.total("s_sc")
            for dt in range(NKT):
                par = dt % 2
                os_ = load_slab(par, ok_d[dt])
                Pb = PE_ if par == 0 else PS
                pbk = "PE" if par == 0 else "PS"
                def op_mms(e, _p=par, _Pb=Pb):
                    i = None
                    for nn_ in range(NQH):
                        i = nc.tensor.matmul(_Pb[:, 0:Tc],
                                             wslab[_p][:, nn_ * 128:(nn_ + 1) * 128],
                                             enc_all[:, nn_ * Tc:(nn_ + 1) * Tc],
                                             start=(nn_ == 0), stop=(nn_ == NQH - 1))
                    return i
                dw2 = [(s_dbg, sch.total("s_dbg"))] if (debug and c == 0) else []
                om = E("tensor", op_mms,
                       waits=[(s_wd, sch.total("s_wd")), bank[pbk], (s_sc, last_enc)] + dw2,
                       inc=(s_mm, "s_mm", 1))
                slab_free[par] = (s_mm, om)
                ra = E("vector", lambda e, _dt=dt, _Pb=Pb: e.tensor_tensor(
                    out=x_f[:, _dt * Tc:(_dt + 1) * Tc],
                    in0=x_f[:, _dt * Tc:(_dt + 1) * Tc],
                    in1=_Pb[:, 0:Tc], op=ALU.add),
                    waits=[(s_mm, om), (s_xl, xl_tot)], inc=(s_vec, "s_vec", 1))
                bank[pbk] = (s_vec, ra)

            sq = E("scalar", lambda e: e.activation(
                out=enc_all[:, 0:W], in_=x_f[:], func=AF.Square),
                waits=[bank["PE"], bank["PS"]], inc=(s_sc, "s_sc", 1))
            rb = E("vector", lambda e: e.tensor_copy(resid_bf[:], x_f[:]),
                   inc=(s_vec, "s_vec", 1))

            def ln_sums(e):
                for k in range(NKT):
                    nc.tensor.matmul(PE_[0:1, 0:Tc], onec_bf[:],
                                     resid_bf[:, k * Tc:(k + 1) * Tc],
                                     start=(k == 0), stop=(k == NKT - 1))
                i = None
                for k in range(NKT):
                    i = nc.tensor.matmul(PS[0:1, 0:Tc], onec_bf[:],
                                         enc_all[:, k * Tc:(k + 1) * Tc],
                                         start=(k == 0), stop=(k == NKT - 1))
                return i
            lns = E("tensor", ln_sums,
                    waits=[(s_vec, rb), (s_sc, sq), bank["PE"], bank["PS"]],
                    inc=(s_mm, "s_mm", 1))

            def ln_stats(e):
                e.tensor_scalar(out=ln_tmp[0:1, 0:Tc], in0=PE_[0:1, 0:Tc],
                                scalar1=1.0 / D_MODEL, scalar2=None, op0=ALU.mult)
                e.tensor_scalar(out=ln_tmp[0:1, Tc:2 * Tc], in0=PS[0:1, 0:Tc],
                                scalar1=1.0 / D_MODEL, scalar2=None, op0=ALU.mult)
                e.tensor_tensor(out=ln_tmp[0:1, 2 * Tc:3 * Tc], in0=ln_tmp[0:1, 0:Tc],
                                in1=ln_tmp[0:1, 0:Tc], op=ALU.mult)
                return e.tensor_tensor(out=ln_tmp[0:1, 3 * Tc:4 * Tc],
                                       in0=ln_tmp[0:1, Tc:2 * Tc],
                                       in1=ln_tmp[0:1, 2 * Tc:3 * Tc], op=ALU.subtract)
            lst = E("vector", ln_stats, waits=[(s_mm, lns)], inc=(s_vec, "s_vec", 1))

            def ln_scalar(e):
                e.activation(out=ln_tmp[0:1, 4 * Tc:5 * Tc], in_=ln_tmp[0:1, 3 * Tc:4 * Tc],
                             func=AF.Ln, bias=fc_s[0:1, 1:2])
                return e.activation(out=ln_tmp[0:1, 4 * Tc:5 * Tc],
                                    in_=ln_tmp[0:1, 4 * Tc:5 * Tc],
                                    func=AF.Exp, scale=fc_s[0:1, 2:3])
            lsc = E("scalar", ln_scalar, waits=[(s_vec, lst)], inc=(s_sc, "s_sc", 1))

            def ln_ms(e):
                e.tensor_tensor(out=ln_tmp[0:1, 5 * Tc:6 * Tc], in0=ln_tmp[0:1, 0:Tc],
                                in1=ln_tmp[0:1, 4 * Tc:5 * Tc], op=ALU.mult)
                spacer(e)
                e.tensor_copy(mr_bf[0:1, 0:Tc], ln_tmp[0:1, 4 * Tc:5 * Tc])
                return e.tensor_copy(mr_bf[0:1, Tc:2 * Tc], ln_tmp[0:1, 5 * Tc:6 * Tc])
            lms = E("vector", ln_ms, waits=[(s_sc, lsc)], inc=(s_vec, "s_vec", 1))

            def ln_bcast(e):
                nc.tensor.matmul(PE_[:, 0:Tc], oner_bf[:], mr_bf[0:1, 0:Tc],
                                 start=True, stop=True)
                return nc.tensor.matmul(PS[:, 0:Tc], oner_bf[:], mr_bf[0:1, Tc:2 * Tc],
                                        start=True, stop=True)
            lbc = E("tensor", ln_bcast, waits=[(s_vec, lms)], inc=(s_mm, "s_mm", 1))

            def ln_apply(e):
                rb_ = PE_.ap()[:, 0:Tc][:, None, :].broadcast_to((128, NKT, Tc))
                ms_ = PS.ap()[:, 0:Tc][:, None, :].broadcast_to((128, NKT, Tc))
                e.tensor_tensor(out=x_f.ap().rearrange("p (k t) -> p k t", k=NKT),
                                in0=x_f.ap().rearrange("p (k t) -> p k t", k=NKT),
                                in1=rb_, op=ALU.mult)
                return e.tensor_tensor(out=x_f.ap().rearrange("p (k t) -> p k t", k=NKT),
                                       in0=x_f.ap().rearrange("p (k t) -> p k t", k=NKT),
                                       in1=ms_, op=ALU.subtract)
            app = E("vector", ln_apply, waits=[(s_mm, lbc)], inc=(s_vec, "s_vec", 1))
            prev_apply = app
            bank["PE"] = (s_vec, app)
            bank["PS"] = (s_vec, app)

            prev_store = E("sync", lambda e, t0=t0: e.dma_start(
                out=x3_r[:, :, t0:t0 + Tc],
                in_=x_f.ap().rearrange("p (k t) -> p k t", k=NKT)),
                waits=[(s_vec, app)], inc=(s_st, "s_st", 16))

        E("sync", lambda e: None, waits=[(s_st, prev_store)])

        with nc.Block() as block:
            sch.run(block)

    return nc


D_MODEL = 2048
NKT = 16
HID = 8192
NJ = HID // 128
NJH = NJ // 2


def _had(n):
    h = np.array([[1.0]])
    while h.shape[0] < n:
        h = np.block([[h, h], [h, -h]])
    return h


def host_prep_hffn(inputs, x3):
    W1 = np.asarray(inputs["L1_ffn_W1"])
    w1_sl = np.ascontiguousarray(
        W1.reshape(64, 128, 16, 128).transpose(0, 3, 2, 1).reshape(64, 128, 2048))
    W3 = np.asarray(inputs["L1_ffn_W3"])
    w3_sl = np.ascontiguousarray(
        W3.reshape(64, 128, 16, 128).transpose(0, 3, 2, 1).reshape(64, 128, 2048))
    w2T = np.ascontiguousarray(np.asarray(inputs["L1_ffn_W2"]).T)
    h64 = (_had(64) / 8.0).astype(np.float32)
    h128 = (_had(128) * (128.0 ** -0.5)).astype(np.float32)
    perm = np.empty(128, np.int64)
    for q in range(128):
        a2, H, jl = q // 64, (q // 32) % 2, q % 32
        perm[q] = 2 * (H * 32 + jl) + a2
    h128p = np.ascontiguousarray(h128[:, perm])
    return {
        "x3": np.ascontiguousarray(x3), "w1_sl": w1_sl, "w3_sl": w3_sl, "w2T": w2T,
        "h64": h64, "h128": h128p,
        "ones_col": np.ones((128, 1), np.float32),
        "ones_row": np.ones((1, 128), np.float32),
        "fconst": np.tile(np.array([[1.0, 1e-5, -0.5, 0.0, 1e-6]], np.float32), (128, 1)),
    }


def build_hffn(Tc=512, T=4096):
    nc = bass.Bass("TRN2", target_bir_lowering=False, debug=False)
    NCH = T // Tc
    W = NKT * Tc

    x3_d = nc.dram_tensor("x3", [D_MODEL, T], F32, kind="ExternalInput").ap()
    w1_d = nc.dram_tensor("w1_sl", [NJ, 128, D_MODEL], F32, kind="ExternalInput").ap()
    w3_d = nc.dram_tensor("w3_sl", [NJ, 128, D_MODEL], F32, kind="ExternalInput").ap()
    w2T_d = nc.dram_tensor("w2T", [HID, D_MODEL], F32, kind="ExternalInput").ap()
    h64_d = nc.dram_tensor("h64", [64, 64], F32, kind="ExternalInput").ap()
    h128_d = nc.dram_tensor("h128", [128, 128], F32, kind="ExternalInput").ap()
    onec_d = nc.dram_tensor("ones_col", [128, 1], F32, kind="ExternalInput").ap()
    oner_d = nc.dram_tensor("ones_row", [1, 128], F32, kind="ExternalInput").ap()
    fconst_d = nc.dram_tensor("fconst", [128, 5], F32, kind="ExternalInput").ap()
    x4_d = nc.dram_tensor("x4", [D_MODEL, T], F32, kind="ExternalOutput").ap()
    t1_d = nc.dram_tensor("t1buf", [HID, D_MODEL], BF16, kind="Internal").ap()
    v_d = nc.dram_tensor("vbuf", [2, NKT, 128, NJH * 128], BF16, kind="Internal").ap()

    x3_r = x3_d.rearrange("(k p) t -> p k t", p=128)
    x4_r = x4_d.rearrange("(k p) t -> p k t", p=128)
    t1_s64 = t1_d.rearrange("(p j) d -> p j d", j=64)
    v3_st = v_d.rearrange("H dt p (jl f) -> H p jl dt f", f=128)

    st = contextlib.ExitStack()
    S = lambda name, shape, dt: st.enter_context(nc.sbuf_tensor(name, shape, dt))
    Ps = lambda name, shape: st.enter_context(nc.psum_tensor(name, shape, F32))

    with st:
        sch = Sched(nc)
        x_bf = S("x_bf", [128, W], BF16)
        x_f = S("x_f", [128, W], F32)
        slabG = [S(f"slabG{i}", [128, NKT * 128], BF16) for i in range(2)]
        slabU = [S(f"slabU{i}", [128, NKT * 128], BF16) for i in range(2)]
        slabD = [S(f"slabD{i}", [128, NJH * 128], BF16) for i in range(2)]
        hid = S("hid", [128, NJH * Tc], BF16)
        tmp_silu = [S(f"tmps{i}", [128, Tc], BF16) for i in range(2)]
        pp_in = [S(f"pp_in{i}", [128, D_MODEL], BF16) for i in range(2)]
        pp_out = [S(f"pp_out{i}", [128, D_MODEL], BF16) for i in range(2)]
        h64_bf = S("h64_bf", [64, 64], BF16)
        h128_bf = S("h128_bf", [128, 128], BF16)
        onec_bf = S("onec_bf", [128, 1], BF16)
        oner_bf = S("oner_bf", [1, 128], BF16)
        fc_s = S("fc_s", [128, 5], F32)
        resid_bf = S("resid_bf", [128, W], BF16)
        r2_bf = S("r2_bf", [128, W], BF16)
        ln_tmp = S("ln_tmp", [1, 6 * Tc], F32)
        mr_bf = S("mr_bf", [1, 2 * Tc], BF16)
        spc = S("spc", [128, 1024], F32)

        PB = [Ps(f"PB{i}", [128, 512]) for i in range(8)]

        s_xl = sch.new_sem(st, "s_xl")
        s_xb = sch.new_sem(st, "s_xb")
        s_wd = sch.new_sem(st, "s_wd")
        s_mm = sch.new_sem(st, "s_mm")
        s_sc = sch.new_sem(st, "s_sc")
        s_vec = sch.new_sem(st, "s_vec")
        s_st = sch.new_sem(st, "s_st")
        s_pp = sch.new_sem(st, "s_pp")
        E = sch.emit

        E("gpsimd", lambda e: e.dma_start(out=h64_bf[:], in_=h64_d), inc=(s_xb, "s_xb", 16))
        E("gpsimd", lambda e: e.dma_start(out=h128_bf[:], in_=h128_d), inc=(s_xb, "s_xb", 16))
        E("gpsimd", lambda e: e.dma_start(out=onec_bf[:], in_=onec_d), inc=(s_xb, "s_xb", 16))
        E("gpsimd", lambda e: e.dma_start(out=oner_bf[:], in_=oner_d), inc=(s_xb, "s_xb", 16))
        E("sync", lambda e: e.dma_start(out=fc_s[:], in_=fconst_d), inc=(s_xl, "s_xl", 16))
        INIT_XB = sch.total("s_xb")
        INIT_XL = sch.total("s_xl")
        E("vector", lambda e: e.memset(spc[:], 0.0), inc=(s_vec, "s_vec", 1))

        bank_free = {i: None for i in range(8)}
        tmp_free = {0: None, 1: None}
        slab_free = {("G", 0): None, ("G", 1): None, ("U", 0): None, ("U", 1): None,
                     ("D", 0): None, ("D", 1): None}
        pp_free = {0: None, 1: None}
        ppo_free = {0: None, 1: None}

        def pp_round(idx, src_ap, dst_ap, hmat, nrow):
            par = idx % 2
            b0 = 4 * par
            ld = E("gpsimd", lambda e, _s=src_ap, _p=par, _n=nrow: e.dma_start(
                out=pp_in[_p][0:_n, :], in_=_s),
                waits=[pp_free[par], (s_xb, INIT_XB)], inc=(s_wd, "s_wd", 16))

            def mms(e, _p=par, _b0=b0, _n=nrow, _h=hmat):
                i = None
                for q in range(4):
                    i = nc.tensor.matmul(PB[_b0 + q][0:_n, 0:512], _h,
                                         pp_in[_p][0:_n, q * 512:(q + 1) * 512],
                                         start=True, stop=True)
                return i
            bw = [bank_free[b0 + q] for q in range(4)]
            mm = E("tensor", mms, waits=[(s_wd, sch.total("s_wd"))] + bw,
                   inc=(s_mm, "s_mm", 1))
            pp_free[par] = (s_mm, mm)

            def evac(e, _p=par, _b0=b0, _n=nrow):
                i = None
                for q in range(4):
                    i = e.copy(pp_out[_p][0:_n, q * 512:(q + 1) * 512],
                               PB[_b0 + q][0:_n, 0:512])
                return i
            ev = E("scalar", evac, waits=[(s_mm, mm)] +
                   ([ppo_free[par]] if ppo_free[par] else []),
                   inc=(s_sc, "s_sc", 1))
            for q in range(4):
                bank_free[b0 + q] = (s_sc, ev)
            if isinstance(dst_ap, tuple):
                _, bidx = dst_ap
                st_ = None
                for Hh in range(2):
                    for a2 in range(2):
                        g0 = (a2 * 2 + Hh) * 32
                        st_ = E("sync", lambda e, _p=par, _H=Hh, _a2=a2, _b=bidx,
                                _g0=g0: e.dma_start(
                                    out=v3_st[_H, 64 * _a2 + _b, :, :, :],
                                    in_=pp_out[_p].ap()[_g0:_g0 + 32, :].rearrange(
                                        "p (dt f) -> p dt f", f=128)),
                                waits=[(s_sc, ev)], inc=(s_pp, "s_pp", 16))
            else:
                st_ = E("sync", lambda e, _p=par, _d=dst_ap, _n=nrow: e.dma_start(
                    out=_d, in_=pp_out[_p][0:_n, :]),
                    waits=[(s_sc, ev)], inc=(s_pp, "s_pp", 16))
            ppo_free[par] = (s_pp, st_)
            return st_

        for a in range(128):
            pp_round(a, w2T_d[a * 64:(a + 1) * 64, :],
                     t1_d[a * 64:(a + 1) * 64, :], h64_bf[:], 64)
        PP1_DONE = sch.total("s_pp")
        pp_free[0] = (s_pp, PP1_DONE)
        pp_free[1] = (s_pp, PP1_DONE)
        for b in range(64):
            pp_round(128 + b, t1_s64[:, b, :], ("v3", b), h128_bf[:], 128)
        PP2_DONE = sch.total("s_pp")

        prev_apply = None
        prev_store_tot = 0
        prev_lastup = None
        state = {"last_xbf_mm": None}

        def up_phase(jn, gsrc, usrc, hid_slot, xb_tot):
            par = jn % 2
            gs = E("gpsimd", lambda e, _g=gsrc, _p=par: e.dma_start(
                out=slabG[_p][:], in_=_g),
                waits=[slab_free[("G", par)]], inc=(s_wd, "s_wd", 16))
            us = E("gpsimd", lambda e, _u=usrc, _p=par: e.dma_start(
                out=slabU[_p][:], in_=_u),
                waits=[slab_free[("U", par)]], inc=(s_wd, "s_wd", 16))
            bg = 0 if par == 0 else 2
            bu = 1 if par == 0 else 3

            def pg_mms(e, _p=par, _b=bg):
                i = None
                for k in range(NKT):
                    i = nc.tensor.matmul(PB[_b][:, 0:Tc],
                                         slabG[_p][:, k * 128:(k + 1) * 128],
                                         x_bf[:, k * Tc:(k + 1) * Tc],
                                         start=(k == 0), stop=(k == NKT - 1))
                return i
            pg = E("tensor", pg_mms,
                   waits=[(s_wd, sch.total("s_wd")), bank_free[bg], (s_xb, xb_tot)],
                   inc=(s_mm, "s_mm", 1))
            slab_free[("G", par)] = (s_mm, pg)

            def pu_mms(e, _p=par, _b=bu):
                i = None
                for k in range(NKT):
                    i = nc.tensor.matmul(PB[_b][:, 0:Tc],
                                         slabU[_p][:, k * 128:(k + 1) * 128],
                                         x_bf[:, k * Tc:(k + 1) * Tc],
                                         start=(k == 0), stop=(k == NKT - 1))
                return i
            pu = E("tensor", pu_mms,
                   waits=[(s_wd, sch.total("s_wd")), bank_free[bu], (s_xb, xb_tot)],
                   inc=(s_mm, "s_mm", 1))
            slab_free[("U", par)] = (s_mm, pu)
            state["last_xbf_mm"] = pu
            sl = E("scalar", lambda e, _p=par, _b=bg: e.activation(
                out=tmp_silu[_p][:], in_=PB[_b][:, 0:Tc], func=AF.Silu),
                waits=[(s_mm, pg), tmp_free[par]], inc=(s_sc, "s_sc", 1))
            bank_free[bg] = (s_sc, sl)
            hv = E("vector", lambda e, _p=par, _b=bu, _h=hid_slot: e.tensor_tensor(
                out=hid[:, _h * Tc:(_h + 1) * Tc], in0=tmp_silu[_p][:],
                in1=PB[_b][:, 0:Tc], op=ALU.mult),
                waits=[(s_mm, pu), (s_sc, sl)], inc=(s_vec, "s_vec", 1))
            bank_free[bu] = (s_vec, hv)
            tmp_free[par] = (s_vec, hv)
            return hv

        for c in range(NCH):
            t0 = c * Tc
            xf_waits = ([(s_st, prev_store_tot)] if prev_store_tot else []) + \
                       ([(s_vec, prev_apply)] if prev_apply else [])
            xl_tot = E("sync", lambda e, t0=t0: e.dma_start(
                out=x_f.ap().rearrange("p (k t) -> p k t", k=NKT),
                in_=x3_r[:, :, t0:t0 + Tc]), waits=xf_waits, inc=(s_xl, "s_xl", 16))
            xb_waits = [(s_mm, state["last_xbf_mm"])] if state["last_xbf_mm"] else []
            xb_tot = E("gpsimd", lambda e, t0=t0: e.dma_start(
                out=x_bf.ap().rearrange("p (k t) -> p k t", k=NKT),
                in_=x3_r[:, :, t0:t0 + Tc]), waits=xb_waits, inc=(s_xb, "s_xb", 16))

            for H in range(2):
                hid_ready = None
                for j in range(NJH):
                    jj = H * NJH + j
                    gsrc = w1_d[jj]
                    usrc = w3_d[jj]
                    hid_ready = up_phase(j, gsrc, usrc, j, xb_tot)
                for dt in range(NKT):
                    par = dt % 2
                    ds = E("gpsimd", lambda e, _H=H, _dt=dt, _p=par: e.dma_start(
                        out=slabD[_p][:], in_=v_d[_H, _dt]),
                        waits=[slab_free[("D", par)], (s_pp, PP2_DONE)],
                        inc=(s_wd, "s_wd", 16))
                    bb = 4 if par == 0 else 5

                    def pd_mms(e, _p=par, _b=bb):
                        i = None
                        for j2 in range(NJH):
                            i = nc.tensor.matmul(PB[_b][:, 0:Tc],
                                                 slabD[_p][:, j2 * 128:(j2 + 1) * 128],
                                                 hid[:, j2 * Tc:(j2 + 1) * Tc],
                                                 start=(j2 == 0), stop=(j2 == NJH - 1))
                        return i
                    pd = E("tensor", pd_mms,
                           waits=[(s_wd, sch.total("s_wd")), bank_free[bb],
                                  (s_vec, hid_ready)],
                           inc=(s_mm, "s_mm", 1))
                    slab_free[("D", par)] = (s_mm, pd)
                    ra = E("vector", lambda e, _b=bb, _dt=dt: e.tensor_tensor(
                        out=x_f[:, _dt * Tc:(_dt + 1) * Tc],
                        in0=x_f[:, _dt * Tc:(_dt + 1) * Tc],
                        in1=PB[_b][:, 0:Tc], op=ALU.add),
                        waits=[(s_mm, pd), (s_xl, xl_tot)], inc=(s_vec, "s_vec", 1))
                    bank_free[bb] = (s_vec, ra)

            sq = E("scalar", lambda e: e.activation(
                out=r2_bf[:], in_=x_f[:], func=AF.Square),
                waits=[bank_free[4], bank_free[5]], inc=(s_sc, "s_sc", 1))
            rb = E("vector", lambda e: e.tensor_copy(resid_bf[:], x_f[:]),
                   inc=(s_vec, "s_vec", 1))

            def ln_sums(e):
                for k in range(NKT):
                    nc.tensor.matmul(PB[6][0:1, 0:Tc], onec_bf[:],
                                     resid_bf[:, k * Tc:(k + 1) * Tc],
                                     start=(k == 0), stop=(k == NKT - 1))
                i = None
                for k in range(NKT):
                    i = nc.tensor.matmul(PB[7][0:1, 0:Tc], onec_bf[:],
                                         r2_bf[:, k * Tc:(k + 1) * Tc],
                                         start=(k == 0), stop=(k == NKT - 1))
                return i
            lns = E("tensor", ln_sums,
                    waits=[(s_vec, rb), (s_sc, sq), bank_free[6], bank_free[7]],
                    inc=(s_mm, "s_mm", 1))

            def ln_stats(e):
                e.tensor_scalar(out=ln_tmp[0:1, 0:Tc], in0=PB[6][0:1, 0:Tc],
                                scalar1=1.0 / D_MODEL, scalar2=None, op0=ALU.mult)
                e.tensor_scalar(out=ln_tmp[0:1, Tc:2 * Tc], in0=PB[7][0:1, 0:Tc],
                                scalar1=1.0 / D_MODEL, scalar2=None, op0=ALU.mult)
                e.tensor_tensor(out=ln_tmp[0:1, 2 * Tc:3 * Tc], in0=ln_tmp[0:1, 0:Tc],
                                in1=ln_tmp[0:1, 0:Tc], op=ALU.mult)
                return e.tensor_tensor(out=ln_tmp[0:1, 3 * Tc:4 * Tc],
                                       in0=ln_tmp[0:1, Tc:2 * Tc],
                                       in1=ln_tmp[0:1, 2 * Tc:3 * Tc], op=ALU.subtract)
            lst = E("vector", ln_stats, waits=[(s_mm, lns)], inc=(s_vec, "s_vec", 1))

            def ln_scalar(e):
                e.activation(out=ln_tmp[0:1, 4 * Tc:5 * Tc], in_=ln_tmp[0:1, 3 * Tc:4 * Tc],
                             func=AF.Ln, bias=fc_s[0:1, 1:2])
                return e.activation(out=ln_tmp[0:1, 4 * Tc:5 * Tc],
                                    in_=ln_tmp[0:1, 4 * Tc:5 * Tc],
                                    func=AF.Exp, scale=fc_s[0:1, 2:3])
            lsc = E("scalar", ln_scalar, waits=[(s_vec, lst)], inc=(s_sc, "s_sc", 1))

            def ln_ms(e):
                e.tensor_tensor(out=ln_tmp[0:1, 5 * Tc:6 * Tc], in0=ln_tmp[0:1, 0:Tc],
                                in1=ln_tmp[0:1, 4 * Tc:5 * Tc], op=ALU.mult)
                e.tensor_copy(spc[:, 512:1024], spc[:, 0:512])
                e.tensor_copy(mr_bf[0:1, 0:Tc], ln_tmp[0:1, 4 * Tc:5 * Tc])
                return e.tensor_copy(mr_bf[0:1, Tc:2 * Tc], ln_tmp[0:1, 5 * Tc:6 * Tc])
            lms = E("vector", ln_ms, waits=[(s_sc, lsc)], inc=(s_vec, "s_vec", 1))

            def ln_bcast(e):
                nc.tensor.matmul(PB[6][:, 0:Tc], oner_bf[:], mr_bf[0:1, 0:Tc],
                                 start=True, stop=True)
                return nc.tensor.matmul(PB[7][:, 0:Tc], oner_bf[:], mr_bf[0:1, Tc:2 * Tc],
                                        start=True, stop=True)
            lbc = E("tensor", ln_bcast, waits=[(s_vec, lms)], inc=(s_mm, "s_mm", 1))

            def ln_apply(e):
                rb_ = PB[6].ap()[:, 0:Tc][:, None, :].broadcast_to((128, NKT, Tc))
                ms_ = PB[7].ap()[:, 0:Tc][:, None, :].broadcast_to((128, NKT, Tc))
                e.tensor_tensor(out=x_f.ap().rearrange("p (k t) -> p k t", k=NKT),
                                in0=x_f.ap().rearrange("p (k t) -> p k t", k=NKT),
                                in1=rb_, op=ALU.mult)
                return e.tensor_tensor(out=x_f.ap().rearrange("p (k t) -> p k t", k=NKT),
                                       in0=x_f.ap().rearrange("p (k t) -> p k t", k=NKT),
                                       in1=ms_, op=ALU.subtract)
            app = E("vector", ln_apply, waits=[(s_mm, lbc)], inc=(s_vec, "s_vec", 1))
            prev_apply = app
            bank_free[6] = (s_vec, app)
            bank_free[7] = (s_vec, app)

            prev_store_tot = E("sync", lambda e, t0=t0: e.dma_start(
                out=x4_r[:, :, t0:t0 + Tc],
                in_=x_f.ap().rearrange("p (k t) -> p k t", k=NKT)),
                waits=[(s_vec, app)], inc=(s_st, "s_st", 16))
            prev_lastup = state["last_xbf_mm"]

        E("sync", lambda e: None, waits=[(s_st, prev_store_tot)])

        with nc.Block() as block:
            sch.run(block)

    return nc



N_CORES = 8


def kernel(**inputs):
    cores = list(range(N_CORES))

    ims = [host_prep_mamba(inputs, c) for c in cores]
    nc1 = build_mamba(Tc=256, T=T_SEQ)
    res1 = run_bass_kernel_spmd(nc1, ims, core_ids=cores)
    x1s = [res1.results[c]["x1"] for c in cores]

    ims = [host_prep_moe(inputs, x1s[c]) for c in cores]
    nc2 = build_moe(Tc=512, T=T_SEQ)
    res2 = run_bass_kernel_spmd(nc2, ims, core_ids=cores)
    x2s = [res2.results[c]["x2"] for c in cores]

    ims = [host_prep_attn(inputs, x2s[c], T=T_SEQ) for c in cores]
    nc3 = build_attn(Tc=512, T=T_SEQ)
    res3 = run_bass_kernel_spmd(nc3, ims, core_ids=cores)
    x3s = [res3.results[c]["x3"] for c in cores]

    ims = [host_prep_hffn(inputs, x3s[c]) for c in cores]
    nc4 = build_hffn(Tc=512, T=T_SEQ)
    res4 = run_bass_kernel_spmd(nc4, ims, core_ids=cores)
    x4s = [res4.results[c]["x4"] for c in cores]

    out = np.stack([np.ascontiguousarray(x.T) for x in x4s], axis=0)
    return out.astype(np.float32)


# version 05 (final)
# speedup vs baseline: 1.1186x; 1.1186x over previous
import numpy as np
import contextlib
import concourse.bass as bass
import concourse.mybir as mybir
from concourse.bass_utils import run_bass_kernel_spmd

F32 = mybir.dt.float32
BF16 = mybir.dt.bfloat16
AF = mybir.ActivationFunctionType
ALU = mybir.AluOpType
AX = mybir.AxisListType

class Sched:
    ENGINES = ("sync", "gpsimd", "tensor", "vector", "scalar")

    def __init__(self, nc):
        self.nc = nc
        self.q = {e: [] for e in self.ENGINES}
        self.tot = {}
        self._waited = {e: {} for e in self.ENGINES}

    def new_sem(self, stack, name):
        h = stack.enter_context(self.nc.semaphore(name))
        self.tot[name] = 0
        return h

    def emit(self, eng, fn, waits=(), inc=None):
        waits = [w for w in waits if w is not None]
        wd = self._waited[eng]
        norm = []
        for w in waits:
            s, v = w
            if v > 0 and wd.get(id(s), -1) < v:
                wd[id(s)] = v
                norm.append((s, v))
        after = None
        if inc is not None:
            s, name, n = inc
            self.tot[name] += n
            after = self.tot[name]

        def thunk(e, _fn=fn, _waits=tuple(norm), _inc=inc):
            for (s, v) in _waits:
                e.wait_ge(s, v)
            instr = _fn(e)
            if _inc is not None:
                s, name, n = _inc
                assert instr is not None, "inc requires an instruction"
                instr.then_inc(s, n)
        self.q[eng].append(thunk)
        return after

    def total(self, name):
        return self.tot[name]

    def run(self, block):
        sched = self

        @block.sync
        def _(e):
            for t in sched.q["sync"]:
                t(e)

        @block.gpsimd
        def _(e):
            for t in sched.q["gpsimd"]:
                t(e)

        @block.tensor
        def _(e):
            for t in sched.q["tensor"]:
                t(e)

        @block.vector
        def _(e):
            for t in sched.q["vector"]:
                t(e)

        @block.scalar
        def _(e):
            for t in sched.q["scalar"]:
                t(e)


D_MODEL = 2048
T_SEQ = 4096
NKT = 16
DSTATE = 16


def host_prep_mamba(inputs, core):
    x = np.asarray(inputs["L0_attn_x"][core])
    xT = np.ascontiguousarray(x.T)
    xpw = np.ascontiguousarray(
        np.asarray(inputs["L0_attn_x_proj_weight"]).reshape(NKT, 128, 96)
        .transpose(1, 0, 2).reshape(128, NKT * 96))
    dtw = np.ascontiguousarray(inputs["L0_attn_dt_proj_weight"])
    dtb = np.ascontiguousarray(
        np.asarray(inputs["L0_attn_dt_proj_bias"]).reshape(NKT, 128).T)
    alog = np.ascontiguousarray(
        np.asarray(inputs["L0_attn_A_log"]).reshape(NKT, 128, DSTATE)
        .transpose(1, 0, 2).reshape(128, NKT * DSTATE))
    dcol = np.ascontiguousarray(
        np.asarray(inputs["L0_attn_D"]).reshape(NKT, 128).T)
    sel16 = np.zeros((16, 16 * 128), np.float32)
    for n in range(16):
        sel16[n, n * 128:(n + 1) * 128] = 1.0
    ones_col = np.ones((128, 1), np.float32)
    ones_row = np.ones((1, 128), np.float32)
    fconst = np.tile(np.array([[1.0, 1e-5, -0.5, 0.0]], np.float32), (128, 1))
    return {
        "xT": xT, "xpw": xpw, "dtw": dtw, "dtb": dtb, "alog": alog,
        "dcol": dcol, "sel16": sel16, "ones_col": ones_col, "ones_row": ones_row,
        "fconst": fconst,
    }



def declare_mamba_dram(nc, T=T_SEQ, x_in=None, x_out=None):
    dr = {}
    dr["xT"] = x_in if x_in is not None else nc.dram_tensor(
        "xT", [D_MODEL, T], F32, kind="ExternalInput").ap()
    dr["xpw"] = nc.dram_tensor("xpw", [128, NKT * 96], F32, kind="ExternalInput").ap()
    dr["dtw"] = nc.dram_tensor("dtw", [64, D_MODEL], F32, kind="ExternalInput").ap()
    dr["dtb"] = nc.dram_tensor("dtb", [128, NKT], F32, kind="ExternalInput").ap()
    dr["alog"] = nc.dram_tensor("alog", [128, NKT * DSTATE], F32, kind="ExternalInput").ap()
    dr["dcol"] = nc.dram_tensor("dcol", [128, NKT], F32, kind="ExternalInput").ap()
    dr["sel16"] = nc.dram_tensor("sel16", [16, 2048], F32, kind="ExternalInput").ap()
    dr["ones_col"] = nc.dram_tensor("ones_col", [128, 1], F32, kind="ExternalInput").ap()
    dr["ones_row"] = nc.dram_tensor("ones_row", [1, 128], F32, kind="ExternalInput").ap()
    dr["fconst"] = nc.dram_tensor("fconst", [128, 4], F32, kind="ExternalInput").ap()
    dr["x1"] = x_out if x_out is not None else nc.dram_tensor(
        "x1", [D_MODEL, T], F32, kind="ExternalOutput").ap()
    return dr


def build_mamba(Tc=256, T=T_SEQ):
    nc = bass.Bass("TRN2", target_bir_lowering=False, debug=False)
    dr = declare_mamba_dram(nc, T=T)
    with contextlib.ExitStack() as stg:
        sems = {n: stg.enter_context(nc.semaphore("m_" + n))
                for n in ["s_ld", "s_lg", "s_mm", "s_sc", "s_vec", "s_st"]}
        emit_mamba(nc, sems, dr, Tc=Tc, T=T, barrier=())
    return nc


def emit_mamba(nc, sems, dr, Tc=256, T=T_SEQ, barrier=(), extra_gpsimd=()):
    NCH = T // Tc
    W = NKT * Tc
    xT_d = dr["xT"]; xpw_d = dr["xpw"]; dtw_d = dr["dtw"]; dtb_d = dr["dtb"]
    alog_d = dr["alog"]; dcol_d = dr["dcol"]; sel16_d = dr["sel16"]
    onec_d = dr["ones_col"]; oner_d = dr["ones_row"]; fconst_d = dr["fconst"]
    x1_d = dr["x1"]

    xT_r = xT_d.rearrange("(k p) t -> p k t", p=128)
    x1_r = x1_d.rearrange("(k p) t -> p k t", p=128)

    st = contextlib.ExitStack()
    S = lambda name, shape, dt: st.enter_context(nc.sbuf_tensor("m_" + name, shape, dt))
    Ps = lambda name, shape: st.enter_context(nc.psum_tensor("m_" + name, shape, F32))
    barrier = tuple(barrier)

    with st:
        xpw_bf = S("xpw_bf", [128, NKT * 96], BF16)
        dtw_bf = S("dtw_bf", [64, D_MODEL], BF16)
        dtb_s = S("dtb_s", [128, NKT], F32)
        alog_s = S("alog_s", [128, NKT * DSTATE], F32)
        acoef = S("acoef", [128, NKT * DSTATE], F32)
        dcol_s = S("dcol_s", [128, NKT], F32)
        d1col = S("d1col", [128, NKT], F32)
        sel_bf = S("sel_bf", [16, 2048], BF16)
        onec_bf = S("onec_bf", [128, 1], BF16)
        oner_bf = S("oner_bf", [1, 128], BF16)
        fc_s = S("fc_s", [128, 4], F32)

        x_s = S("x_s", [128, W], F32)
        x_bf = S("x_bf", [128, W], BF16)
        delta_all = S("delta_all", [128, W], F32)
        u_s = S("u_s", [128, W], F32)
        y_all = S("y_all", [128, W], F32)
        Bbc = S("Bbc", [128, DSTATE * Tc], BF16)
        Cbc = S("Cbc", [128, DSTATE * Tc], BF16)
        deltain_bf = S("deltain_bf", [64, Tc], BF16)
        Bsb = S("Bsb", [16, Tc], BF16)
        Csb = S("Csb", [16, Tc], BF16)
        ez = S("ez", [128, max(Tc, NKT * DSTATE)], F32)
        Tp = Tc + 1
        dA = [S(f"dA{i}", [128, DSTATE * Tp], F32) for i in range(2)]
        dBu = S("dBu", [128, DSTATE * Tp], F32)
        states = S("states", [128, DSTATE * Tp], BF16)
        ymul_bf = S("ymul_bf", [128, DSTATE * Tc], BF16)
        carry = S("carry", [128, NKT * DSTATE], F32)
        resid_bf = S("resid_bf", [128, W], BF16)
        ln_tmp = S("ln_tmp", [1, 4 * Tc], F32)
        mr_bf = S("mr_bf", [1, 2 * Tc], BF16)

        P0 = Ps("P0", [64, 512])
        P1 = Ps("P1", [16, 512])
        P2 = Ps("P2", [128, 512])
        P3 = Ps("P3", [128, 512])
        P4 = Ps("P4", [128, 512])
        P5 = Ps("P5", [128, 512])
        P6 = Ps("P6", [1, 512])
        P7 = Ps("P7", [128, 512])

        s_ld = sems["s_ld"]; s_lg = sems["s_lg"]; s_mm = sems["s_mm"]
        s_sc = sems["s_sc"]; s_vec = sems["s_vec"]; s_st = sems["s_st"]

        MM = 44
        SC = 45
        VC = 21
        INIT_LD = 64
        INIT_LG = 80
        INIT_SC = 1
        INIT_VC = 1

        def mmb(c): return c * MM
        def scb(c): return INIT_SC + c * SC
        def vcb(c): return INIT_VC + c * VC

        nTc = DSTATE * Tc

        with nc.Block() as block:

            @block.sync
            def _(sync):
                for (bs, bv) in barrier:
                    sync.wait_ge(bs, bv)
                sync.dma_start(out=dtb_s[:], in_=dtb_d).then_inc(s_ld, 16)
                sync.dma_start(out=alog_s[:], in_=alog_d).then_inc(s_ld, 16)
                sync.dma_start(out=dcol_s[:], in_=dcol_d).then_inc(s_ld, 16)
                sync.dma_start(out=fc_s[:], in_=fconst_d).then_inc(s_ld, 16)
                for c in range(NCH):
                    if c > 0:
                        sync.wait_ge(s_vec, vcb(c - 1) + VC)
                        sync.dma_start(
                            out=x1_r[:, :, (c - 1) * Tc:c * Tc],
                            in_=u_s.ap().rearrange("p (k t) -> p k t", k=NKT),
                        ).then_inc(s_st, 16)
                        sync.wait_ge(s_sc, scb(c - 1) + 43)
                    else:
                        sync.wait_ge(s_ld, 64)
                    sync.dma_start(
                        out=x_s.ap().rearrange("p (k t) -> p k t", k=NKT),
                        in_=xT_r[:, :, c * Tc:(c + 1) * Tc]).then_inc(s_ld, 16)
                c = NCH
                sync.wait_ge(s_vec, vcb(c - 1) + VC)
                sync.dma_start(
                    out=x1_r[:, :, (c - 1) * Tc:c * Tc],
                    in_=u_s.ap().rearrange("p (k t) -> p k t", k=NKT),
                ).then_inc(s_st, 16)
                sync.wait_ge(s_st, NCH * 16)

            @block.gpsimd
            def _(g):
                for (bs, bv) in barrier:
                    g.wait_ge(bs, bv)
                extra = list(extra_gpsimd)
                g.dma_start(out=xpw_bf[:], in_=xpw_d).then_inc(s_lg, 16)
                g.dma_start(out=dtw_bf[:], in_=dtw_d).then_inc(s_lg, 16)
                g.dma_start(out=sel_bf[:], in_=sel16_d).then_inc(s_lg, 16)
                g.dma_start(out=onec_bf[:], in_=onec_d).then_inc(s_lg, 16)
                g.dma_start(out=oner_bf[:], in_=oner_d).then_inc(s_lg, 16)
                for c in range(NCH):
                    if c > 0:
                        g.wait_ge(s_sc, scb(c - 1) + 44)
                        g.wait_ge(s_mm, mmb(c - 1) + 43)
                    g.dma_start(
                        out=x_bf.ap().rearrange("p (k t) -> p k t", k=NKT),
                        in_=xT_r[:, :, c * Tc:(c + 1) * Tc]).then_inc(s_lg, 16)
                    for _ in range(2):
                        if extra:
                            extra.pop(0)(g)
                while extra:
                    extra.pop(0)(g)

            @block.tensor
            def _(te):
                for (bs, bv) in barrier:
                    te.wait_ge(bs, bv)
                for c in range(NCH):
                    te.wait_ge(s_ld, INIT_LD + 16 * (c + 1))
                    te.wait_ge(s_lg, INIT_LG + 16 * (c + 1))
                    if c > 0:
                        te.wait_ge(s_sc, scb(c))
                        te.wait_ge(s_vec, vcb(c))
                    else:
                        te.wait_ge(s_vec, INIT_VC)
                    for k in range(NKT):
                        i = nc.tensor.matmul(
                            P0[0:64, 0:Tc],
                            xpw_bf[:, k * 96:k * 96 + 64],
                            x_bf[:, k * Tc:(k + 1) * Tc],
                            start=(k == 0), stop=(k == NKT - 1))
                    i.then_inc(s_mm, 1)
                    for k in range(NKT):
                        nc.tensor.matmul(
                            P1[0:16, 0:Tc],
                            xpw_bf[:, k * 96 + 64:k * 96 + 80],
                            x_bf[:, k * Tc:(k + 1) * Tc],
                            start=(k == 0), stop=(k == NKT - 1))
                    for k in range(NKT):
                        i = nc.tensor.matmul(
                            P1[0:16, Tc:2 * Tc],
                            xpw_bf[:, k * 96 + 80:k * 96 + 96],
                            x_bf[:, k * Tc:(k + 1) * Tc],
                            start=(k == 0), stop=(k == NKT - 1))
                    i.then_inc(s_mm, 1)
                    for kk in range(8):
                        te.wait_ge(s_sc, scb(c) + 1)
                        if kk >= 2:
                            te.wait_ge(s_sc, scb(c) + 2 + (kk - 1))
                        Pb = P2 if kk % 2 == 0 else P3
                        nc.tensor.matmul(
                            Pb[:, 0:Tc], dtw_bf[:, (2 * kk) * 128:(2 * kk + 1) * 128],
                            deltain_bf[:], start=True, stop=True)
                        nc.tensor.matmul(
                            Pb[:, Tc:2 * Tc], dtw_bf[:, (2 * kk + 1) * 128:(2 * kk + 2) * 128],
                            deltain_bf[:], start=True, stop=True).then_inc(s_mm, 1)
                    for n in range(DSTATE):
                        te.wait_ge(s_sc, scb(c) + 2)
                        if n >= 2:
                            te.wait_ge(s_sc, scb(c) + 10 + (n - 1))
                        Pb = P4 if n % 2 == 0 else P5
                        nc.tensor.matmul(Pb[:, 0:Tc], sel_bf[:, n * 128:(n + 1) * 128],
                                         Bsb[:], start=True, stop=True).then_inc(s_mm, 1)
                        nc.tensor.matmul(Pb[:, Tc:2 * Tc], sel_bf[:, n * 128:(n + 1) * 128],
                                         Csb[:], start=True, stop=True).then_inc(s_mm, 1)
                    te.wait_ge(s_sc, scb(c) + 44)
                    te.wait_ge(s_vec, vcb(c) + 19)
                    for k in range(NKT):
                        nc.tensor.matmul(P6[0:1, 0:Tc], onec_bf[:],
                                         resid_bf[:, k * Tc:(k + 1) * Tc],
                                         start=(k == 0), stop=(k == NKT - 1))
                    for k in range(NKT):
                        i = nc.tensor.matmul(P6[0:1, Tc:2 * Tc], onec_bf[:],
                                             x_bf[:, k * Tc:(k + 1) * Tc],
                                             start=(k == 0), stop=(k == NKT - 1))
                    i.then_inc(s_mm, 1)
                    te.wait_ge(s_sc, scb(c) + 45)
                    nc.tensor.matmul(P7[:, 0:Tc], oner_bf[:], mr_bf[0:1, 0:Tc],
                                     start=True, stop=True)
                    nc.tensor.matmul(P7[:, Tc:2 * Tc], oner_bf[:], mr_bf[0:1, Tc:2 * Tc],
                                     start=True, stop=True).then_inc(s_mm, 1)

            @block.scalar
            def _(sc):
                for (bs, bv) in barrier:
                    sc.wait_ge(bs, bv)
                sc.wait_ge(s_ld, 64)
                sc.activation(out=ez[:, 0:NKT * DSTATE], in_=alog_s[:],
                              func=AF.Exp).then_inc(s_sc, 1)
                for c in range(NCH):
                    sc.wait_ge(s_mm, mmb(c) + 1)
                    sc.copy(deltain_bf[:], P0[0:64, 0:Tc]).then_inc(s_sc, 1)
                    sc.wait_ge(s_mm, mmb(c) + 2)
                    sc.copy(Bsb[:], P1[0:16, 0:Tc])
                    sc.copy(Csb[:], P1[0:16, Tc:2 * Tc]).then_inc(s_sc, 1)
                    if c > 0:
                        sc.wait_ge(s_st, 16 * c)
                    for kk in range(8):
                        sc.wait_ge(s_mm, mmb(c) + 2 + kk + 1)
                        Pb = P2 if kk % 2 == 0 else P3
                        for j in range(2):
                            k = 2 * kk + j
                            sc.activation(out=ez[:, 0:Tc], in_=Pb[:, j * Tc:(j + 1) * Tc],
                                          func=AF.Exp, bias=dtb_s[:, k:k + 1])
                            i = sc.activation(out=delta_all[:, k * Tc:(k + 1) * Tc],
                                              in_=ez[:, 0:Tc], func=AF.Ln, bias=fc_s[:, 0:1])
                        i.then_inc(s_sc, 1)
                    for n in range(DSTATE):
                        sc.wait_ge(s_mm, mmb(c) + 10 + 2 * n + 2)
                        Pb = P4 if n % 2 == 0 else P5
                        sc.copy(Bbc[:, n * Tc:(n + 1) * Tc], Pb[:, 0:Tc])
                        sc.copy(Cbc[:, n * Tc:(n + 1) * Tc], Pb[:, Tc:2 * Tc]).then_inc(s_sc, 1)
                    for k in range(NKT):
                        if k >= 2:
                            sc.wait_ge(s_vec, vcb(c) + 1 + (k - 1))
                        for n in range(DSTATE):
                            i = sc.activation(
                                out=dA[k % 2][:, n * Tp + 1:(n + 1) * Tp],
                                in_=delta_all[:, k * Tc:(k + 1) * Tc],
                                func=AF.Exp, scale=acoef[:, k * DSTATE + n:k * DSTATE + n + 1])
                        i.then_inc(s_sc, 1)
                    sc.wait_ge(s_vec, vcb(c) + 17)
                    if c > 0:
                        sc.wait_ge(s_st, 16 * c)
                    for k in range(NKT):
                        i = sc.activation(out=u_s[:, k * Tc:(k + 1) * Tc],
                                          in_=x_s[:, k * Tc:(k + 1) * Tc],
                                          func=AF.Copy, scale=d1col[:, k:k + 1])
                    i.then_inc(s_sc, 1)
                    sc.wait_ge(s_vec, vcb(c) + 18)
                    sc.activation(out=x_bf[:], in_=delta_all[:],
                                  func=AF.Square).then_inc(s_sc, 1)
                    sc.wait_ge(s_vec, vcb(c) + 20)
                    sc.activation(out=ln_tmp[0:1, Tc:2 * Tc], in_=ln_tmp[0:1, 3 * Tc:4 * Tc],
                                  func=AF.Ln, bias=fc_s[0:1, 1:2])
                    sc.activation(out=ln_tmp[0:1, 2 * Tc:3 * Tc], in_=ln_tmp[0:1, Tc:2 * Tc],
                                  func=AF.Exp, scale=fc_s[0:1, 2:3])
                    sc.copy(mr_bf[0:1, 0:Tc], ln_tmp[0:1, 0:Tc])
                    sc.copy(mr_bf[0:1, Tc:2 * Tc], ln_tmp[0:1, 2 * Tc:3 * Tc]).then_inc(s_sc, 1)

            @block.vector
            def _(v):
                for (bs, bv) in barrier:
                    v.wait_ge(bs, bv)
                v.wait_ge(s_sc, 1)
                v.wait_ge(s_ld, 64)
                v.tensor_scalar_mul(acoef[:], ez[:, 0:NKT * DSTATE], -1.0)
                v.tensor_scalar_add(d1col[:], dcol_s[:], 1.0)
                v.memset(dA[0].ap().rearrange("p (n t) -> p n t", n=DSTATE)[:, :, 0:1], 0.0)
                v.memset(dA[1].ap().rearrange("p (n t) -> p n t", n=DSTATE)[:, :, 0:1], 0.0)
                v.memset(carry[:], 0.0).then_inc(s_vec, 1)
                for c in range(NCH):
                    v.wait_ge(s_sc, scb(c) + 10)
                    v.tensor_tensor(out=u_s[:], in0=delta_all[:], in1=x_s[:],
                                    op=ALU.mult).then_inc(s_vec, 1)
                    for k in range(NKT):
                        v.wait_ge(s_sc, scb(c) + 27 + k)
                        u_b = u_s.ap()[:, k * Tc:(k + 1) * Tc][:, None, :].broadcast_to(
                            (128, DSTATE, Tc))
                        v.tensor_copy(
                            dBu.ap().rearrange("p (n t) -> p n t", n=DSTATE)[:, :, 0],
                            carry[:, k * DSTATE:(k + 1) * DSTATE])
                        v.tensor_tensor(
                            out=dBu.ap().rearrange("p (n t) -> p n t", n=DSTATE)[:, :, 1:Tp],
                            in0=u_b,
                            in1=Bbc.ap().rearrange("p (n t) -> p n t", n=DSTATE),
                            op=ALU.mult)
                        v.tensor_tensor_scan(
                            out=states[:], data0=dA[k % 2][:], data1=dBu[:],
                            initial=0.0, op0=ALU.mult, op1=ALU.add)
                        v.tensor_copy(
                            carry[:, k * DSTATE:(k + 1) * DSTATE],
                            states.ap().rearrange("p (n t) -> p n t", n=DSTATE)[:, :, Tp - 1])
                        v.tensor_tensor(
                            out=ymul_bf.ap().rearrange("p (n t) -> p n t", n=DSTATE),
                            in0=states.ap().rearrange("p (n t) -> p n t", n=DSTATE)[:, :, 1:Tp],
                            in1=Cbc.ap().rearrange("p (n t) -> p n t", n=DSTATE),
                            op=ALU.mult)
                        ym = ymul_bf
                        v.tensor_tensor(out=ym[:, 0:8 * Tc], in0=ym[:, 0:8 * Tc],
                                        in1=ym[:, 8 * Tc:16 * Tc], op=ALU.add)
                        v.tensor_tensor(out=ym[:, 0:4 * Tc], in0=ym[:, 0:4 * Tc],
                                        in1=ym[:, 4 * Tc:8 * Tc], op=ALU.add)
                        v.tensor_tensor(out=ym[:, 0:2 * Tc], in0=ym[:, 0:2 * Tc],
                                        in1=ym[:, 2 * Tc:4 * Tc], op=ALU.add)
                        v.tensor_tensor(out=y_all[:, k * Tc:(k + 1) * Tc],
                                        in0=ym[:, 0:Tc], in1=ym[:, Tc:2 * Tc],
                                        op=ALU.add).then_inc(s_vec, 1)
                    v.wait_ge(s_sc, scb(c) + 43)
                    v.tensor_tensor(out=delta_all[:], in0=u_s[:], in1=y_all[:],
                                    op=ALU.add).then_inc(s_vec, 1)
                    v.tensor_copy(resid_bf[:], delta_all[:]).then_inc(s_vec, 1)
                    v.wait_ge(s_mm, mmb(c) + 43)
                    v.tensor_scalar(out=ln_tmp[0:1, 0:Tc], in0=P6[0:1, 0:Tc],
                                    scalar1=1.0 / D_MODEL, scalar2=None, op0=ALU.mult)
                    v.tensor_scalar(out=ln_tmp[0:1, Tc:2 * Tc], in0=P6[0:1, Tc:2 * Tc],
                                    scalar1=1.0 / D_MODEL, scalar2=None, op0=ALU.mult)
                    v.tensor_tensor(out=ln_tmp[0:1, 2 * Tc:3 * Tc], in0=ln_tmp[0:1, 0:Tc],
                                    in1=ln_tmp[0:1, 0:Tc], op=ALU.mult)
                    v.tensor_tensor(out=ln_tmp[0:1, 3 * Tc:4 * Tc], in0=ln_tmp[0:1, Tc:2 * Tc],
                                    in1=ln_tmp[0:1, 2 * Tc:3 * Tc],
                                    op=ALU.subtract).then_inc(s_vec, 1)
                    v.wait_ge(s_mm, mmb(c) + 44)
                    mean_b = P7.ap()[:, 0:Tc][:, None, :].broadcast_to((128, NKT, Tc))
                    rstd_b = P7.ap()[:, Tc:2 * Tc][:, None, :].broadcast_to((128, NKT, Tc))
                    v.tensor_tensor(out=y_all.ap().rearrange("p (k t) -> p k t", k=NKT),
                                    in0=delta_all.ap().rearrange("p (k t) -> p k t", k=NKT),
                                    in1=mean_b, op=ALU.subtract)
                    v.tensor_tensor(out=u_s.ap().rearrange("p (k t) -> p k t", k=NKT),
                                    in0=y_all.ap().rearrange("p (k t) -> p k t", k=NKT),
                                    in1=rstd_b, op=ALU.mult).then_inc(s_vec, 1)

    return (sems["s_st"], NCH * 16)


D_MODEL = 2048
NKT = 16
SH_INTER = 5632
MOE_INTER = 1408
NEXP = 8


def host_prep_moe(inputs, x1):
    gup = np.asarray(inputs["L0_ffn_shared_gate_up_w"])
    gup_sl = np.ascontiguousarray(
        gup.reshape(88, 128, 16, 128).transpose(0, 3, 2, 1)
        .reshape(88, 128, 2048))
    sdw = np.asarray(inputs["L0_ffn_shared_down_w"])
    sdw_sl = np.ascontiguousarray(
        sdw.reshape(16, 128, 2, 22, 128).transpose(2, 0, 4, 3, 1)
        .reshape(2, 16, 128, 22 * 128))
    w13 = np.asarray(inputs["L0_ffn_w13"])
    w13_sl = np.ascontiguousarray(
        w13.reshape(8, 22, 128, 16, 128).transpose(0, 1, 4, 3, 2)
        .reshape(8, 22, 128, 2048))
    w2 = np.asarray(inputs["L0_ffn_w2"])
    w2_sl = np.ascontiguousarray(
        w2.reshape(8, 16, 128, 11, 128).transpose(0, 1, 4, 3, 2)
        .reshape(8, 16, 128, 11 * 128))
    gwT = np.ascontiguousarray(np.asarray(inputs["L0_ffn_gate_w"]).T)
    sgw_col = np.ascontiguousarray(
        np.asarray(inputs["L0_ffn_shared_gate_w"])[0].reshape(16, 128).T)
    sel8 = np.zeros((8, 8 * 128), np.float32)
    for n in range(8):
        sel8[n, n * 128:(n + 1) * 128] = 1.0
    return {
        "x1": (np.ascontiguousarray(x1) if x1 is not None else None), "gup_sl": gup_sl, "sdw_sl": sdw_sl,
        "w13_sl": w13_sl, "w2_sl": w2_sl, "gwT": gwT, "sgw_col": sgw_col, "sel8": sel8,
        "ident": np.eye(128, dtype=np.float32),
        "ones_col": np.ones((128, 1), np.float32),
        "ones_row": np.ones((1, 128), np.float32),
        "fconst": np.tile(np.array([[1.0, 1e-5, -0.5, 0.0]], np.float32), (128, 1)),
    }


MOE_SEMS = ["s_xl", "s_xb", "s_wd", "s_mm", "s_sc", "s_vec", "s_st", "s_wp"]


def declare_moe_dram(nc, T=4096, x_in=None, x_out=None, debug=False, pfx=""):
    dr = {}
    dr["x1"] = x_in if x_in is not None else nc.dram_tensor(
        "x1", [D_MODEL, T], F32, kind="ExternalInput").ap()
    dr["gup_sl"] = nc.dram_tensor("gup_sl", [88, 128, 2048], F32, kind="ExternalInput").ap()
    dr["sdw_sl"] = nc.dram_tensor("sdw_sl", [2, 16, 128, 22 * 128], F32, kind="ExternalInput").ap()
    dr["w13_sl"] = nc.dram_tensor("w13_sl", [NEXP, 22, 128, 2048], F32, kind="ExternalInput").ap()
    dr["w2_sl"] = nc.dram_tensor("w2_sl", [NEXP, 16, 128, 11 * 128], F32, kind="ExternalInput").ap()
    dr["gwT"] = nc.dram_tensor("gwT", [D_MODEL, 8], F32, kind="ExternalInput").ap()
    dr["sgw_col"] = nc.dram_tensor("sgw_col", [128, NKT], F32, kind="ExternalInput").ap()
    dr["sel8"] = nc.dram_tensor("sel8", [8, 1024], F32, kind="ExternalInput").ap()
    dr["ident"] = nc.dram_tensor("ident", [128, 128], F32, kind="ExternalInput").ap()
    dr["ones_col"] = nc.dram_tensor("ones_col" + pfx, [128, 1], F32, kind="ExternalInput").ap()
    dr["ones_row"] = nc.dram_tensor("ones_row" + pfx, [1, 128], F32, kind="ExternalInput").ap()
    dr["fconst"] = nc.dram_tensor("fconst" + pfx, [128, 4], F32, kind="ExternalInput").ap()
    dr["x2"] = x_out if x_out is not None else nc.dram_tensor(
        "x2", [D_MODEL, T], F32, kind="ExternalOutput").ap()
    dr["gup_bf"] = nc.dram_tensor("gup_bf", [88, 128, 2048], BF16, kind="Internal").ap()
    dr["sdw_bf"] = nc.dram_tensor("sdw_bf", [2, 16, 128, 22 * 128], BF16, kind="Internal").ap()
    dr["w13_bf"] = nc.dram_tensor("w13_bf", [NEXP, 22, 128, 2048], BF16, kind="Internal").ap()
    dr["w2_bf"] = nc.dram_tensor("w2_bf", [NEXP, 16, 128, 11 * 128], BF16, kind="Internal").ap()
    if debug:
        dr["dbg_w"] = nc.dram_tensor("dbg_w", [128, 32], F32, kind="ExternalOutput").ap()
        dr["dbg_sh"] = nc.dram_tensor("dbg_sh", [128, NKT * 512], F32, kind="ExternalOutput").ap()
        dr["dbg_hid"] = nc.dram_tensor("dbg_hid", [128, 11 * 512], F32, kind="ExternalOutput").ap()
    return dr


def moe_prepass_thunks(dr, s_wp):
    jobs = []
    for j0 in range(0, 88, 11):
        jobs.append((dr["gup_sl"][j0:j0 + 11], dr["gup_bf"][j0:j0 + 11]))
    for e in range(NEXP):
        jobs.append((dr["w13_sl"][e], dr["w13_bf"][e]))
        jobs.append((dr["w2_sl"][e], dr["w2_bf"][e]))
    for H in range(2):
        jobs.append((dr["sdw_sl"][H], dr["sdw_bf"][H]))
    thunks = []
    for (srcap, dstap) in jobs:
        def t(e, _s=srcap, _d=dstap, _sem=s_wp):
            e.dma_start(out=_d, in_=_s).then_inc(_sem, 16)
        thunks.append(t)
    return thunks, 16 * len(jobs)


def build_moe(Tc=512, T=4096, debug=False):
    nc = bass.Bass("TRN2", target_bir_lowering=False, debug=False)
    dr = declare_moe_dram(nc, T=T, debug=debug)
    with contextlib.ExitStack() as stg:
        sems = {n: stg.enter_context(nc.semaphore("f_" + n)) for n in MOE_SEMS}
        emit_moe(nc, sems, dr, Tc=Tc, T=T, barrier=(), debug=debug)
    return nc


def emit_moe(nc, sems, dr, Tc=512, T=4096, barrier=(), debug=False, pp_done=None):
    NCH = T // Tc
    W = NKT * Tc
    NB = Tc // 128
    x1_d = dr["x1"]; gup_d = dr["gup_sl"]; sdw_d = dr["sdw_sl"]
    w13_d = dr["w13_sl"]; w2_d = dr["w2_sl"]; gwT_d = dr["gwT"]
    sgw_d = dr["sgw_col"]; sel8_d = dr["sel8"]; ident_d = dr["ident"]
    onec_d = dr["ones_col"]; oner_d = dr["ones_row"]; fconst_d = dr["fconst"]
    x2_d = dr["x2"]
    gup_bfd = dr["gup_bf"]; sdw_bfd = dr["sdw_bf"]
    w13_bfd = dr["w13_bf"]; w2_bfd = dr["w2_bf"]
    if debug:
        dbg_w_d = dr["dbg_w"]; dbg_sh_d = dr["dbg_sh"]; dbg_hid_d = dr["dbg_hid"]

    x1_r = x1_d.rearrange("(k p) t -> p k t", p=128)
    x2_r = x2_d.rearrange("(k p) t -> p k t", p=128)
    gwT_r = gwT_d.rearrange("(k p) e -> p k e", p=128)

    st = contextlib.ExitStack()
    S = lambda name, shape, dt: st.enter_context(nc.sbuf_tensor("f_" + name, shape, dt))
    Ps = lambda name, shape: st.enter_context(nc.psum_tensor("f_" + name, shape, F32))

    with st:
        sch = Sched(nc)
        for _eng in Sched.ENGINES:
            sch.emit(_eng, lambda e: None, waits=tuple(barrier))
        x_bf = S("x_bf", [128, W], BF16)
        x_f = S("x_f", [128, W], F32)
        slabG = [S(f"slabG{i}", [128, NKT * 128], BF16) for i in range(2)]
        slabU = [S(f"slabU{i}", [128, NKT * 128], BF16) for i in range(2)]
        slabD = [S(f"slabD{i}", [128, 22 * 128], BF16) for i in range(2)]
        hid = S("hid", [128, 22 * Tc], BF16)
        out_acc = S("out_acc", [128, W], F32)
        tmp_silu = [S(f"tmps{i}", [128, Tc], BF16) for i in range(2)]
        gw_bf = S("gw_bf", [128, NKT * 8], BF16)
        sgw_bf = S("sgw_bf", [128, NKT * 1], BF16)
        sel8_bf = S("sel8_bf", [8, 1024], BF16)
        ident_f = S("ident_f", [128, 128], F32)
        onec_bf = S("onec_bf", [128, 1], BF16)
        oner_bf = S("oner_bf", [1, 128], BF16)
        fc_s = S("fc_s", [128, 4], F32)
        nmax = S("nmax", [128, NB], F32)
        ssum = S("ssum", [128, NB], F32)
        sinv = S("sinv", [128, NB], F32)
        pexp = S("pexp", [128, NB * 8], F32)
        pfull = S("pfull", [128, NB * 8], F32)
        m1 = S("m1", [128, NB], F32)
        m2 = S("m2", [128, NB], F32)
        ge = S("ge", [128, NB * 8], F32)
        pmask = S("pmask", [128, NB * 8], F32)
        wfull = S("wfull", [128, NB * 8], F32)
        wrow = S("wrow", [8, Tc], BF16)
        sig_bf = S("sig_bf", [1, Tc], BF16)
        resid_bf = S("resid_bf", [128, W], BF16)
        ln_tmp = S("ln_tmp", [1, 4 * Tc], F32)
        mr_bf = S("mr_bf", [1, 2 * Tc], BF16)
        spc = S("spc", [128, 1024], F32)

        PB = [Ps(f"PB{i}", [128, 512]) for i in range(8)]

        s_xl = sems["s_xl"]; s_xb = sems["s_xb"]; s_wd = sems["s_wd"]
        s_mm = sems["s_mm"]; s_sc = sems["s_sc"]; s_vec = sems["s_vec"]
        s_st = sems["s_st"]
        for n in MOE_SEMS:
            sch.tot[n] = 0
        E = sch.emit
        s_wp = sems["s_wp"]
        if pp_done is None:
            thunks, wp_tot = moe_prepass_thunks(dr, s_wp)
            for t in thunks:
                E("gpsimd", t)
            sch.tot["s_wp"] = wp_tot
            pp_done = (s_wp, wp_tot)

        E("gpsimd", lambda e: e.dma_start(out=sel8_bf[:], in_=sel8_d), inc=(s_xb, "s_xb", 16))
        E("gpsimd", lambda e: e.dma_start(out=onec_bf[:], in_=onec_d), inc=(s_xb, "s_xb", 16))
        E("gpsimd", lambda e: e.dma_start(out=oner_bf[:], in_=oner_d), inc=(s_xb, "s_xb", 16))
        E("gpsimd", lambda e: e.dma_start(
            out=gw_bf.ap().rearrange("p (k e) -> p k e", k=NKT), in_=gwT_r),
            inc=(s_xb, "s_xb", 16))
        E("gpsimd", lambda e: e.dma_start(out=sgw_bf[:], in_=sgw_d),
            inc=(s_xb, "s_xb", 16))
        E("sync", lambda e: e.dma_start(out=fc_s[:], in_=fconst_d), inc=(s_xl, "s_xl", 16))
        E("sync", lambda e: e.dma_start(out=ident_f[:], in_=ident_d), inc=(s_xl, "s_xl", 16))
        E("vector", lambda e: e.memset(spc[:], 0.0), inc=(s_vec, "s_vec", 1))
        INIT_XB = sch.total("s_xb")
        INIT_XL = sch.total("s_xl")

        bank_free = {i: None for i in range(8)}
        tmp_free = {0: None, 1: None}
        slab_free = {("G", 0): None, ("G", 1): None, ("U", 0): None, ("U", 1): None,
                     ("D", 0): None, ("D", 1): None}
        prev_apply = None
        prev_store_tot = 0
        prev_lastup = None
        prev_residadd = None
        state = {"last_xbf_mm": None}

        def up_phase(jn, gsrc, usrc, hid_slot):
            par = jn % 2
            gs = E("sync", lambda e, _g=gsrc, _p=par: e.dma_start(
                out=slabG[_p][:], in_=_g),
                waits=[slab_free[("G", par)], pp_done], inc=(s_wd, "s_wd", 16))
            us = E("sync", lambda e, _u=usrc, _p=par: e.dma_start(
                out=slabU[_p][:], in_=_u),
                waits=[slab_free[("U", par)], pp_done], inc=(s_wd, "s_wd", 16))
            bg = 0 if par == 0 else 2
            bu = 1 if par == 0 else 3

            def pg_mms(e, _p=par, _b=bg):
                i = None
                for k in range(NKT):
                    i = nc.tensor.matmul(PB[_b][:, 0:Tc],
                                         slabG[_p][:, k * 128:(k + 1) * 128],
                                         x_bf[:, k * Tc:(k + 1) * Tc],
                                         start=(k == 0), stop=(k == NKT - 1))
                return i
            pg = E("tensor", pg_mms, waits=[(s_wd, sch.total("s_wd")), bank_free[bg]],
                   inc=(s_mm, "s_mm", 1))
            slab_free[("G", par)] = (s_mm, pg)

            def pu_mms(e, _p=par, _b=bu):
                i = None
                for k in range(NKT):
                    i = nc.tensor.matmul(PB[_b][:, 0:Tc],
                                         slabU[_p][:, k * 128:(k + 1) * 128],
                                         x_bf[:, k * Tc:(k + 1) * Tc],
                                         start=(k == 0), stop=(k == NKT - 1))
                return i
            pu = E("tensor", pu_mms, waits=[(s_wd, sch.total("s_wd")), bank_free[bu]],
                   inc=(s_mm, "s_mm", 1))
            slab_free[("U", par)] = (s_mm, pu)
            state["last_xbf_mm"] = pu
            sl = E("scalar", lambda e, _p=par, _b=bg: e.activation(
                out=tmp_silu[_p][:], in_=PB[_b][:, 0:Tc], func=AF.Silu),
                waits=[(s_mm, pg), tmp_free[par]], inc=(s_sc, "s_sc", 1))
            bank_free[bg] = (s_sc, sl)
            hv = E("vector", lambda e, _p=par, _b=bu, _h=hid_slot: e.tensor_tensor(
                out=hid[:, _h * Tc:(_h + 1) * Tc], in0=tmp_silu[_p][:],
                in1=PB[_b][:, 0:Tc], op=ALU.mult),
                waits=[(s_mm, pu), (s_sc, sl)], inc=(s_vec, "s_vec", 1))
            bank_free[bu] = (s_vec, hv)
            tmp_free[par] = (s_vec, hv)
            return hv

        def down_phase(dtn, nslab, dsrc, evac_mode, hid_ready):
            par = dtn % 2
            ds = E("sync", lambda e, _d=dsrc, _p=par, _n=nslab: e.dma_start(
                out=slabD[_p][:, 0:_n * 128], in_=_d),
                waits=[slab_free[("D", par)], pp_done], inc=(s_wd, "s_wd", 16))
            bb = 4 if par == 0 else 5

            def pd_mms(e, _p=par, _b=bb, _n=nslab):
                i = None
                for j in range(_n):
                    i = nc.tensor.matmul(PB[_b][:, 0:Tc],
                                         slabD[_p][:, j * 128:(j + 1) * 128],
                                         hid[:, j * Tc:(j + 1) * Tc],
                                         start=(j == 0), stop=(j == _n - 1))
                return i
            pd = E("tensor", pd_mms,
                   waits=[(s_wd, sch.total("s_wd")), bank_free[bb], (s_vec, hid_ready)],
                   inc=(s_mm, "s_mm", 1))
            slab_free[("D", par)] = (s_mm, pd)
            if evac_mode == "copy":
                ev = E("vector", lambda e, _b=bb, _dt=dtn: e.tensor_copy(
                    out_acc[:, _dt * Tc:(_dt + 1) * Tc], PB[_b][:, 0:Tc]),
                    waits=[(s_mm, pd)], inc=(s_vec, "s_vec", 1))
            else:
                ev = E("vector", lambda e, _b=bb, _dt=dtn: e.tensor_tensor(
                    out=out_acc[:, _dt * Tc:(_dt + 1) * Tc],
                    in0=out_acc[:, _dt * Tc:(_dt + 1) * Tc],
                    in1=PB[_b][:, 0:Tc], op=ALU.add),
                    waits=[(s_mm, pd)], inc=(s_vec, "s_vec", 1))
            bank_free[bb] = (s_vec, ev)
            return ev

        for c in range(NCH):
            t0 = c * Tc

            xf_waits = [prev_residadd and (s_vec, prev_residadd)]
            if prev_store_tot:
                xf_waits.append((s_st, prev_store_tot))
            xl_tot = E("sync", lambda e, t0=t0: e.dma_start(
                out=x_f.ap().rearrange("p (k t) -> p k t", k=NKT),
                in_=x1_r[:, :, t0:t0 + Tc]), waits=xf_waits, inc=(s_xl, "s_xl", 16))
            xb_waits = [prev_lastup and (s_mm, prev_lastup)]
            xb_tot = E("gpsimd", lambda e, t0=t0: e.dma_start(
                out=x_bf.ap().rearrange("p (k t) -> p k t", k=NKT),
                in_=x1_r[:, :, t0:t0 + Tc]), waits=xb_waits, inc=(s_xb, "s_xb", 16))

            def router_mms(e):
                i = None
                for b in range(NB):
                    for k in range(NKT):
                        i = nc.tensor.matmul(
                            PB[6][:, b * 8:(b + 1) * 8],
                            x_bf[:, k * Tc + b * 128:k * Tc + (b + 1) * 128],
                            gw_bf[:, k * 8:(k + 1) * 8],
                            start=(k == 0), stop=(k == NKT - 1))
                return i
            rt = E("tensor", router_mms,
                   waits=[(s_xb, xb_tot), bank_free[6],
                          prev_apply and (s_vec, prev_apply)],
                   inc=(s_mm, "s_mm", 1))

            def rt_max(e):
                i = None
                for b in range(NB):
                    i = e.reduce_max(nmax[:, b:b + 1], PB[6][:, b * 8:(b + 1) * 8],
                                     axis=AX.X, negate=True)
                return i
            rtm = E("vector", rt_max, waits=[(s_mm, rt)], inc=(s_vec, "s_vec", 1))

            def rt_exp(e):
                i = None
                for b in range(NB):
                    i = e.activation(out=pexp[:, b * 8:(b + 1) * 8],
                                     in_=PB[6][:, b * 8:(b + 1) * 8], func=AF.Exp,
                                     bias=nmax[:, b:b + 1],
                                     accum_out=ssum[:, b:b + 1])
                return i
            rte = E("scalar", rt_exp, waits=[(s_vec, rtm)], inc=(s_sc, "s_sc", 1))
            bank_free[6] = (s_sc, rte)

            def rt_recip(e):
                return e.reciprocal(sinv[:], ssum[:])
            rtr = E("vector", rt_recip, waits=[(s_sc, rte)], inc=(s_vec, "s_vec", 1))

            def rt_m1(e):
                i = None
                for b in range(NB):
                    i = e.reduce_max(m1[:, b:b + 1], pexp[:, b * 8:(b + 1) * 8],
                                     axis=AX.X, negate=True)
                return i
            rtm1 = E("vector", rt_m1, waits=[(s_sc, rte)], inc=(s_vec, "s_vec", 1))

            def rt_s1(e):
                i = None
                for b in range(NB):
                    i = e.activation(out=ge[:, b * 8:(b + 1) * 8],
                                     in_=pexp[:, b * 8:(b + 1) * 8], func=AF.Sign,
                                     bias=m1[:, b:b + 1])
                return i
            rs1 = E("scalar", rt_s1, waits=[(s_vec, rtm1)], inc=(s_sc, "s_sc", 1))

            def spacer(e):
                e.tensor_copy(spc[:, 512:1024], spc[:, 0:512])
            def rt_pm(e):
                e.tensor_scalar_add(pmask[:], ge[:], 1.0)
                spacer(e)
                e.tensor_scalar_mul(pmask[:], pmask[:], -1e30)
                spacer(e)
                e.tensor_tensor(out=pmask[:], in0=pmask[:], in1=pexp[:], op=ALU.add)
                spacer(e)
                i = None
                for b in range(NB):
                    i = e.reduce_max(m2[:, b:b + 1], pmask[:, b * 8:(b + 1) * 8],
                                     axis=AX.X, negate=True)
                return i
            rpm = E("vector", rt_pm, waits=[(s_sc, rs1)], inc=(s_vec, "s_vec", 1))

            def rt_s2(e):
                i = None
                for b in range(NB):
                    i = e.activation(out=ge[:, b * 8:(b + 1) * 8],
                                     in_=pexp[:, b * 8:(b + 1) * 8], func=AF.Sign,
                                     bias=m2[:, b:b + 1])
                return i
            rs2 = E("scalar", rt_s2, waits=[(s_vec, rpm)], inc=(s_sc, "s_sc", 1))

            def rt_w(e):
                e.tensor_scalar_add(pmask[:], ge[:], 1.0)
                spacer(e)
                e.tensor_scalar_min(pmask[:], pmask[:], 1.0)
                spacer(e)
                e.tensor_tensor(out=wfull[:], in0=pexp[:], in1=pmask[:], op=ALU.mult)
                spacer(e)
                i = None
                for b in range(NB):
                    i = e.tensor_tensor(out=wfull[:, b * 8:(b + 1) * 8],
                                        in0=wfull[:, b * 8:(b + 1) * 8],
                                        in1=sinv.ap()[:, b:b + 1].broadcast_to((128, 8)),
                                        op=ALU.mult)
                return i
            rtw = E("vector", rt_w, waits=[(s_sc, rs2)], inc=(s_vec, "s_vec", 1))
            rtt = rtw

            def w_tp(e):
                i = None
                for b in range(NB):
                    i = nc.tensor.transpose(PB[5][0:8, b * 128:(b + 1) * 128],
                                            wfull[:, b * 8:(b + 1) * 8], ident_f[:])
                return i
            wtp = E("tensor", w_tp, waits=[(s_vec, rtt), bank_free[5], (s_xl, INIT_XL)],
                    inc=(s_mm, "s_mm", 1))
            wr = E("scalar", lambda e: e.copy(wrow[:], PB[5][0:8, 0:Tc]),
                   waits=[(s_mm, wtp)], inc=(s_sc, "s_sc", 1))
            bank_free[5] = (s_sc, wr)

            def sg_mms(e):
                i = None
                for k in range(NKT):
                    i = nc.tensor.matmul(PB[7][0:1, 0:Tc],
                                         sgw_bf[:, k:k + 1],
                                         x_bf[:, k * Tc:(k + 1) * Tc],
                                         start=(k == 0), stop=(k == NKT - 1))
                return i
            sg = E("tensor", sg_mms, waits=[(s_xb, xb_tot), bank_free[7]],
                   inc=(s_mm, "s_mm", 1))
            sgs = E("scalar", lambda e: e.activation(out=sig_bf[:], in_=PB[7][0:1, 0:Tc],
                                                     func=AF.Sigmoid),
                    waits=[(s_mm, sg)], inc=(s_sc, "s_sc", 1))
            bank_free[7] = (s_sc, sgs)
            sigbc = E("tensor", lambda e: nc.tensor.matmul(
                PB[6][:, 0:Tc], oner_bf[:], sig_bf[:], start=True, stop=True),
                waits=[(s_sc, sgs), bank_free[6]], inc=(s_mm, "s_mm", 1))

            for H in range(2):
                hid_ready = None
                for j in range(22):
                    jj = H * 22 + j
                    gsrc = gup_bfd[jj]
                    usrc = gup_bfd[44 + jj]
                    hid_ready = up_phase(j, gsrc, usrc, j)
                for dt in range(NKT):
                    dsrc = sdw_bfd[H, dt]
                    down_phase(dt, 22, dsrc, "copy" if H == 0 else "add", hid_ready)

            def gate_tt(e):
                sb = PB[6].ap()[:, 0:Tc][:, None, :].broadcast_to((128, NKT, Tc))
                return e.tensor_tensor(out=out_acc.ap().rearrange("p (k t) -> p k t", k=NKT),
                                       in0=out_acc.ap().rearrange("p (k t) -> p k t", k=NKT),
                                       in1=sb, op=ALU.mult)
            gt = E("vector", gate_tt, waits=[(s_mm, sigbc)], inc=(s_vec, "s_vec", 1))
            bank_free[6] = (s_vec, gt)
            if debug and c == 0:
                E("sync", lambda e: e.dma_start(out=dbg_w_d, in_=wfull[:]),
                  waits=[(s_vec, gt)], inc=(s_st, "s_st", 16))
                dbg1 = E("sync", lambda e: e.dma_start(out=dbg_sh_d, in_=out_acc[:]),
                  waits=[(s_vec, gt)], inc=(s_st, "s_st", 16))
                prev_store_tot = dbg1

            for ex in range(NEXP):
                pw = E("tensor", lambda e, _e=ex: nc.tensor.matmul(
                    PB[6][:, 0:Tc], sel8_bf[:, _e * 128:(_e + 1) * 128], wrow[:],
                    start=True, stop=True),
                    waits=[bank_free[6], (s_sc, wr)], inc=(s_mm, "s_mm", 1))
                hid_ready = None
                for j in range(11):
                    gsrc = w13_bfd[ex, j]
                    usrc = w13_bfd[ex, 11 + j]
                    hid_ready = up_phase(j, gsrc, usrc, j)

                def scale_tt(e):
                    wb = PB[6].ap()[:, 0:Tc][:, None, :].broadcast_to((128, 11, Tc))
                    return e.tensor_tensor(
                        out=hid.ap()[:, 0:11 * Tc].rearrange("p (j t) -> p j t", j=11),
                        in0=hid.ap()[:, 0:11 * Tc].rearrange("p (j t) -> p j t", j=11),
                        in1=wb, op=ALU.mult)
                sct = E("vector", scale_tt, waits=[(s_mm, pw), (s_vec, hid_ready)],
                        inc=(s_vec, "s_vec", 1))
                bank_free[6] = (s_vec, sct)
                if debug and c == 0 and ex == 0:
                    dbg2 = E("gpsimd", lambda e: e.dma_start(out=dbg_hid_d, in_=hid[:, 0:11*Tc]),
                      waits=[(s_vec, sct)], inc=(s_st, "s_st", 16))
                    down_gate = dbg2
                for dt in range(NKT):
                    dsrc = w2_bfd[ex, dt]
                    down_phase(dt, 11, dsrc, "add", sct)

            ra = E("vector", lambda e: e.tensor_tensor(out=x_f[:], in0=x_f[:],
                                                       in1=out_acc[:], op=ALU.add),
                   waits=[(s_xl, xl_tot)], inc=(s_vec, "s_vec", 1))
            prev_residadd = ra

            sq = E("scalar", lambda e: e.activation(
                out=hid[:, 0:W], in_=x_f[:], func=AF.Square),
                waits=[(s_vec, ra)], inc=(s_sc, "s_sc", 1))
            rb = E("vector", lambda e: e.tensor_copy(resid_bf[:], x_f[:]),
                   inc=(s_vec, "s_vec", 1))

            def ln_sums(e):
                for k in range(NKT):
                    nc.tensor.matmul(PB[7][0:1, 0:Tc], onec_bf[:],
                                     resid_bf[:, k * Tc:(k + 1) * Tc],
                                     start=(k == 0), stop=(k == NKT - 1))
                i = None
                for k in range(NKT):
                    i = nc.tensor.matmul(PB[5][0:1, 0:Tc], onec_bf[:],
                                         hid[:, k * Tc:(k + 1) * Tc],
                                         start=(k == 0), stop=(k == NKT - 1))
                return i
            lns = E("tensor", ln_sums,
                    waits=[(s_vec, rb), (s_sc, sq), bank_free[7], bank_free[5]],
                    inc=(s_mm, "s_mm", 1))

            def ln_stats(e):
                e.tensor_scalar(out=ln_tmp[0:1, 0:Tc], in0=PB[7][0:1, 0:Tc],
                                scalar1=1.0 / D_MODEL, scalar2=None, op0=ALU.mult)
                e.tensor_scalar(out=ln_tmp[0:1, Tc:2 * Tc], in0=PB[5][0:1, 0:Tc],
                                scalar1=1.0 / D_MODEL, scalar2=None, op0=ALU.mult)
                e.tensor_tensor(out=ln_tmp[0:1, 2 * Tc:3 * Tc], in0=ln_tmp[0:1, 0:Tc],
                                in1=ln_tmp[0:1, 0:Tc], op=ALU.mult)
                return e.tensor_tensor(out=ln_tmp[0:1, 3 * Tc:4 * Tc],
                                       in0=ln_tmp[0:1, Tc:2 * Tc],
                                       in1=ln_tmp[0:1, 2 * Tc:3 * Tc], op=ALU.subtract)
            lst = E("vector", ln_stats, waits=[(s_mm, lns)], inc=(s_vec, "s_vec", 1))

            def ln_scalar(e):
                e.activation(out=ln_tmp[0:1, Tc:2 * Tc], in_=ln_tmp[0:1, 3 * Tc:4 * Tc],
                             func=AF.Ln, bias=fc_s[0:1, 1:2])
                e.activation(out=ln_tmp[0:1, 2 * Tc:3 * Tc], in_=ln_tmp[0:1, Tc:2 * Tc],
                             func=AF.Exp, scale=fc_s[0:1, 2:3])
                e.copy(mr_bf[0:1, 0:Tc], ln_tmp[0:1, 0:Tc])
                return e.copy(mr_bf[0:1, Tc:2 * Tc], ln_tmp[0:1, 2 * Tc:3 * Tc])
            lsc = E("scalar", ln_scalar, waits=[(s_vec, lst), (s_xl, INIT_XL)],
                    inc=(s_sc, "s_sc", 1))
            bank_free[7] = (s_vec, lst)
            bank_free[5] = (s_vec, lst)

            def ln_bcast(e):
                nc.tensor.matmul(PB[6][:, 0:Tc], oner_bf[:], mr_bf[0:1, 0:Tc],
                                 start=True, stop=True)
                return nc.tensor.matmul(PB[7][:, 0:Tc], oner_bf[:], mr_bf[0:1, Tc:2 * Tc],
                                        start=True, stop=True)
            lbc = E("tensor", ln_bcast, waits=[(s_sc, lsc), bank_free[6], bank_free[7]],
                    inc=(s_mm, "s_mm", 1))

            def ln_apply(e):
                mb = PB[6].ap()[:, 0:Tc][:, None, :].broadcast_to((128, NKT, Tc))
                rb_ = PB[7].ap()[:, 0:Tc][:, None, :].broadcast_to((128, NKT, Tc))
                e.tensor_tensor(out=out_acc.ap().rearrange("p (k t) -> p k t", k=NKT),
                                in0=x_f.ap().rearrange("p (k t) -> p k t", k=NKT),
                                in1=mb, op=ALU.subtract)
                return e.tensor_tensor(out=x_f.ap().rearrange("p (k t) -> p k t", k=NKT),
                                       in0=out_acc.ap().rearrange("p (k t) -> p k t", k=NKT),
                                       in1=rb_, op=ALU.mult)
            app = E("vector", ln_apply, waits=[(s_mm, lbc)], inc=(s_vec, "s_vec", 1))
            prev_apply = app
            bank_free[6] = (s_vec, app)
            bank_free[7] = (s_vec, app)

            prev_store_tot = E("sync", lambda e, t0=t0: e.dma_start(
                out=x2_r[:, :, t0:t0 + Tc],
                in_=x_f.ap().rearrange("p (k t) -> p k t", k=NKT)),
                waits=[(s_vec, app)], inc=(s_st, "s_st", 16))
            prev_lastup = state["last_xbf_mm"]

        E("sync", lambda e: None, waits=[(s_st, prev_store_tot)])

        with nc.Block() as block:
            sch.run(block)

    return (s_st, prev_store_tot)


D_MODEL = 2048
NKT = 16
NQH = 16
NKH = 8
HD = 128
WINDOW = 1024
ATTN_SCALE = 0.08838834764831845
NEG = -1.0e9


def host_prep_attn(inputs, x2, T=4096):
    qk = np.asarray(inputs["L1_attn_q_kernel"])
    qk_sl = np.ascontiguousarray(
        qk.reshape(16, 16, 128, 128).transpose(0, 2, 1, 3).reshape(16, 128, 2048))
    kvk = np.asarray(inputs["L1_attn_kv_kernel"])
    kvk_sl = np.ascontiguousarray(
        kvk.reshape(2, 8, 16, 128, 128).transpose(0, 1, 3, 2, 4).reshape(2, 8, 128, 2048))
    ok = np.asarray(inputs["L1_attn_out_kernel"])
    ok_sl = np.ascontiguousarray(
        ok.reshape(16, 128, 16, 128).transpose(2, 1, 0, 3).reshape(16, 128, 2048))
    qns = np.asarray(inputs["L1_attn_q_norm_scale"]).reshape(128, 1)
    kns = np.asarray(inputs["L1_attn_k_norm_scale"]).reshape(128, 1)
    ts_ = 10000.0 ** (np.arange(0, HD, 2, dtype=np.float64) / HD)
    sinus = np.arange(T, dtype=np.float64)[None, :] / ts_[:, None]
    cosT = np.tile(np.cos(sinus).astype(np.float32), (2, 1))
    sinT = np.tile(np.sin(sinus).astype(np.float32), (2, 1))
    ii = np.arange(128)[:, None]
    jj = np.arange(128)[None, :]
    maskA = np.where(jj < ii, 0.0, NEG).astype(np.float32)
    maskC = np.where(ii <= jj, 0.0, NEG).astype(np.float32)
    return {
        "x2": (np.ascontiguousarray(x2) if x2 is not None else None),
        "qk_sl": qk_sl, "kvk_sl": kvk_sl, "ok_sl": ok_sl,
        "qns": np.ascontiguousarray(qns), "kns": np.ascontiguousarray(kns),
        "cosT": cosT, "sinT": sinT, "maskA": maskA, "maskC": maskC,
        "ident": np.eye(128, dtype=np.float32),
        "ones_col": np.ones((128, 1), np.float32),
        "ones_row": np.ones((1, 128), np.float32),
        "fconst": np.tile(np.array([[1.0, 1e-5, -0.5, 0.0, 1e-6]], np.float32), (128, 1)),
    }


ATTN_SEMS = ["s_xl", "s_xb", "s_wd", "s_mm", "s_sc", "s_vec", "s_st", "s_dbg"]


def declare_attn_dram(nc, T=4096, x_in=None, x_out=None, debug=False, pfx=""):
    dr = {}
    dr["x2"] = x_in if x_in is not None else nc.dram_tensor(
        "x2", [D_MODEL, T], F32, kind="ExternalInput").ap()
    dr["qk_sl"] = nc.dram_tensor("qk_sl", [NQH, 128, D_MODEL], F32, kind="ExternalInput").ap()
    dr["kvk_sl"] = nc.dram_tensor("kvk_sl", [2, NKH, 128, D_MODEL], F32, kind="ExternalInput").ap()
    dr["ok_sl"] = nc.dram_tensor("ok_sl", [NKT, 128, D_MODEL], F32, kind="ExternalInput").ap()
    dr["qns"] = nc.dram_tensor("qns", [128, 1], F32, kind="ExternalInput").ap()
    dr["kns"] = nc.dram_tensor("kns", [128, 1], F32, kind="ExternalInput").ap()
    dr["cosT"] = nc.dram_tensor("cosT", [128, T], F32, kind="ExternalInput").ap()
    dr["sinT"] = nc.dram_tensor("sinT", [128, T], F32, kind="ExternalInput").ap()
    dr["maskA"] = nc.dram_tensor("maskA", [128, 128], F32, kind="ExternalInput").ap()
    dr["maskC"] = nc.dram_tensor("maskC", [128, 128], F32, kind="ExternalInput").ap()
    dr["ident"] = nc.dram_tensor("ident" + pfx, [128, 128], F32, kind="ExternalInput").ap()
    dr["ones_col"] = nc.dram_tensor("ones_col" + pfx, [128, 1], F32, kind="ExternalInput").ap()
    dr["ones_row"] = nc.dram_tensor("ones_row" + pfx, [1, 128], F32, kind="ExternalInput").ap()
    dr["fconst"] = nc.dram_tensor("fconst" + pfx, [128, 5], F32, kind="ExternalInput").ap()
    dr["x3"] = x_out if x_out is not None else nc.dram_tensor(
        "x3", [D_MODEL, T], F32, kind="ExternalOutput").ap()
    if debug:
        dbg_kc_d = nc.dram_tensor("dbg_kc", [128, 3 * NKH * Tc], F32, kind="ExternalOutput").ap()
        dbg_vc_d = nc.dram_tensor("dbg_vc", [128, 3 * NKH * Tc], F32, kind="ExternalOutput").ap()
        dbg_q_d = nc.dram_tensor("dbg_q", [128, Tc], F32, kind="ExternalOutput").ap()
        dbg_p_d = nc.dram_tensor("dbg_p", [128, 9 * 128], F32, kind="ExternalOutput").ap()
        dbg_enc_d = nc.dram_tensor("dbg_enc", [128, NQH * Tc], F32, kind="ExternalOutput").ap()
        dbg_s_d = nc.dram_tensor("dbg_s", [128, 1152], F32, kind="ExternalOutput").ap()
        dr.update(dbg_kc=dbg_kc_d, dbg_vc=dbg_vc_d, dbg_q=dbg_q_d, dbg_p=dbg_p_d,
                  dbg_enc=dbg_enc_d, dbg_s=dbg_s_d)
    return dr


def build_attn(Tc=512, T=4096, debug=False):
    nc = bass.Bass("TRN2", target_bir_lowering=False, debug=False)
    dr = declare_attn_dram(nc, T=T, debug=debug)
    with contextlib.ExitStack() as stg:
        sems = {n: stg.enter_context(nc.semaphore("a_" + n)) for n in ATTN_SEMS}
        emit_attn(nc, sems, dr, Tc=Tc, T=T, barrier=(), debug=debug)
    return nc


def emit_attn(nc, sems, dr, Tc=512, T=4096, barrier=(), debug=False):
    NCH = T // Tc
    W = NKT * Tc
    NB = Tc // 128
    x2_d = dr["x2"]; qk_d = dr["qk_sl"]; kvk_d = dr["kvk_sl"]; ok_d = dr["ok_sl"]
    qns_d = dr["qns"]; kns_d = dr["kns"]; cos_d = dr["cosT"]; sin_d = dr["sinT"]
    maskA_d = dr["maskA"]; maskC_d = dr["maskC"]; ident_d = dr["ident"]
    onec_d = dr["ones_col"]; oner_d = dr["ones_row"]; fconst_d = dr["fconst"]
    x3_d = dr["x3"]
    if debug:
        dbg_kc_d = dr["dbg_kc"]; dbg_vc_d = dr["dbg_vc"]; dbg_q_d = dr["dbg_q"]
        dbg_p_d = dr["dbg_p"]; dbg_enc_d = dr["dbg_enc"]; dbg_s_d = dr["dbg_s"]

    x2_r = x2_d.rearrange("(k p) t -> p k t", p=128)
    x3_r = x3_d.rearrange("(k p) t -> p k t", p=128)

    st = contextlib.ExitStack()
    S = lambda name, shape, dt: st.enter_context(nc.sbuf_tensor("a_" + name, shape, dt))
    PsT = lambda name, shape, dt: st.enter_context(nc.psum_tensor("a_" + name, shape, dt))

    with st:
        sch = Sched(nc)
        for _eng in Sched.ENGINES:
            sch.emit(_eng, lambda e: None, waits=tuple(barrier))
        x_bf = S("x_bf", [128, W], BF16)
        x_f = S("x_f", [128, W], F32)
        wslab = [S(f"wslab{i}", [128, NKT * 128], BF16) for i in range(2)]
        kcache = S("kcache", [128, 3 * NKH * Tc], BF16)
        vcache = S("vcache", [128, 3 * NKH * Tc], BF16)
        enc_all = S("enc_all", [128, NQH * Tc], BF16)
        pf = S("pf", [128, Tc], F32)
        pn = S("pn", [128, Tc], F32)
        p2 = S("p2", [128, Tc], BF16)
        vbf = S("vbf", [128, Tc], BF16)
        qrope = [S(f"qrope{i}", [128, Tc], BF16) for i in range(2)]
        expS = [S(f"expS{i}", [128, Tc], BF16) for i in range(2)]
        rinv = S("rinv", [1, Tc], F32)
        rinv_bf = S("rinv_bf", [1, Tc], BF16)
        encf_bf = S("encf_bf", [128, Tc], BF16)
        cos_s = S("cos_s", [128, Tc], F32)
        sin_s = S("sin_s", [128, Tc], F32)
        rtmp = S("rtmp", [64, 2 * Tc], F32)
        qns_s = S("qns_s", [128, 1], F32)
        kns_s = S("kns_s", [128, 1], F32)
        ident_bf = S("ident_bf", [128, 128], BF16)
        maskA_bf = S("maskA_bf", [128, 128], BF16)
        maskC_bf = S("maskC_bf", [128, 128], BF16)
        onec_bf = S("onec_bf", [128, 1], BF16)
        oner_bf = S("oner_bf", [1, 128], BF16)
        fc_s = S("fc_s", [128, 5], F32)
        rr = S("rr", [1, Tc], F32)
        rr_bf = S("rr_bf", [1, Tc], BF16)
        spc = S("spc", [128, 1024], F32)
        dbg_s_sb = S("dbg_s_sb", [128, 1152], F32)
        resid_bf = S("resid_bf", [128, W], BF16)
        ln_tmp = S("ln_tmp", [1, 6 * Tc], F32)
        mr_bf = S("mr_bf", [1, 2 * Tc], BF16)

        PS = PsT("PS", [128, 512], F32)
        PS2 = PsT("PS2", [128, 512], F32)
        PE_ = PsT("PE_", [128, 512], F32)
        PENCb = [PsT(f"PENC{i}", [128, 512], F32) for i in range(2)]
        PSUMS = PsT("PSUMS", [1, 512], F32)
        PRB = PsT("PRB", [128, 512], F32)
        PTv = PsT("PTv", [128, 512], BF16)

        s_xl = sems["s_xl"]; s_xb = sems["s_xb"]; s_wd = sems["s_wd"]
        s_mm = sems["s_mm"]; s_sc = sems["s_sc"]; s_vec = sems["s_vec"]
        s_st = sems["s_st"]; s_dbg = sems["s_dbg"]
        for n in ATTN_SEMS:
            sch.tot[n] = 0
        E = sch.emit

        for ap_, dst in [(qns_d, qns_s), (kns_d, kns_s), (fconst_d, fc_s)]:
            E("sync", lambda e, _a=ap_, _d=dst: e.dma_start(out=_d[:], in_=_a),
              inc=(s_xl, "s_xl", 16))
        for ap_, dst in [(ident_d, ident_bf), (maskA_d, maskA_bf), (maskC_d, maskC_bf),
                         (onec_d, onec_bf), (oner_d, oner_bf)]:
            E("gpsimd", lambda e, _a=ap_, _d=dst: e.dma_start(out=_d[:], in_=_a),
              inc=(s_xb, "s_xb", 16))
        INIT_XL = sch.total("s_xl")
        E("vector", lambda e: e.memset(spc[:], 0.0), inc=(s_vec, "s_vec", 1))
        qsc = E("vector", lambda e: e.tensor_scalar_mul(qns_s[:], qns_s[:], ATTN_SCALE),
                waits=[(s_xl, INIT_XL)], inc=(s_vec, "s_vec", 1))

        bank = {"PS": None, "PS2": None, "PE": None, "PR": None, "PT": None}
        pf_free = [None]
        p2_free = [None]
        slab_free = {0: None, 1: None}
        prev_store = 0
        prev_apply = None
        state = {"last_xbf_mm": None, "last_kc_read": None, "last_vc_read": None,
                 "kc_written": {}, "vc_written": {}}

        def spacer(e):
            e.tensor_copy(spc[:, 512:1024], spc[:, 0:512])

        for c in range(NCH):
            t0 = c * Tc
            slot = c % 3

            xf_w = [(s_st, prev_store)] if prev_store else []
            xl_tot = E("sync", lambda e, t0=t0: e.dma_start(
                out=x_f.ap().rearrange("p (k t) -> p k t", k=NKT),
                in_=x2_r[:, :, t0:t0 + Tc]), waits=xf_w + ([(s_vec, prev_apply)] if prev_apply else []),
                inc=(s_xl, "s_xl", 16))
            xb_w = [(s_mm, state["last_xbf_mm"])] if state["last_xbf_mm"] else []
            xb_tot = E("gpsimd", lambda e, t0=t0: e.dma_start(
                out=x_bf.ap().rearrange("p (k t) -> p k t", k=NKT),
                in_=x2_r[:, :, t0:t0 + Tc]), waits=xb_w, inc=(s_xb, "s_xb", 16))
            cs_tot = E("sync", lambda e, t0=t0: e.dma_start(out=cos_s[:], in_=cos_d[:, t0:t0 + Tc]),
                       waits=[(s_vec, prev_apply)] if prev_apply else [],
                       inc=(s_xl, "s_xl", 16))
            cs_tot = E("sync", lambda e, t0=t0: e.dma_start(out=sin_s[:], in_=sin_d[:, t0:t0 + Tc]),
                       inc=(s_xl, "s_xl", 16))

            def load_slab(par, src):
                h = E("gpsimd", lambda e, _s=src, _p=par: e.dma_start(
                    out=wslab[_p][:], in_=_s),
                    waits=[slab_free[par]], inc=(s_wd, "s_wd", 16))
                return h

            def rms_rope(scale_col, dst, dst_off, hsrc_evac, rope_extra):
                sq = E("scalar", lambda e: e.activation(out=p2[:], in_=pf[:], func=AF.Square),
                       waits=[p2_free[0]] if p2_free[0] else [], inc=(s_sc, "s_sc", 1))
                def ssq_mm(e):
                    return nc.tensor.matmul(PE_[0:1, 0:Tc], onec_bf[:], p2[:],
                                            start=True, stop=True)
                sq_mm = E("tensor", ssq_mm, waits=[(s_sc, sq)],
                          inc=(s_mm, "s_mm", 1))
                p2_free[0] = (s_mm, sq_mm)
                def rr_v(e):
                    return e.tensor_scalar(out=rr[:], in0=PE_[0:1, 0:Tc],
                                           scalar1=1.0 / HD, scalar2=None, op0=ALU.mult)
                rrv = E("vector", rr_v, waits=[(s_mm, sq_mm)], inc=(s_vec, "s_vec", 1))
                def rr_s(e):
                    e.activation(out=rr[:], in_=rr[:], func=AF.Ln, bias=fc_s[0:1, 4:5])
                    e.activation(out=rr[:], in_=rr[:], func=AF.Exp, scale=fc_s[0:1, 2:3])
                    return e.copy(rr_bf[:], rr[:])
                rrs = E("scalar", rr_s, waits=[(s_vec, rrv)], inc=(s_sc, "s_sc", 1))
                def rb_mm(e):
                    return nc.tensor.matmul(PE_[:, 0:Tc], oner_bf[:], rr_bf[:],
                                            start=True, stop=True)
                rbm = E("tensor", rb_mm, waits=[(s_sc, rrs)], inc=(s_mm, "s_mm", 1))
                def norm_v(e):
                    return e.scalar_tensor_tensor(out=pn[:], in0=pf[:], scalar=scale_col[:, 0:1],
                                                  in1=PE_[:, 0:Tc], op0=ALU.mult, op1=ALU.mult)
                nrm = E("vector", norm_v, waits=[(s_mm, rbm)], inc=(s_vec, "s_vec", 1))
                bank["PE"] = (s_vec, nrm)
                pf_free[0] = (s_vec, nrm)
                def rope_v(e):
                    e.tensor_tensor(out=rtmp[:, 0:Tc], in0=pn[0:64, :], in1=cos_s[0:64, :], op=ALU.mult)
                    e.tensor_tensor(out=rtmp[:, Tc:2 * Tc], in0=pn[64:128, :], in1=sin_s[64:128, :], op=ALU.mult)
                    e.tensor_tensor(out=dst[0:64, dst_off:dst_off + Tc],
                                    in0=rtmp[:, 0:Tc], in1=rtmp[:, Tc:2 * Tc], op=ALU.subtract)
                    e.tensor_tensor(out=rtmp[:, 0:Tc], in0=pn[64:128, :], in1=cos_s[64:128, :], op=ALU.mult)
                    e.tensor_tensor(out=rtmp[:, Tc:2 * Tc], in0=pn[0:64, :], in1=sin_s[0:64, :], op=ALU.mult)
                    return e.tensor_tensor(out=dst[64:128, dst_off:dst_off + Tc],
                                           in0=rtmp[:, 0:Tc], in1=rtmp[:, Tc:2 * Tc], op=ALU.add)
                rp = E("vector", rope_v,
                       waits=[(s_xl, cs_tot)] + [w for w in rope_extra if w],
                       inc=(s_vec, "s_vec", 1))
                return rp

            for kh in range(NKH):
                par = kh % 2
                ks = load_slab(par, kvk_d[0, kh])
                def kp_mms(e, _p=par):
                    i = None
                    for k in range(NKT):
                        i = nc.tensor.matmul(PE_[:, 0:Tc],
                                             wslab[_p][:, k * 128:(k + 1) * 128],
                                             x_bf[:, k * Tc:(k + 1) * Tc],
                                             start=(k == 0), stop=(k == NKT - 1))
                    return i
                kp = E("tensor", kp_mms,
                       waits=[(s_wd, sch.total("s_wd")), bank["PE"], (s_xb, xb_tot)],
                       inc=(s_mm, "s_mm", 1))
                slab_free[par] = (s_mm, kp)
                state["last_xbf_mm"] = kp
                kev = E("scalar", lambda e: e.copy(pf[:], PE_[:, 0:Tc]),
                        waits=[(s_mm, kp)] + ([pf_free[0]] if pf_free[0] else []),
                        inc=(s_sc, "s_sc", 1))
                bank["PE"] = (s_sc, kev)
                vs = load_slab(1 - par, kvk_d[1, kh])
                def vp_mms(e, _p=1 - par):
                    i = None
                    for k in range(NKT):
                        i = nc.tensor.matmul(PS[:, 0:Tc],
                                             wslab[_p][:, k * 128:(k + 1) * 128],
                                             x_bf[:, k * Tc:(k + 1) * Tc],
                                             start=(k == 0), stop=(k == NKT - 1))
                    return i
                vp = E("tensor", vp_mms,
                       waits=[(s_wd, sch.total("s_wd")), bank["PS"], (s_xb, xb_tot)],
                       inc=(s_mm, "s_mm", 1))
                slab_free[1 - par] = (s_mm, vp)
                state["last_xbf_mm"] = vp
                vev = E("scalar", lambda e: e.copy(vbf[:], PS[:, 0:Tc]),
                        waits=[(s_mm, vp)], inc=(s_sc, "s_sc", 1))
                bank["PS"] = (s_sc, vev)
                koff = (slot * NKH + kh) * Tc
                rp = rms_rope(kns_s, kcache, koff, kev, [state["last_kc_read"]])
                state["kc_written"][(slot, kh)] = (s_vec, rp)
                def v_tp(e):
                    i = None
                    for tb in range(NB):
                        i = nc.tensor.transpose(PTv[:, tb * 128:(tb + 1) * 128],
                                                vbf[:, tb * 128:(tb + 1) * 128], ident_bf[:])
                    return i
                vtp = E("tensor", v_tp, waits=[(s_sc, vev), bank.get("PTv"), (s_xb, sch.total("s_xb"))],
                        inc=(s_mm, "s_mm", 1))
                voff = (slot * NKH + kh) * Tc
                vc_war = state["last_vc_read"]
                vce = E("scalar", lambda e, _vo=voff: e.copy(
                    vcache[:, _vo:_vo + Tc], PTv[:, 0:NB * 128]),
                    waits=[(s_mm, vtp)] + ([vc_war] if vc_war else []),
                    inc=(s_sc, "s_sc", 1))
                bank["PTv"] = (s_sc, vce)
                state["vc_written"][(slot, kh)] = (s_sc, vce)


            dbg_gate = None
            if debug and c == 0:
                kc_all = [state["kc_written"][(0, kh)] for kh in range(NKH)]
                vc_all = [state["vc_written"][(0, kh)] for kh in range(NKH)]
                E("gpsimd", lambda e: e.dma_start(out=dbg_kc_d[:, 0:NKH * Tc],
                                                   in_=kcache[:, 0:NKH * Tc]),
                  waits=kc_all, inc=(s_dbg, "s_dbg", 16))
                dbg_gate = E("gpsimd", lambda e: e.dma_start(out=dbg_vc_d[:, 0:NKH * Tc],
                                                             in_=vcache[:, 0:NKH * Tc]),
                  waits=vc_all, inc=(s_dbg, "s_dbg", 16))

            for n in range(NQH):
                par = n % 2
                qs = load_slab(par, qk_d[n])
                def qp_mms(e, _p=par):
                    i = None
                    for k in range(NKT):
                        i = nc.tensor.matmul(PE_[:, 0:Tc],
                                             wslab[_p][:, k * 128:(k + 1) * 128],
                                             x_bf[:, k * Tc:(k + 1) * Tc],
                                             start=(k == 0), stop=(k == NKT - 1))
                    return i
                qp = E("tensor", qp_mms,
                       waits=[(s_wd, sch.total("s_wd")), bank["PE"], (s_xb, xb_tot)],
                       inc=(s_mm, "s_mm", 1))
                slab_free[par] = (s_mm, qp)
                state["last_xbf_mm"] = qp
                qev = E("scalar", lambda e: e.copy(pf[:], PE_[:, 0:Tc]),
                        waits=[(s_mm, qp)] + ([pf_free[0]] if pf_free[0] else []),
                        inc=(s_sc, "s_sc", 1))
                bank["PE"] = (s_sc, qev)
                hpar = n % 2
                rp = rms_rope(qns_s, qrope[hpar], 0, qev,
                              [state.get(f"qrope_free{hpar}")])
                if debug and c == 0 and n == 0:
                    dbg_gate = E("gpsimd", lambda e: e.dma_start(out=dbg_q_d, in_=qrope[:]),
                      waits=[(s_vec, rp)], inc=(s_dbg, "s_dbg", 16))
                kh = n // 2

                tb0 = t0 // 128
                sb_list = list(range(max(0, tb0 - 8), tb0 + NB))
                hpar2 = n % 2
                PENC = PENCb[hpar2]
                first = True
                expf_par = {0: None, 1: None}
                smm_h2 = {}
                meta2 = {}

                def emit_st(i_sb):
                    sb = sb_list[i_sb]
                    tb_lo = max(sb, tb0)
                    tb_hi = min(tb0 + NB - 1, sb + 8)
                    span = (tb_hi - tb_lo + 1) * 128
                    toff = (tb_lo - tb0) * 128
                    sc_ = (sb // NB) % 3
                    koff = (sc_ * NKH + kh) * Tc + (sb % NB) * 128
                    spar = i_sb % 2
                    PSb = PS if spar == 0 else PS2
                    bkey = "PS" if spar == 0 else "PS2"
                    has_diag = sb >= tb0
                    has_edge = (sb + 8) <= (tb0 + NB - 1)
                    meta2[i_sb] = (sb, span, toff, koff, spar, PSb, bkey, sc_)

                    def st_mms(e, _ko=koff, _to=toff, _sp=span, _PSb=PSb,
                               _qr=qrope[hpar2], _hd=has_diag, _he=has_edge,
                               _sb=sb, _tb0=tb0):
                        i = nc.tensor.matmul(
                            _PSb[:, 0:_sp],
                            kcache[:, _ko:_ko + 128],
                            _qr[:, _to:_to + _sp],
                            start=True, stop=not (_hd or _he))
                        if _hd:
                            i = nc.tensor.matmul(
                                _PSb[:, (_sb - _tb0) * 128 - _to:(_sb - _tb0) * 128 - _to + 128],
                                ident_bf[:], maskC_bf[:],
                                start=False, stop=not _he)
                        if _he:
                            i = nc.tensor.matmul(
                                _PSb[:, (_sb + 8 - _tb0) * 128 - _to:(_sb + 8 - _tb0) * 128 - _to + 128],
                                ident_bf[:], maskA_bf[:],
                                start=False, stop=True)
                        return i
                    kcw = state["kc_written"].get((sc_, kh))
                    smm = E("tensor", st_mms,
                            waits=[(s_vec, rp), bank[bkey], kcw],
                            inc=(s_mm, "s_mm", 1))
                    state["last_s_mm"] = smm
                    smm_h2[i_sb] = smm

                def emit_pv(i_sb):
                    nonlocal first
                    sb, span, toff, koff, spar, PSb, bkey, sc_ = meta2[i_sb]
                    smm = smm_h2[i_sb]
                    ef = expf_par[spar]
                    exp_ = E("scalar", lambda e, _sp=span, _P=PSb, _b=spar: e.activation(
                        out=expS[_b][:, 0:_sp], in_=_P[:, 0:_sp], func=AF.Exp),
                        waits=[(s_mm, smm)] + ([ef] if ef else []),
                        inc=(s_sc, "s_sc", 1))
                    bank[bkey] = (s_sc, exp_)
                    vcw = state["vc_written"].get((sc_, kh))
                    pvw = [(s_sc, exp_), vcw]
                    if first:
                        pvw.append(bank.get("PENC%d" % hpar2))
                        pvw.append(bank.get("PSUMS"))
                    lastsb = (i_sb == len(sb_list) - 1)

                    def both_mms(e, _sp=span, _to=toff, _b=spar, _f=first,
                                 _vo=koff, _PE2=PENC, _PS=PSUMS, _l=lastsb):
                        nc.tensor.matmul(_PS[0:1, _to:_to + _sp], onec_bf[:],
                                         expS[_b][:, 0:_sp], start=_f, stop=_l)
                        return nc.tensor.matmul(_PE2[:, _to:_to + _sp],
                                                vcache[:, _vo:_vo + 128],
                                                expS[_b][:, 0:_sp],
                                                start=_f, stop=_l)
                    pv = E("tensor", both_mms, waits=pvw, inc=(s_mm, "s_mm", 1))
                    state["last_pv"] = pv
                    expf_par[spar] = (s_mm, pv)
                    first = False

                emit_st(0)
                for i_sb in range(len(sb_list)):
                    if i_sb + 1 < len(sb_list):
                        emit_st(i_sb + 1)
                    emit_pv(i_sb)
                rcp = E("vector", lambda e, _PS=PSUMS: e.reciprocal(
                    rinv[0:1, :], _PS[0:1, 0:Tc]),
                    waits=[(s_mm, state["last_pv"])], inc=(s_vec, "s_vec", 1))
                rcb = E("scalar", lambda e: e.copy(rinv_bf[0:1, :], rinv[0:1, :]),
                        waits=[(s_vec, rcp)], inc=(s_sc, "s_sc", 1))
                bank["PSUMS"] = (s_vec, rcp)
                rbm2 = E("tensor", lambda e: nc.tensor.matmul(
                    PRB[:, 0:Tc], oner_bf[:], rinv_bf[0:1, :], start=True, stop=True),
                    waits=[(s_sc, rcb), bank.get("PRB")], inc=(s_mm, "s_mm", 1))
                eevc = E("scalar", lambda e, _PE2=PENC: e.copy(encf_bf[:], _PE2[:, 0:Tc]),
                         waits=[(s_mm, state["last_pv"])], inc=(s_sc, "s_sc", 1))
                bank["PENC%d" % hpar2] = (s_sc, eevc)
                eno = E("vector", lambda e, _n=n: e.tensor_tensor(
                    out=enc_all[:, _n * Tc:(_n + 1) * Tc], in0=encf_bf[:],
                    in1=PRB[:, 0:Tc], op=ALU.mult),
                    waits=[(s_mm, rbm2), (s_sc, eevc)], inc=(s_vec, "s_vec", 1))
                bank["PRB"] = (s_vec, eno)

            state["last_kc_read"] = (s_mm, state["last_s_mm"])
            state["last_vc_read"] = (s_mm, state["last_pv"])

            last_enc = sch.total("s_sc")
            for dt in range(NKT):
                par = dt % 2
                os_ = load_slab(par, ok_d[dt])
                Pb = PE_ if par == 0 else PS
                pbk = "PE" if par == 0 else "PS"
                def op_mms(e, _p=par, _Pb=Pb):
                    i = None
                    for nn_ in range(NQH):
                        i = nc.tensor.matmul(_Pb[:, 0:Tc],
                                             wslab[_p][:, nn_ * 128:(nn_ + 1) * 128],
                                             enc_all[:, nn_ * Tc:(nn_ + 1) * Tc],
                                             start=(nn_ == 0), stop=(nn_ == NQH - 1))
                    return i
                dw2 = [(s_dbg, sch.total("s_dbg"))] if (debug and c == 0) else []
                om = E("tensor", op_mms,
                       waits=[(s_wd, sch.total("s_wd")), bank[pbk], (s_sc, last_enc)] + dw2,
                       inc=(s_mm, "s_mm", 1))
                slab_free[par] = (s_mm, om)
                ra = E("vector", lambda e, _dt=dt, _Pb=Pb: e.tensor_tensor(
                    out=x_f[:, _dt * Tc:(_dt + 1) * Tc],
                    in0=x_f[:, _dt * Tc:(_dt + 1) * Tc],
                    in1=_Pb[:, 0:Tc], op=ALU.add),
                    waits=[(s_mm, om), (s_xl, xl_tot)], inc=(s_vec, "s_vec", 1))
                bank[pbk] = (s_vec, ra)

            sq = E("scalar", lambda e: e.activation(
                out=enc_all[:, 0:W], in_=x_f[:], func=AF.Square),
                waits=[bank["PE"], bank["PS"]], inc=(s_sc, "s_sc", 1))
            rb = E("vector", lambda e: e.tensor_copy(resid_bf[:], x_f[:]),
                   inc=(s_vec, "s_vec", 1))

            def ln_sums(e):
                for k in range(NKT):
                    nc.tensor.matmul(PE_[0:1, 0:Tc], onec_bf[:],
                                     resid_bf[:, k * Tc:(k + 1) * Tc],
                                     start=(k == 0), stop=(k == NKT - 1))
                i = None
                for k in range(NKT):
                    i = nc.tensor.matmul(PS[0:1, 0:Tc], onec_bf[:],
                                         enc_all[:, k * Tc:(k + 1) * Tc],
                                         start=(k == 0), stop=(k == NKT - 1))
                return i
            lns = E("tensor", ln_sums,
                    waits=[(s_vec, rb), (s_sc, sq), bank["PE"], bank["PS"]],
                    inc=(s_mm, "s_mm", 1))

            def ln_stats(e):
                e.tensor_scalar(out=ln_tmp[0:1, 0:Tc], in0=PE_[0:1, 0:Tc],
                                scalar1=1.0 / D_MODEL, scalar2=None, op0=ALU.mult)
                e.tensor_scalar(out=ln_tmp[0:1, Tc:2 * Tc], in0=PS[0:1, 0:Tc],
                                scalar1=1.0 / D_MODEL, scalar2=None, op0=ALU.mult)
                e.tensor_tensor(out=ln_tmp[0:1, 2 * Tc:3 * Tc], in0=ln_tmp[0:1, 0:Tc],
                                in1=ln_tmp[0:1, 0:Tc], op=ALU.mult)
                return e.tensor_tensor(out=ln_tmp[0:1, 3 * Tc:4 * Tc],
                                       in0=ln_tmp[0:1, Tc:2 * Tc],
                                       in1=ln_tmp[0:1, 2 * Tc:3 * Tc], op=ALU.subtract)
            lst = E("vector", ln_stats, waits=[(s_mm, lns)], inc=(s_vec, "s_vec", 1))

            def ln_scalar(e):
                e.activation(out=ln_tmp[0:1, 4 * Tc:5 * Tc], in_=ln_tmp[0:1, 3 * Tc:4 * Tc],
                             func=AF.Ln, bias=fc_s[0:1, 1:2])
                return e.activation(out=ln_tmp[0:1, 4 * Tc:5 * Tc],
                                    in_=ln_tmp[0:1, 4 * Tc:5 * Tc],
                                    func=AF.Exp, scale=fc_s[0:1, 2:3])
            lsc = E("scalar", ln_scalar, waits=[(s_vec, lst)], inc=(s_sc, "s_sc", 1))

            def ln_ms(e):
                e.tensor_tensor(out=ln_tmp[0:1, 5 * Tc:6 * Tc], in0=ln_tmp[0:1, 0:Tc],
                                in1=ln_tmp[0:1, 4 * Tc:5 * Tc], op=ALU.mult)
                spacer(e)
                e.tensor_copy(mr_bf[0:1, 0:Tc], ln_tmp[0:1, 4 * Tc:5 * Tc])
                return e.tensor_copy(mr_bf[0:1, Tc:2 * Tc], ln_tmp[0:1, 5 * Tc:6 * Tc])
            lms = E("vector", ln_ms, waits=[(s_sc, lsc)], inc=(s_vec, "s_vec", 1))

            def ln_bcast(e):
                nc.tensor.matmul(PE_[:, 0:Tc], oner_bf[:], mr_bf[0:1, 0:Tc],
                                 start=True, stop=True)
                return nc.tensor.matmul(PS[:, 0:Tc], oner_bf[:], mr_bf[0:1, Tc:2 * Tc],
                                        start=True, stop=True)
            lbc = E("tensor", ln_bcast, waits=[(s_vec, lms)], inc=(s_mm, "s_mm", 1))

            def ln_apply(e):
                rb_ = PE_.ap()[:, 0:Tc][:, None, :].broadcast_to((128, NKT, Tc))
                ms_ = PS.ap()[:, 0:Tc][:, None, :].broadcast_to((128, NKT, Tc))
                e.tensor_tensor(out=x_f.ap().rearrange("p (k t) -> p k t", k=NKT),
                                in0=x_f.ap().rearrange("p (k t) -> p k t", k=NKT),
                                in1=rb_, op=ALU.mult)
                return e.tensor_tensor(out=x_f.ap().rearrange("p (k t) -> p k t", k=NKT),
                                       in0=x_f.ap().rearrange("p (k t) -> p k t", k=NKT),
                                       in1=ms_, op=ALU.subtract)
            app = E("vector", ln_apply, waits=[(s_mm, lbc)], inc=(s_vec, "s_vec", 1))
            prev_apply = app
            bank["PE"] = (s_vec, app)
            bank["PS"] = (s_vec, app)

            prev_store = E("sync", lambda e, t0=t0: e.dma_start(
                out=x3_r[:, :, t0:t0 + Tc],
                in_=x_f.ap().rearrange("p (k t) -> p k t", k=NKT)),
                waits=[(s_vec, app)], inc=(s_st, "s_st", 16))

        E("sync", lambda e: None, waits=[(s_st, prev_store)])

        with nc.Block() as block:
            sch.run(block)

    return (s_st, prev_store)


D_MODEL = 2048
NKT = 16
HID = 8192
NJ = HID // 128
NJH = NJ // 2


def _had(n):
    h = np.array([[1.0]])
    while h.shape[0] < n:
        h = np.block([[h, h], [h, -h]])
    return h


def host_prep_hffn(inputs, x3):
    W1 = np.asarray(inputs["L1_ffn_W1"])
    w1_sl = np.ascontiguousarray(
        W1.reshape(64, 128, 16, 128).transpose(0, 3, 2, 1).reshape(64, 128, 2048))
    W3 = np.asarray(inputs["L1_ffn_W3"])
    w3_sl = np.ascontiguousarray(
        W3.reshape(64, 128, 16, 128).transpose(0, 3, 2, 1).reshape(64, 128, 2048))
    w2T = np.ascontiguousarray(np.asarray(inputs["L1_ffn_W2"]).T)
    h64 = (_had(64) / 8.0).astype(np.float32)
    h128 = (_had(128) * (128.0 ** -0.5)).astype(np.float32)
    perm = np.empty(128, np.int64)
    for q in range(128):
        a2, H, jl = q // 64, (q // 32) % 2, q % 32
        perm[q] = 2 * (H * 32 + jl) + a2
    h128p = np.ascontiguousarray(h128[:, perm])
    return {
        "x3": (np.ascontiguousarray(x3) if x3 is not None else None), "w1_sl": w1_sl, "w3_sl": w3_sl, "w2T": w2T,
        "h64": h64, "h128": h128p,
        "ones_col": np.ones((128, 1), np.float32),
        "ones_row": np.ones((1, 128), np.float32),
        "fconst": np.tile(np.array([[1.0, 1e-5, -0.5, 0.0, 1e-6]], np.float32), (128, 1)),
    }


HFFN_SEMS = ["s_xl", "s_xb", "s_wd", "s_mm", "s_sc", "s_vec", "s_st", "s_pp"]


def declare_hffn_dram(nc, T=4096, x_in=None, x_out=None, pfx=""):
    dr = {}
    dr["x3"] = x_in if x_in is not None else nc.dram_tensor(
        "x3", [D_MODEL, T], F32, kind="ExternalInput").ap()
    dr["w1_sl"] = nc.dram_tensor("w1_sl", [NJ, 128, D_MODEL], F32, kind="ExternalInput").ap()
    dr["w3_sl"] = nc.dram_tensor("w3_sl", [NJ, 128, D_MODEL], F32, kind="ExternalInput").ap()
    dr["w2T"] = nc.dram_tensor("w2T", [HID, D_MODEL], F32, kind="ExternalInput").ap()
    dr["h64"] = nc.dram_tensor("h64", [64, 64], F32, kind="ExternalInput").ap()
    dr["h128"] = nc.dram_tensor("h128", [128, 128], F32, kind="ExternalInput").ap()
    dr["ones_col"] = nc.dram_tensor("ones_col" + pfx, [128, 1], F32, kind="ExternalInput").ap()
    dr["ones_row"] = nc.dram_tensor("ones_row" + pfx, [1, 128], F32, kind="ExternalInput").ap()
    dr["fconst"] = nc.dram_tensor("fconst" + pfx, [128, 5], F32, kind="ExternalInput").ap()
    dr["x4"] = x_out if x_out is not None else nc.dram_tensor(
        "x4", [D_MODEL, T], F32, kind="ExternalOutput").ap()
    dr["t1buf"] = nc.dram_tensor("t1buf", [HID, D_MODEL], BF16, kind="Internal").ap()
    dr["vbuf"] = nc.dram_tensor("vbuf", [2, NKT, 128, NJH * 128], BF16, kind="Internal").ap()
    return dr


def build_hffn(Tc=512, T=4096):
    nc = bass.Bass("TRN2", target_bir_lowering=False, debug=False)
    dr = declare_hffn_dram(nc, T=T)
    with contextlib.ExitStack() as stg:
        sems = {n: stg.enter_context(nc.semaphore("h_" + n)) for n in HFFN_SEMS}
        emit_hffn(nc, sems, dr, Tc=Tc, T=T, barrier=())
    return nc


def emit_hffn(nc, sems, dr, Tc=512, T=4096, barrier=()):
    NCH = T // Tc
    W = NKT * Tc
    x3_d = dr["x3"]; w1_d = dr["w1_sl"]; w3_d = dr["w3_sl"]; w2T_d = dr["w2T"]
    h64_d = dr["h64"]; h128_d = dr["h128"]
    onec_d = dr["ones_col"]; oner_d = dr["ones_row"]; fconst_d = dr["fconst"]
    x4_d = dr["x4"]; t1_d = dr["t1buf"]; v_d = dr["vbuf"]

    x3_r = x3_d.rearrange("(k p) t -> p k t", p=128)
    x4_r = x4_d.rearrange("(k p) t -> p k t", p=128)
    t1_s64 = t1_d.rearrange("(p j) d -> p j d", j=64)
    v3_st = v_d.rearrange("H dt p (jl f) -> H p jl dt f", f=128)

    st = contextlib.ExitStack()
    S = lambda name, shape, dt: st.enter_context(nc.sbuf_tensor("h_" + name, shape, dt))
    Ps = lambda name, shape: st.enter_context(nc.psum_tensor("h_" + name, shape, F32))

    with st:
        sch = Sched(nc)
        for _eng in Sched.ENGINES:
            sch.emit(_eng, lambda e: None, waits=tuple(barrier))
        x_bf = S("x_bf", [128, W], BF16)
        x_f = S("x_f", [128, W], F32)
        slabG = [S(f"slabG{i}", [128, NKT * 128], BF16) for i in range(2)]
        slabU = [S(f"slabU{i}", [128, NKT * 128], BF16) for i in range(2)]
        slabD = [S(f"slabD{i}", [128, NJH * 128], BF16) for i in range(2)]
        hid = S("hid", [128, NJH * Tc], BF16)
        tmp_silu = [S(f"tmps{i}", [128, Tc], BF16) for i in range(2)]
        pp_in = [S(f"pp_in{i}", [128, D_MODEL], BF16) for i in range(2)]
        pp_out = [S(f"pp_out{i}", [128, D_MODEL], BF16) for i in range(2)]
        h64_bf = S("h64_bf", [64, 64], BF16)
        h128_bf = S("h128_bf", [128, 128], BF16)
        onec_bf = S("onec_bf", [128, 1], BF16)
        oner_bf = S("oner_bf", [1, 128], BF16)
        fc_s = S("fc_s", [128, 5], F32)
        resid_bf = S("resid_bf", [128, W], BF16)
        r2_bf = S("r2_bf", [128, W], BF16)
        ln_tmp = S("ln_tmp", [1, 6 * Tc], F32)
        mr_bf = S("mr_bf", [1, 2 * Tc], BF16)
        spc = S("spc", [128, 1024], F32)

        PB = [Ps(f"PB{i}", [128, 512]) for i in range(8)]

        s_xl = sems["s_xl"]; s_xb = sems["s_xb"]; s_wd = sems["s_wd"]
        s_mm = sems["s_mm"]; s_sc = sems["s_sc"]; s_vec = sems["s_vec"]
        s_st = sems["s_st"]; s_pp = sems["s_pp"]
        for n in HFFN_SEMS:
            sch.tot[n] = 0
        E = sch.emit

        E("gpsimd", lambda e: e.dma_start(out=h64_bf[:], in_=h64_d), inc=(s_xb, "s_xb", 16))
        E("gpsimd", lambda e: e.dma_start(out=h128_bf[:], in_=h128_d), inc=(s_xb, "s_xb", 16))
        E("gpsimd", lambda e: e.dma_start(out=onec_bf[:], in_=onec_d), inc=(s_xb, "s_xb", 16))
        E("gpsimd", lambda e: e.dma_start(out=oner_bf[:], in_=oner_d), inc=(s_xb, "s_xb", 16))
        E("sync", lambda e: e.dma_start(out=fc_s[:], in_=fconst_d), inc=(s_xl, "s_xl", 16))
        INIT_XB = sch.total("s_xb")
        INIT_XL = sch.total("s_xl")
        E("vector", lambda e: e.memset(spc[:], 0.0), inc=(s_vec, "s_vec", 1))

        bank_free = {i: None for i in range(8)}
        tmp_free = {0: None, 1: None}
        slab_free = {("G", 0): None, ("G", 1): None, ("U", 0): None, ("U", 1): None,
                     ("D", 0): None, ("D", 1): None}
        pp_free = {0: None, 1: None}
        ppo_free = {0: None, 1: None}

        def pp_round(idx, src_ap, dst_ap, hmat, nrow):
            par = idx % 2
            b0 = 4 * par
            ld = E("gpsimd", lambda e, _s=src_ap, _p=par, _n=nrow: e.dma_start(
                out=pp_in[_p][0:_n, :], in_=_s),
                waits=[pp_free[par], (s_xb, INIT_XB)], inc=(s_wd, "s_wd", 16))

            def mms(e, _p=par, _b0=b0, _n=nrow, _h=hmat):
                i = None
                for q in range(4):
                    i = nc.tensor.matmul(PB[_b0 + q][0:_n, 0:512], _h,
                                         pp_in[_p][0:_n, q * 512:(q + 1) * 512],
                                         start=True, stop=True)
                return i
            bw = [bank_free[b0 + q] for q in range(4)]
            mm = E("tensor", mms, waits=[(s_wd, sch.total("s_wd"))] + bw,
                   inc=(s_mm, "s_mm", 1))
            pp_free[par] = (s_mm, mm)

            def evac(e, _p=par, _b0=b0, _n=nrow):
                i = None
                for q in range(4):
                    i = e.copy(pp_out[_p][0:_n, q * 512:(q + 1) * 512],
                               PB[_b0 + q][0:_n, 0:512])
                return i
            ev = E("scalar", evac, waits=[(s_mm, mm)] +
                   ([ppo_free[par]] if ppo_free[par] else []),
                   inc=(s_sc, "s_sc", 1))
            for q in range(4):
                bank_free[b0 + q] = (s_sc, ev)
            if isinstance(dst_ap, tuple):
                _, bidx = dst_ap
                st_ = None
                for Hh in range(2):
                    for a2 in range(2):
                        g0 = (a2 * 2 + Hh) * 32
                        st_ = E("sync", lambda e, _p=par, _H=Hh, _a2=a2, _b=bidx,
                                _g0=g0: e.dma_start(
                                    out=v3_st[_H, 64 * _a2 + _b, :, :, :],
                                    in_=pp_out[_p].ap()[_g0:_g0 + 32, :].rearrange(
                                        "p (dt f) -> p dt f", f=128)),
                                waits=[(s_sc, ev)], inc=(s_pp, "s_pp", 16))
            else:
                st_ = E("sync", lambda e, _p=par, _d=dst_ap, _n=nrow: e.dma_start(
                    out=_d, in_=pp_out[_p][0:_n, :]),
                    waits=[(s_sc, ev)], inc=(s_pp, "s_pp", 16))
            ppo_free[par] = (s_pp, st_)
            return st_

        for a in range(128):
            pp_round(a, w2T_d[a * 64:(a + 1) * 64, :],
                     t1_d[a * 64:(a + 1) * 64, :], h64_bf[:], 64)
        PP1_DONE = sch.total("s_pp")
        pp_free[0] = (s_pp, PP1_DONE)
        pp_free[1] = (s_pp, PP1_DONE)
        for b in range(64):
            pp_round(128 + b, t1_s64[:, b, :], ("v3", b), h128_bf[:], 128)
        PP2_DONE = sch.total("s_pp")

        prev_apply = None
        prev_store_tot = 0
        prev_lastup = None
        state = {"last_xbf_mm": None}

        def up_phase(jn, gsrc, usrc, hid_slot, xb_tot):
            par = jn % 2
            gs = E("gpsimd", lambda e, _g=gsrc, _p=par: e.dma_start(
                out=slabG[_p][:], in_=_g),
                waits=[slab_free[("G", par)]], inc=(s_wd, "s_wd", 16))
            us = E("gpsimd", lambda e, _u=usrc, _p=par: e.dma_start(
                out=slabU[_p][:], in_=_u),
                waits=[slab_free[("U", par)]], inc=(s_wd, "s_wd", 16))
            bg = 0 if par == 0 else 2
            bu = 1 if par == 0 else 3

            def pg_mms(e, _p=par, _b=bg):
                i = None
                for k in range(NKT):
                    i = nc.tensor.matmul(PB[_b][:, 0:Tc],
                                         slabG[_p][:, k * 128:(k + 1) * 128],
                                         x_bf[:, k * Tc:(k + 1) * Tc],
                                         start=(k == 0), stop=(k == NKT - 1))
                return i
            pg = E("tensor", pg_mms,
                   waits=[(s_wd, sch.total("s_wd")), bank_free[bg], (s_xb, xb_tot)],
                   inc=(s_mm, "s_mm", 1))
            slab_free[("G", par)] = (s_mm, pg)

            def pu_mms(e, _p=par, _b=bu):
                i = None
                for k in range(NKT):
                    i = nc.tensor.matmul(PB[_b][:, 0:Tc],
                                         slabU[_p][:, k * 128:(k + 1) * 128],
                                         x_bf[:, k * Tc:(k + 1) * Tc],
                                         start=(k == 0), stop=(k == NKT - 1))
                return i
            pu = E("tensor", pu_mms,
                   waits=[(s_wd, sch.total("s_wd")), bank_free[bu], (s_xb, xb_tot)],
                   inc=(s_mm, "s_mm", 1))
            slab_free[("U", par)] = (s_mm, pu)
            state["last_xbf_mm"] = pu
            sl = E("scalar", lambda e, _p=par, _b=bg: e.activation(
                out=tmp_silu[_p][:], in_=PB[_b][:, 0:Tc], func=AF.Silu),
                waits=[(s_mm, pg), tmp_free[par]], inc=(s_sc, "s_sc", 1))
            bank_free[bg] = (s_sc, sl)
            hv = E("vector", lambda e, _p=par, _b=bu, _h=hid_slot: e.tensor_tensor(
                out=hid[:, _h * Tc:(_h + 1) * Tc], in0=tmp_silu[_p][:],
                in1=PB[_b][:, 0:Tc], op=ALU.mult),
                waits=[(s_mm, pu), (s_sc, sl)], inc=(s_vec, "s_vec", 1))
            bank_free[bu] = (s_vec, hv)
            tmp_free[par] = (s_vec, hv)
            return hv

        for c in range(NCH):
            t0 = c * Tc
            xf_waits = ([(s_st, prev_store_tot)] if prev_store_tot else []) + \
                       ([(s_vec, prev_apply)] if prev_apply else [])
            xl_tot = E("sync", lambda e, t0=t0: e.dma_start(
                out=x_f.ap().rearrange("p (k t) -> p k t", k=NKT),
                in_=x3_r[:, :, t0:t0 + Tc]), waits=xf_waits, inc=(s_xl, "s_xl", 16))
            xb_waits = [(s_mm, state["last_xbf_mm"])] if state["last_xbf_mm"] else []
            xb_tot = E("gpsimd", lambda e, t0=t0: e.dma_start(
                out=x_bf.ap().rearrange("p (k t) -> p k t", k=NKT),
                in_=x3_r[:, :, t0:t0 + Tc]), waits=xb_waits, inc=(s_xb, "s_xb", 16))

            for H in range(2):
                hid_ready = None
                for j in range(NJH):
                    jj = H * NJH + j
                    gsrc = w1_d[jj]
                    usrc = w3_d[jj]
                    hid_ready = up_phase(j, gsrc, usrc, j, xb_tot)
                for dt in range(NKT):
                    par = dt % 2
                    ds = E("gpsimd", lambda e, _H=H, _dt=dt, _p=par: e.dma_start(
                        out=slabD[_p][:], in_=v_d[_H, _dt]),
                        waits=[slab_free[("D", par)], (s_pp, PP2_DONE)],
                        inc=(s_wd, "s_wd", 16))
                    bb = 4 if par == 0 else 5

                    def pd_mms(e, _p=par, _b=bb):
                        i = None
                        for j2 in range(NJH):
                            i = nc.tensor.matmul(PB[_b][:, 0:Tc],
                                                 slabD[_p][:, j2 * 128:(j2 + 1) * 128],
                                                 hid[:, j2 * Tc:(j2 + 1) * Tc],
                                                 start=(j2 == 0), stop=(j2 == NJH - 1))
                        return i
                    pd = E("tensor", pd_mms,
                           waits=[(s_wd, sch.total("s_wd")), bank_free[bb],
                                  (s_vec, hid_ready)],
                           inc=(s_mm, "s_mm", 1))
                    slab_free[("D", par)] = (s_mm, pd)
                    ra = E("vector", lambda e, _b=bb, _dt=dt: e.tensor_tensor(
                        out=x_f[:, _dt * Tc:(_dt + 1) * Tc],
                        in0=x_f[:, _dt * Tc:(_dt + 1) * Tc],
                        in1=PB[_b][:, 0:Tc], op=ALU.add),
                        waits=[(s_mm, pd), (s_xl, xl_tot)], inc=(s_vec, "s_vec", 1))
                    bank_free[bb] = (s_vec, ra)

            sq = E("scalar", lambda e: e.activation(
                out=r2_bf[:], in_=x_f[:], func=AF.Square),
                waits=[bank_free[4], bank_free[5]], inc=(s_sc, "s_sc", 1))
            rb = E("vector", lambda e: e.tensor_copy(resid_bf[:], x_f[:]),
                   inc=(s_vec, "s_vec", 1))

            def ln_sums(e):
                for k in range(NKT):
                    nc.tensor.matmul(PB[6][0:1, 0:Tc], onec_bf[:],
                                     resid_bf[:, k * Tc:(k + 1) * Tc],
                                     start=(k == 0), stop=(k == NKT - 1))
                i = None
                for k in range(NKT):
                    i = nc.tensor.matmul(PB[7][0:1, 0:Tc], onec_bf[:],
                                         r2_bf[:, k * Tc:(k + 1) * Tc],
                                         start=(k == 0), stop=(k == NKT - 1))
                return i
            lns = E("tensor", ln_sums,
                    waits=[(s_vec, rb), (s_sc, sq), bank_free[6], bank_free[7]],
                    inc=(s_mm, "s_mm", 1))

            def ln_stats(e):
                e.tensor_scalar(out=ln_tmp[0:1, 0:Tc], in0=PB[6][0:1, 0:Tc],
                                scalar1=1.0 / D_MODEL, scalar2=None, op0=ALU.mult)
                e.tensor_scalar(out=ln_tmp[0:1, Tc:2 * Tc], in0=PB[7][0:1, 0:Tc],
                                scalar1=1.0 / D_MODEL, scalar2=None, op0=ALU.mult)
                e.tensor_tensor(out=ln_tmp[0:1, 2 * Tc:3 * Tc], in0=ln_tmp[0:1, 0:Tc],
                                in1=ln_tmp[0:1, 0:Tc], op=ALU.mult)
                return e.tensor_tensor(out=ln_tmp[0:1, 3 * Tc:4 * Tc],
                                       in0=ln_tmp[0:1, Tc:2 * Tc],
                                       in1=ln_tmp[0:1, 2 * Tc:3 * Tc], op=ALU.subtract)
            lst = E("vector", ln_stats, waits=[(s_mm, lns)], inc=(s_vec, "s_vec", 1))

            def ln_scalar(e):
                e.activation(out=ln_tmp[0:1, 4 * Tc:5 * Tc], in_=ln_tmp[0:1, 3 * Tc:4 * Tc],
                             func=AF.Ln, bias=fc_s[0:1, 1:2])
                return e.activation(out=ln_tmp[0:1, 4 * Tc:5 * Tc],
                                    in_=ln_tmp[0:1, 4 * Tc:5 * Tc],
                                    func=AF.Exp, scale=fc_s[0:1, 2:3])
            lsc = E("scalar", ln_scalar, waits=[(s_vec, lst)], inc=(s_sc, "s_sc", 1))

            def ln_ms(e):
                e.tensor_tensor(out=ln_tmp[0:1, 5 * Tc:6 * Tc], in0=ln_tmp[0:1, 0:Tc],
                                in1=ln_tmp[0:1, 4 * Tc:5 * Tc], op=ALU.mult)
                e.tensor_copy(spc[:, 512:1024], spc[:, 0:512])
                e.tensor_copy(mr_bf[0:1, 0:Tc], ln_tmp[0:1, 4 * Tc:5 * Tc])
                return e.tensor_copy(mr_bf[0:1, Tc:2 * Tc], ln_tmp[0:1, 5 * Tc:6 * Tc])
            lms = E("vector", ln_ms, waits=[(s_sc, lsc)], inc=(s_vec, "s_vec", 1))

            def ln_bcast(e):
                nc.tensor.matmul(PB[6][:, 0:Tc], oner_bf[:], mr_bf[0:1, 0:Tc],
                                 start=True, stop=True)
                return nc.tensor.matmul(PB[7][:, 0:Tc], oner_bf[:], mr_bf[0:1, Tc:2 * Tc],
                                        start=True, stop=True)
            lbc = E("tensor", ln_bcast, waits=[(s_vec, lms)], inc=(s_mm, "s_mm", 1))

            def ln_apply(e):
                rb_ = PB[6].ap()[:, 0:Tc][:, None, :].broadcast_to((128, NKT, Tc))
                ms_ = PB[7].ap()[:, 0:Tc][:, None, :].broadcast_to((128, NKT, Tc))
                e.tensor_tensor(out=x_f.ap().rearrange("p (k t) -> p k t", k=NKT),
                                in0=x_f.ap().rearrange("p (k t) -> p k t", k=NKT),
                                in1=rb_, op=ALU.mult)
                return e.tensor_tensor(out=x_f.ap().rearrange("p (k t) -> p k t", k=NKT),
                                       in0=x_f.ap().rearrange("p (k t) -> p k t", k=NKT),
                                       in1=ms_, op=ALU.subtract)
            app = E("vector", ln_apply, waits=[(s_mm, lbc)], inc=(s_vec, "s_vec", 1))
            prev_apply = app
            bank_free[6] = (s_vec, app)
            bank_free[7] = (s_vec, app)

            prev_store_tot = E("sync", lambda e, t0=t0: e.dma_start(
                out=x4_r[:, :, t0:t0 + Tc],
                in_=x_f.ap().rearrange("p (k t) -> p k t", k=NKT)),
                waits=[(s_vec, app)], inc=(s_st, "s_st", 16))
            prev_lastup = state["last_xbf_mm"]

        E("sync", lambda e: None, waits=[(s_st, prev_store_tot)])

        with nc.Block() as block:
            sch.run(block)

    return (s_st, prev_store_tot)


MAMBA_SEMS = ["s_ld", "s_lg", "s_mm", "s_sc", "s_vec", "s_st"]


def build_fused(T=T_SEQ, Tc_m=256, Tc=512):
    nc = bass.Bass("TRN2", target_bir_lowering=False, debug=False)
    x1 = nc.dram_tensor("x1buf", [2048, T], F32, kind="Internal").ap()
    x2 = nc.dram_tensor("x2buf", [2048, T], F32, kind="Internal").ap()
    x3 = nc.dram_tensor("x3buf", [2048, T], F32, kind="Internal").ap()
    x4 = nc.dram_tensor("x4", [2048, T], F32, kind="ExternalOutput").ap()

    dr1 = declare_mamba_dram(nc, T=T, x_out=x1)
    dr2 = declare_moe_dram(nc, T=T, x_in=x1, x_out=x2, pfx="_2")
    dr3 = declare_attn_dram(nc, T=T, x_in=x2, x_out=x3, pfx="_3")
    dr4 = declare_hffn_dram(nc, T=T, x_in=x3, x_out=x4, pfx="_4")

    with contextlib.ExitStack() as stg:
        sems1 = {n: stg.enter_context(nc.semaphore("m_" + n)) for n in MAMBA_SEMS}
        sems2 = {n: stg.enter_context(nc.semaphore("f_" + n)) for n in MOE_SEMS}
        sems3 = {n: stg.enter_context(nc.semaphore("a_" + n)) for n in ATTN_SEMS}
        sems4 = {n: stg.enter_context(nc.semaphore("h_" + n)) for n in HFFN_SEMS}
        ppt, wp_tot = moe_prepass_thunks(dr2, sems2["s_wp"])
        b1 = emit_mamba(nc, sems1, dr1, Tc=Tc_m, T=T, barrier=(),
                        extra_gpsimd=ppt)
        b2 = emit_moe(nc, sems2, dr2, Tc=Tc, T=T, barrier=(b1,),
                      pp_done=(sems2["s_wp"], wp_tot))
        b3 = emit_attn(nc, sems3, dr3, Tc=Tc, T=T, barrier=(b2,))
        b4 = emit_hffn(nc, sems4, dr4, Tc=Tc, T=T, barrier=(b3,))
    return nc


def host_prep_fused(inputs, core):
    im = {}
    m1 = host_prep_mamba(inputs, core)
    im.update(m1)
    m2 = host_prep_moe(inputs, None)
    m3 = host_prep_attn(inputs, None, T=T_SEQ)
    m4 = host_prep_hffn(inputs, None)
    pfx_keys = {"_2": ("ones_col", "ones_row", "fconst"),
                "_3": ("ones_col", "ones_row", "fconst", "ident"),
                "_4": ("ones_col", "ones_row", "fconst")}
    for m, sfx in [(m2, "_2"), (m3, "_3"), (m4, "_4")]:
        for k, v in m.items():
            if v is None or k in ("x1", "x2", "x3"):
                continue
            if k in pfx_keys[sfx]:
                im[k + sfx] = v
            else:
                im[k] = v
    return im



N_CORES = 8


def kernel(**inputs):
    cores = list(range(N_CORES))
    ims = [host_prep_fused(inputs, c) for c in cores]
    nc = build_fused(T=T_SEQ)
    res = run_bass_kernel_spmd(nc, ims, core_ids=cores)
    out = np.stack([np.ascontiguousarray(res.results[c]["x4"].T) for c in cores], axis=0)
    return out.astype(np.float32)
